# version 04
# baseline (speedup 1.0000x reference)
import numpy as np
import ml_dtypes
import concourse.bass as bass
import concourse.mybir as mybir
from concourse.bass_utils import run_bass_kernel_spmd
from contextlib import ExitStack

F32 = mybir.dt.float32
BF16 = mybir.dt.bfloat16
I32 = mybir.dt.int32
AF = mybir.ActivationFunctionType
OP = mybir.AluOpType
PAIRS = [[0, 1], [2, 3], [4, 5], [6, 7]]
BF = ml_dtypes.bfloat16

D = 1024
T = 2048
NT = 4096
QC = 512
NQC = T // QC
NCH = 8


class Sems:
    def __init__(self, ctx, nc):
        object.__setattr__(self, "ctx", ctx)
        object.__setattr__(self, "nc", nc)
        object.__setattr__(self, "d", {})

    def h(self, name):
        if name not in self.d:
            self.d[name] = [self.ctx.enter_context(self.nc.semaphore(name)), 0]
        return self.d[name][0]

    def inc(self, instr, name, k=1):
        h = self.h(name)
        instr.then_inc(h, k)
        self.d[name][1] += k
        return self.d[name][1]

    def val(self, name):
        self.h(name)
        return self.d[name][1]

    def wait(self, eng, name, v=None):
        eng.wait_ge(self.h(name), self.val(name) if v is None else v)


class Prog:

    def __init__(self, nc, ctx, dbg):
        self.nc, self.ctx, self.dbg = nc, ctx, dbg
        self.s = Sems(ctx, nc)
        E = ctx.enter_context
        self.X = E(nc.sbuf_tensor("X", [128, NCH, T], F32))
        self.BIGA = E(nc.sbuf_tensor("BIGA", [128, 16384], BF16))
        self.ROWS = E(nc.sbuf_tensor("ROWS", [128, 6, QC], F32))
        self.TMP = E(nc.sbuf_tensor("TMP", [128, 1, QC], F32))
        self.R32 = E(nc.sbuf_tensor("R32", [128, 7, QC], mybir.dt.float32r))
        self.ONE1B = E(nc.sbuf_tensor("ONE1B", [1, 128], BF16))
        self.ONE1F = E(nc.sbuf_tensor("ONE1F", [1, 128], F32))
        self.ONECB = E(nc.sbuf_tensor("ONECB", [128, 1], BF16))
        self.EPS = E(nc.sbuf_tensor("EPS", [128, 1], F32))
        self.NHALF = E(nc.sbuf_tensor("NHALF", [128, 1], F32))
        self.C0125 = E(nc.sbuf_tensor("C0125", [128, 1], F32))
        self.GR = E(nc.gpsimd.register("GRA"))
        self.GR2 = E(nc.gpsimd.register("GRB"))
        self.PSALL = E(nc.psum_tensor("PSALL", [128, 8, QC], F32))
        self.PS = [self.PSALL[:, i, :] for i in range(8)]
        self.G = [bass.AP(self.PSALL, i * 2 * QC, [[8 * QC, 128], [1, 2 * QC]])
                  for i in range(2)]

    def XN(self, c, t0, n):
        return self.BIGA[:, c * T + t0: c * T + t0 + n]

    def p0_load(self, x_fm):
        nc, s = self.nc, self.s
        with nc.Block() as blk:
            @blk.gpsimd
            def _(g):
                g.memset(self.ONE1B[:], 1.0)
                g.memset(self.ONE1F[:], 1.0)
                g.memset(self.ONECB[:], 1.0)
                g.memset(self.EPS[:], 1e-5)
                g.memset(self.C0125[:], 0.125)
                s.inc(g.memset(self.NHALF[:], -0.5), "consts", 1)

            @blk.vector
            def _(v):
                s.wait(v, "consts")
                v.memset(self.ROWS[32:33, 0, :], 1.0)
                v.drain()
                s.inc(v.tensor_copy(self.R32[32:33, 2, :], self.ROWS[32:33, 0, :]), "consts", 1)

            @blk.sync
            def _(sy):
                for c in range(NCH):
                    s.inc(sy.dma_start(out=self.X[:, c, :],
                                       in_=x_fm[c * 128:(c + 1) * 128, :]), "xld", 16)

    def topln(self, tag, dst, after=()):
        nc, s = self.nc, self.s
        after = [(name, s.val(name) if v is None else v) for name, v in after]

        def gate(eng):
            for name, v in after:
                eng.wait_ge(s.h(name), v)
        exact = dst == "xf"
        cast_d = "tl_cast"
        sq_d = "tl_sq"
        sum_d = "tl_sum"
        row_d = "tl_row"
        bc_d = "tl_bc"
        ap_d = "tl_ap"
        b = {k: s.val(k) for k in (cast_d, sq_d, sum_d, row_d, bc_d, ap_d)}
        lctx = ExitStack()
        TC = lctx.enter_context(nc.sbuf_tensor(f"TC{tag}", [128, NCH, QC], BF16))
        SQB = lctx.enter_context(nc.sbuf_tensor(f"SQB{tag}", [128, NCH, QC], BF16))
        RBx = lctx.enter_context(nc.sbuf_tensor(f"RB{tag}", [1, 4, QC], BF16))
        self.TC, self.RB = TC, RBx
        R = self.ROWS

        with nc.Block() as blk:
            @blk.gpsimd
            def _(g):
                gate(g)
                s.wait(g, "xld")
                s.wait(g, "consts")
                for qc in range(NQC):
                    if qc >= 1:
                        g.wait_ge(s.h(sum_d), b[sum_d] + qc)
                    for c in range(NCH):
                        i = g.tensor_copy(self.TC[:, c, :],
                                          self.X[:, c, qc * QC:(qc + 1) * QC])
                    s.inc(i, cast_d, 1)

            @blk.scalar
            def _(sc):
                gate(sc)
                for qc in range(NQC):
                    sc.wait_ge(s.h(cast_d), b[cast_d] + qc + 1)
                    s.inc(sc.activation(SQB[:, :, :], self.TC[:, :, :], AF.Square), sq_d, 1)
                    sc.wait_ge(s.h(row_d), b[row_d] + 3 * qc + 1)
                    sc.activation(R[0:1, 3, :], R[0:1, 3, :], AF.Ln, bias=self.EPS[0:1, :])
                    sc.drain()
                    s.inc(sc.activation(R[0:1, 4, :], R[0:1, 3, :], AF.Exp,
                                        scale=self.NHALF[0:1, :]), row_d, 1)

            @blk.tensor
            def _(t):
                gate(t)
                for qc in range(NQC):
                    t.wait_ge(s.h(cast_d), b[cast_d] + qc + 1)
                    if qc >= 1:
                        t.wait_ge(s.h(row_d), b[row_d] + 3 * (qc - 1) + 1)
                    for c in range(NCH):
                        t.matmul(self.PS[6][0:1, :], self.ONECB[:], self.TC[:, c, :],
                                 start=(c == 0), stop=(c == NCH - 1))
                    t.wait_ge(s.h(sq_d), b[sq_d] + qc + 1)
                    for c in range(NCH):
                        i = t.matmul(self.PS[7][0:1, :], self.ONECB[:], SQB[:, c, :],
                                     start=(c == 0), stop=(c == NCH - 1))
                    s.inc(i, sum_d, 1)
                    t.wait_ge(s.h(row_d), b[row_d] + 3 * qc + 3)
                    if qc >= 1:
                        t.wait_ge(s.h(ap_d), b[ap_d] + qc)
                    if exact:
                        t.matmul(self.PS[4][:, :], self.ONE1F[:], R[0:1, 4, :], start=True, stop=True)
                        i = t.matmul(self.PS[5][:, :], self.ONE1F[:], R[0:1, 5, :], start=True, stop=True)
                    else:
                        t.matmul(self.PS[4][:, :], self.ONE1B[:], self.RB[0:1, 0, :], start=True, stop=True)
                        i = t.matmul(self.PS[5][:, :], self.ONE1B[:], self.RB[0:1, 1, :], start=True, stop=True)
                    s.inc(i, bc_d, 1)

            @blk.vector
            def _(v):
                gate(v)
                for qc in range(NQC):
                    v.wait_ge(s.h(sum_d), b[sum_d] + qc + 1)
                    v.tensor_scalar_mul(R[0:1, 2, :], self.PS[6][0:1, :], 1.0 / D)
                    v.tensor_scalar_mul(R[0:1, 3, :], self.PS[7][0:1, :], 1.0 / D)
                    v.drain()
                    v.tensor_tensor(self.TMP[0:1, 0, 0:QC], R[0:1, 2, :], R[0:1, 2, :], OP.mult)
                    v.drain()
                    s.inc(v.tensor_sub(R[0:1, 3, :], R[0:1, 3, :], self.TMP[0:1, 0, 0:QC]), row_d, 1)
                    v.wait_ge(s.h(row_d), b[row_d] + 3 * qc + 2)
                    v.tensor_mul(R[0:1, 5, :], R[0:1, 2, :], R[0:1, 4, :])
                    v.drain()
                    v.tensor_scalar_mul(R[0:1, 5, :], R[0:1, 5, :], -1.0)
                    v.drain()
                    if not exact:
                        v.tensor_copy(self.RB[0:1, 0, :], R[0:1, 4, :])
                        i = v.tensor_copy(self.RB[0:1, 1, :], R[0:1, 5, :])
                    else:
                        i = v.engine_nop()
                    s.inc(i, row_d, 1)
                    v.wait_ge(s.h(bc_d), b[bc_d] + qc + 1)
                    for c in range(NCH):
                        v.scalar_tensor_tensor(self.TMP[:, 0, :], self.X[:, c, qc * QC:(qc + 1) * QC],
                                               1.0, self.PS[4][:, :], OP.mult, OP.mult)
                        v.drain()
                        if dst == "xn":
                            out = self.XN(c, qc * QC, QC)
                        else:
                            out = self.X[:, c, qc * QC:(qc + 1) * QC]
                        i = v.tensor_add(out, self.TMP[:, 0, :], self.PS[5][:, :])
                        v.drain()
                    s.inc(i, ap_d, 1)
        lctx.close()
        return ap_d


    def attn_a(self, io, ln_ap, ln_base):
        nc, s = self.nc, self.s
        R, R32 = self.ROWS, self.R32
        NJ = [4 * (qc + 1) for qc in range(8)]
        UPREF = [0]
        for qc in range(8):
            for gp in range(4):
                UPREF.append(UPREF[-1] + NJ[qc])
        NU = UPREF[-1]

        lctx = ExitStack()
        E = lctx.enter_context
        WST = E(nc.sbuf_tensor("WST", [128, 2, NCH, 128], BF16))
        BATT = E(nc.sbuf_tensor("BATT", [128, 10], F32))
        BVR = E(nc.sbuf_tensor("BVR", [1, 256], BF16))
        STG = E(nc.sbuf_tensor("STG", [128, 2, QC], BF16))
        STGV = E(nc.sbuf_tensor("STGV", [128, 2, 2, 128], BF16))
        KS = E(nc.sbuf_tensor("KS", [128, NT], BF16))
        VS = E(nc.sbuf_tensor("VS", [128, 32, 2, 65], BF16))
        PT = E(nc.sbuf_tensor("PT", [128, 2, 2, QC], BF16))
        OT = E(nc.sbuf_tensor("OT", [128, 2, 8, QC], BF16))
        TMPB = E(nc.sbuf_tensor("TMPB", [128, QC], BF16))
        LBC = E(nc.sbuf_tensor("LBC", [128, 2, QC], F32))
        AM = E(nc.sbuf_tensor("AM", [128, 4, QC], BF16))
        WOA = E(nc.sbuf_tensor("WOA", [64, 8, NCH, 128], BF16))
        YPS = E(nc.sbuf_tensor("YPS", [128, 2, QC], F32))
        YB = E(nc.sbuf_tensor("YB", [128, QC], F32))
        BO = E(nc.sbuf_tensor("BO", [128, NCH], F32))
        HSTG = E(nc.sbuf_tensor("HSTG", [128, 2, 2048], BF16))
        FLG = E(nc.sbuf_tensor("FLG", [128, 2], F32))

        PS, G = self.PS, self.G
        QSv = lambda gp, c0, n: self.BIGA[:, gp * NT + c0: gp * NT + c0 + n]

        names = ("aw awc st_qk st_v ev_qk ev_v mm_qk mm_v cc_agf cc_agv rb_k rb_q rb_v "
                 "st_d exprm expm av_d lrow lbc otd b67r sq2 s2m ssum strow st_ar "
                 "cc_ar rb_ar arrow bcap apm wod wo_ev st_y cc_rs rb_y xadd lnabr awl stq0 stq1 stv0 stv1 sty0 sty1 star0 star1").split()
        b = {k: s.val(k) for k in names}

        def b67_use(qc, kind, i=0):
            g0 = (qc // 2) * 42
            odd = qc % 2
            if kind == "lbc":
                return g0 + odd * 5 + i
            if kind == "stats":
                return g0 + odd * 5 + 4
            if kind == "bc":
                return g0 + 10 + odd * 16 + i
            return g0 + 18 + odd * 16 + i

        def agf_row(oc):
            return (oc // 4) * 640 + (oc % 4) * 128 if oc < 8 else (oc - 8) * 640 + 512
        GBLK = {(0, 0): 0, (0, 1): 3, (1, 0): 1, (1, 1): 2}
        RSQ = [(0, 0), (0, 512), (1, 0), (1, 512), (1, 1024), (1, 1536), (0, 1024), (0, 1536)]

        with nc.Block() as blk:
            @blk.gpsimd
            def _(g):
                s.wait(g, "consts")
                s.inc(g.dma_start(out=BATT[:], in_=io["b_a"][:, :]), "aw", 16)
                s.inc(g.dma_start(out=BVR[:], in_=io["bv_row"][:, :]), "aw", 16)
                s.inc(g.dma_start(out=AM[:], in_=io["amask"][:, :]), "aw", 16)
                s.inc(g.dma_start(out=WOA[:], in_=io["wo_a"][0:64, :]), "aw", 16)
                s.inc(g.dma_start(out=R[32:33, 4:6, :], in_=io["lnab_a"][0:1, :]), "aw", 16)
                s.inc(g.dma_start(out=BO[:], in_=io["bo_fm"][:, :]), "aw", 16)
                s.inc(g.dma_start(out=FLG[:], in_=io["hflag"][:, :]), "aw", 16)
                for oc in range(10):
                    if oc >= 2:
                        g.wait_ge(s.h("mm_qk"), b["mm_qk"] + (oc - 1) * 4)
                    g.wait_ge(s.h("awl"), b["awl"] + oc * 16)
                    s.inc(g.dma_start(out=WST[:, oc % 2, :, :],
                                      in_=bass.AP(io["w_a"], oc * 128,
                                                  [[1536, 128], [1536 * 128, 8], [1, 128]])), "awl", 16)
                g.wait_ge(s.h("mm_qk"), b["mm_qk"] + 40)
                for vb in range(2):
                    g.wait_ge(s.h("awl"), b["awl"] + (10 + vb) * 16)
                    s.inc(g.dma_start(out=WST[:, vb, :, :],
                                      in_=bass.AP(io["w_a"], (10 + vb) * 128,
                                                  [[1536, 128], [1536 * 128, 8], [1, 128]])), "awl", 16)
                s.inc(g.memset(VS[:, :, :, 64:65], 1.0), "awc", 1)
                g.wait_ge(s.h("stq0"), b["stq0"] + 20 * 16)
                g.wait_ge(s.h("stq1"), b["stq1"] + 20 * 16)
                g.wait_ge(s.h("stv0"), b["stv0"] + 8 * 16)
                g.wait_ge(s.h("stv1"), b["stv1"] + 8 * 16)
                s.inc(g.collective_compute("AllGather", OP.bypass, replica_groups=PAIRS,
                                           ins=[io["agin_f"].ap().opt()],
                                           outs=[io["agout_f"].ap().opt()]), "cc_agf", 1)
                s.inc(g.collective_compute("AllGather", OP.bypass, replica_groups=PAIRS,
                                           ins=[io["agin_v"].ap().opt()],
                                           outs=[io["agout_v"].ap().opt()]), "cc_agv", 1)
                s.wait(g, "cc_agf")
                nu = 0
                for r in range(2):
                    g.wait_ge(s.h("lbc"), b["lbc"] + nu)
                    for hh in range(2):
                        row = r * 1280 + hh * 640 + 512
                        s.inc(g.dma_start(out=HSTG[:, hh, :],
                                          in_=bass.AP(io["agout_f"], row * T,
                                                      [[2048, 128], [1024, 2], [1, 1024]])), "rb_k", 16)
                    nu += 1
                for gp in range(4):
                    for r in range(2):
                        g.wait_ge(s.h("lbc"), b["lbc"] + nu)
                        for hh in range(2):
                            row = r * 1280 + hh * 640 + gp * 128
                            s.inc(g.dma_start(out=HSTG[:, hh, :],
                                              in_=bass.AP(io["agout_f"], row * T,
                                                          [[2048, 128], [1024, 2], [1, 1024]])), "rb_q", 16)
                        nu += 1
                s.wait(g, "cc_agv")
                for r in range(2):
                    for pos in range(2):
                        for lh in range(2):
                            g.wait_ge(s.h("lbc"), b["lbc"] + nu)
                            for hh in range(2):
                                off = (r * NT + hh * T + pos * 1024) * 128 + lh * 64
                                s.inc(g.dma_start(out=bass.AP(HSTG, hh * 2048,
                                                              [[2 * 2048, 128], [64, 8], [1, 64]]),
                                                  in_=bass.AP(io["agout_v"], off,
                                                              [[128, 128], [128 * 128, 8], [1, 64]])), "rb_v", 16)
                            nu += 1

            @blk.tensor
            def _(t):
                for oc in range(10):
                    t.wait_ge(s.h("awl"), b["awl"] + (oc + 1) * 16)
                    for tq in range(NQC):
                        u = oc * 4 + tq
                        t.wait_ge(s.h(ln_ap), ln_base + tq + 1)
                        if u >= 2:
                            t.wait_ge(s.h("ev_qk"), b["ev_qk"] + u - 1)
                        for kc in range(NCH):
                            i = t.matmul(PS[6 + u % 2][:, :], WST[:, oc % 2, kc, :],
                                         self.XN(kc, tq * QC, QC),
                                         start=(kc == 0), stop=(kc == NCH - 1))
                        s.inc(i, "mm_qk", 1)
                t.wait_ge(s.h("awl"), b["awl"] + 12 * 16)
                for tt in range(16):
                    t.wait_ge(s.h(ln_ap), ln_base + tt // 4 + 1)
                    if tt >= 2:
                        t.wait_ge(s.h("ev_v"), b["ev_v"] + tt - 1)
                    for kc in range(NCH):
                        t.matmul(PS[4 + tt % 2][:, 0:256],
                                 self.XN(kc, tt * 128, 128),
                                 WST[:, :, kc, :], start=(kc == 0), stop=False)
                    i = t.matmul(PS[4 + tt % 2][:, 0:256], self.ONE1B[:], BVR[:],
                                 start=False, stop=True)
                    s.inc(i, "mm_v", 1)

            @blk.vector
            def _(v):
                v.wait_ge(s.h("aw"), b["aw"] + 7 * 16)
                v.memset(R[64:65, 4, 0:128], 1.0)
                v.drain()
                s.inc(v.tensor_copy(R32[64:65, 2, 0:128], R[64:65, 4, 0:128]), "lnabr", 1)
                for u in range(40):
                    oc = u // 4
                    v.wait_ge(s.h("mm_qk"), b["mm_qk"] + u + 1)
                    if u >= 2:
                        v.wait_ge(s.h(f"stq{u % 2}"), b[f"stq{u % 2}"] + (u // 2) * 16)
                    s.inc(v.tensor_scalar(STG[:, u % 2, :], PS[6 + u % 2][:, :],
                                          BATT[:, oc:oc + 1], None, OP.add), "ev_qk", 1)
                for tt in range(16):
                    v.wait_ge(s.h("mm_v"), b["mm_v"] + tt + 1)
                    if tt >= 2:
                        v.wait_ge(s.h(f"stv{tt % 2}"), b[f"stv{tt % 2}"] + (tt // 2) * 16)
                    s.inc(v.tensor_copy(STGV[:, tt % 2, :, :],
                                        PS[4 + tt % 2][:, 0:256]), "ev_v", 1)
                nu = 0
                for r in range(2):
                    dst = bass.AP(KS, GBLK[(r, 0)] * 1024,
                                  [[NT, 128], [(GBLK[(r, 1)] - GBLK[(r, 0)]) * 1024, 2], [1, 1024]])
                    v.wait_ge(s.h("rb_k"), b["rb_k"] + (r + 1) * 32)
                    v.tensor_scalar_mul(HSTG[:, 0, :], HSTG[:, 0, :], FLG[:, 0:1])
                    v.drain()
                    s.inc(v.scalar_tensor_tensor(dst, HSTG[:, 1, :], FLG[:, 1:2],
                                                 HSTG[:, 0, :], OP.mult, OP.add), "lbc", 1)
                    v.drain()
                    nu += 1
                for gp in range(4):
                    for r in range(2):
                        dst = bass.AP(self.BIGA, gp * NT + GBLK[(r, 0)] * 1024,
                                      [[16384, 128], [(GBLK[(r, 1)] - GBLK[(r, 0)]) * 1024, 2], [1, 1024]])
                        v.wait_ge(s.h("rb_q"), b["rb_q"] + (gp * 2 + r + 1) * 32)
                        v.tensor_scalar_mul(HSTG[:, 0, :], HSTG[:, 0, :], FLG[:, 0:1])
                        v.drain()
                        s.inc(v.scalar_tensor_tensor(dst, HSTG[:, 1, :], FLG[:, 1:2],
                                                     HSTG[:, 0, :], OP.mult, OP.add), "lbc", 1)
                        v.drain()
                        nu += 1
                for r in range(2):
                    for pos in range(2):
                        gb = GBLK[(r, pos)]
                        for lh in range(2):
                            dst = bass.AP(VS, gb * 8 * 130 + lh * 65,
                                          [[32 * 130, 128], [130, 8], [1, 64]])
                            k = (r * 2 + pos) * 2 + lh
                            v.wait_ge(s.h("rb_v"), b["rb_v"] + (k + 1) * 32)
                            h0 = bass.AP(HSTG, 0, [[2 * 2048, 128], [64, 8], [1, 64]])
                            h1 = bass.AP(HSTG, 2048, [[2 * 2048, 128], [64, 8], [1, 64]])
                            v.tensor_scalar_mul(h0, h0, FLG[:, 0:1])
                            v.drain()
                            s.inc(v.scalar_tensor_tensor(dst, h1, FLG[:, 1:2],
                                                         h0, OP.mult, OP.add), "lbc", 1)
                            v.drain()
                s.inc(v.tensor_copy(R32[32:33, 5:7, :], R[32:33, 4:6, :]), "lnabr", 1)


            @blk.sync
            def _(sy):
                for u in range(40):
                    oc, tq = u // 4, u % 4
                    sy.wait_ge(s.h("ev_qk"), b["ev_qk"] + u + 1)
                    s.inc(sy.dma_start(out=io["agin_f"][agf_row(oc):agf_row(oc) + 128,
                                                        tq * QC:(tq + 1) * QC],
                                       in_=STG[:, u % 2, :]), f"stq{u % 2}", 16)
                for tt in range(16):
                    sy.wait_ge(s.h("ev_v"), b["ev_v"] + tt + 1)
                    dst = bass.AP(io["agin_v"], tt * 128 * 128,
                                  [[128, 128], [2048 * 128, 2], [1, 128]])
                    s.inc(sy.dma_start(out=dst, in_=STGV[:, tt % 2, :, :]), f"stv{tt % 2}", 16)

        if "o_ks" in io:
            with nc.Block() as blkd:
                @blkd.gpsimd
                def _(g):
                    bst = s.val("st")
                    s.inc(g.dma_start(out=io["o_ks"][:, :], in_=KS[:, :]), "st", 16)
                    for gp in range(4):
                        s.inc(g.dma_start(out=io["o_qs"][gp * 128:(gp + 1) * 128, :],
                                          in_=self.BIGA[:, gp * NT:(gp + 1) * NT]), "st", 16)
                    s.inc(g.dma_start(out=io["o_vs"][:, :],
                                      in_=bass.AP(VS, 0, [[32 * 130, 128], [1, 32 * 130]])), "st", 16)
                    g.wait_ge(s.h("st"), bst + 6 * 16)

        def epi_range(qc):
            return (qc - 1, qc) if qc % 2 == 1 else ()

        with nc.Block() as blk:
            @blk.tensor
            def _(t):
                t.wait_ge(s.h("lbc"), b["lbc"] + 18)
                t.wait_ge(s.h("awc"), b["awc"] + 1)
                t.wait_ge(s.h("ev_qk"), b["ev_qk"] + 40)
                t.wait_ge(s.h("ev_v"), b["ev_v"] + 16)
                for qc in range(8):
                    for gp in range(4):
                        w = qc * 4 + gp
                        u0, nj = UPREF[w], NJ[qc]
                        for j in range(nj):
                            u = u0 + j
                            if u >= 2:
                                t.wait_ge(s.h("expm"), b["expm"] + u - 1)
                            t.matmul(PS[2 * (u % 2)][:, :], KS[0:64, j * 128:(j + 1) * 128],
                                     QSv(gp, qc * QC, QC)[0:64, :], start=True, stop=True,
                                     tile_position=(0, 0))
                            s.inc(t.matmul(PS[2 * (u % 2) + 1][:, :],
                                           KS[64:128, j * 128:(j + 1) * 128],
                                           QSv(gp, qc * QC, QC)[64:128, :], start=True, stop=True,
                                           tile_position=(64, 0)), "st_d", 1)
                            for jj in ([j - 1] if j >= 1 else []) + ([j] if j == nj - 1 else []):
                                uu = u0 + jj
                                t.wait_ge(s.h("expm"), b["expm"] + uu + 1)
                                if jj == 0:
                                    t.wait_ge(s.h("otd"), b["otd"] + 2 * w)
                                for lh in range(2):
                                    i = t.matmul(PS[4 + lh][0:65, :], VS[:, jj, lh, :],
                                                 PT[:, uu % 2, lh, :],
                                                 start=(jj == 0), stop=(jj == nj - 1))
                                s.inc(i, "av_d", 1)
                        t.wait_ge(s.h("lrow"), b["lrow"] + w + 1)
                        t.wait_ge(s.h("b67r"), b["b67r"] + b67_use(qc, "lbc", gp))
                        t.wait_ge(s.h("lnabr"), b["lnabr"] + 2)
                        t.matmul(PS[6][0:64, :], R32[64:65, 2, 0:64], R32[64:65, 0, :],
                                 start=True, stop=True)
                        s.inc(t.matmul(PS[7][0:64, :], R32[64:65, 2, 0:64], R32[64:65, 1, :],
                                       start=True, stop=True), "lbc", 1)
                    t.wait_ge(s.h("otd"), b["otd"] + 8 * (qc + 1))
                    t.wait_ge(s.h("b67r"), b["b67r"] + b67_use(qc, "stats"))
                    for m in range(8):
                        i = t.matmul(PS[6][32:33, :], self.ONECB[0:64, :], OT[0:64, qc % 2, m, :],
                                     start=(m == 0), stop=(m == 7))
                    s.inc(i, "ssum", 1)
                    for rnd in range(4):
                        t.wait_ge(s.h("sq2"), b["sq2"] + qc * 4 + rnd + 1)
                        for mm2 in range(2):
                            i = t.matmul(PS[7][32:33, :], self.ONECB[0:64, :], STG[0:64, mm2, :],
                                         start=(rnd == 0 and mm2 == 0), stop=(rnd == 3 and mm2 == 1))
                        s.inc(i, "s2m", 1)
                    s.inc(t.drain(), "ssum", 1)
                    for q2 in epi_range(qc):
                        t.wait_ge(s.h("lnabr"), b["lnabr"] + 2)
                        for m in range(8):
                            t.wait_ge(s.h("arrow"), b["arrow"] + 3 * q2 + 3)
                            t.wait_ge(s.h("b67r"), b["b67r"] + b67_use(q2, "bc", m))
                            par = 3 * (q2 % 2)
                            lg_m = R32[32:33, 5, m * 64:(m + 1) * 64]
                            lb_m = R32[32:33, 6, m * 64:(m + 1) * 64]
                            t.matmul(PS[6][0:64, :], lg_m, R32[32:33, 0 + par, :],
                                     start=True, stop=True)
                            t.matmul(PS[7][0:64, :], lg_m, R32[32:33, 1 + par, :],
                                     start=True, stop=False)
                            s.inc(t.matmul(PS[7][0:64, :], lb_m, R32[32:33, 2, :],
                                           start=False, stop=True), "bcap", 1)
                        for dch in range(NCH):
                            t.wait_ge(s.h("apm"), b["apm"] + 8 * (q2 + 1))
                            t.wait_ge(s.h("b67r"), b["b67r"] + b67_use(q2, "wo", dch))
                            for m in range(8):
                                i = t.matmul(PS[6 + dch % 2][:, :], WOA[0:64, m, dch, :],
                                             OT[0:64, q2 % 2, m, :], start=(m == 0), stop=(m == 7))
                            s.inc(i, "wod", 1)

            @blk.scalar
            def _(sc):
                for qc in range(8):
                    for gp in range(4):
                        w = qc * 4 + gp
                        u0, nj = UPREF[w], NJ[qc]
                        for j in range(nj):
                            u = u0 + j
                            sc.wait_ge(s.h("st_d"), b["st_d"] + u + 1)
                            if u >= 2:
                                sc.wait_ge(s.h("av_d"), b["av_d"] + u - 1)
                            i = sc.activation(PT[:, u % 2, :, :], G[u % 2], AF.Exp,
                                              scale=self.C0125[:])
                            if j >= 4 * qc:
                                s.inc(i, "exprm", 1)
                            else:
                                s.inc(i, "expm", 1)
                    sc.wait_ge(s.h("otd"), b["otd"] + 8 * (qc + 1))
                    for rnd in range(4):
                        if qc * 4 + rnd >= 1:
                            sc.wait_ge(s.h("s2m"), b["s2m"] + qc * 4 + rnd)
                        s.inc(sc.activation(STG[0:64, :, :], OT[0:64, qc % 2, 2 * rnd:2 * rnd + 2, :],
                                            AF.Square), "sq2", 1)
                    for q2 in epi_range(qc):
                        sc.wait_ge(s.h("arrow"), b["arrow"] + 3 * q2 + 1)
                        sc.activation(R[32:33, 3, :], R[32:33, 3, :], AF.Ln, bias=self.EPS[32:33, :])
                        sc.drain()
                        s.inc(sc.activation(R[32:33, 4, :], R[32:33, 3, :], AF.Exp,
                                            scale=self.NHALF[32:33, :]), "arrow", 1)

            @blk.vector
            def _(v):
                nmask = 0
                for qc in range(8):
                    for gp in range(4):
                        w = qc * 4 + gp
                        u0, nj = UPREF[w], NJ[qc]
                        for j in range(4 * qc, nj):
                            u = u0 + j
                            k = j - 4 * qc
                            v.wait_ge(s.h("exprm"), b["exprm"] + nmask + 1)
                            nmask += 1
                            for lh in range(2):
                                i = v.tensor_mul(PT[:, u % 2, lh, :], PT[:, u % 2, lh, :],
                                                 AM[:, k, :])
                            s.inc(i, "expm", 1)
                        v.wait_ge(s.h("av_d"), b["av_d"] + u0 + nj)
                        v.tensor_copy(R[64:65, 0, :], PS[4][64:65, :])
                        v.tensor_copy(R[64:65, 1, :], PS[5][64:65, :])
                        v.drain()
                        v.reciprocal(R[64:65, 2, :], R[64:65, 0, :])
                        v.reciprocal(R[64:65, 3, :], R[64:65, 1, :])
                        v.drain()
                        v.tensor_copy(R32[64:65, 0, :], R[64:65, 2, :])
                        s.inc(v.tensor_copy(R32[64:65, 1, :], R[64:65, 3, :]), "lrow", 1)
                        v.wait_ge(s.h("lbc"), b["lbc"] + 18 + w + 1)
                        v.tensor_copy(LBC[0:64, 0, :], PS[6][0:64, :])
                        v.tensor_copy(LBC[0:64, 1, :], PS[7][0:64, :])
                        s.inc(v.drain(), "b67r", 1)
                        if qc >= 2:
                            v.wait_ge(s.h("wod"), b["wod"] + 8 * (qc - 1))
                        for lh in range(2):
                            m = gp * 2 + lh
                            s.inc(v.scalar_tensor_tensor(OT[0:64, qc % 2, m, :], PS[4 + lh][0:64, :],
                                                         1.0, LBC[0:64, lh, :], OP.mult, OP.mult),
                                  "otd", 1)
                    v.wait_ge(s.h("ssum"), b["ssum"] + 2 * (qc + 1))
                    if qc >= 1:
                        pn = f"star{(qc - 1) % 2}"
                        v.wait_ge(s.h(pn), b[pn] + ((qc - 1) // 2 + 1) * 32)
                    v.tensor_copy(R[32:33, 0, :], PS[6][32:33, :])
                    v.tensor_copy(R[32:33, 1, :], PS[7][32:33, :])
                    s.inc(v.drain(), "b67r", 1)
                    s.inc(v.drain(), "strow", 1)
                    for q2 in epi_range(qc):
                        v.wait_ge(s.h("rb_ar"), b["rb_ar"] + (q2 + 1) * 32)
                        v.tensor_scalar_mul(R[32:33, 2, :], R[32:33, 4, :], 1.0 / D)
                        v.tensor_scalar_mul(R[32:33, 3, :], R[32:33, 5, :], 1.0 / D)
                        v.drain()
                        v.tensor_tensor(self.TMP[32:33, 0, 0:QC], R[32:33, 2, :], R[32:33, 2, :], OP.mult)
                        v.drain()
                        s.inc(v.tensor_sub(R[32:33, 3, :], R[32:33, 3, :], self.TMP[32:33, 0, 0:QC]),
                              "arrow", 1)
                        v.wait_ge(s.h("arrow"), b["arrow"] + 3 * q2 + 2)
                        par = 3 * (q2 % 2)
                        v.tensor_mul(R[32:33, 5, :], R[32:33, 2, :], R[32:33, 4, :])
                        v.drain()
                        v.tensor_scalar_mul(R[32:33, 5, :], R[32:33, 5, :], -1.0)
                        v.drain()
                        v.tensor_copy(R32[32:33, 0 + par, :], R[32:33, 4, :])
                        s.inc(v.tensor_copy(R32[32:33, 1 + par, :], R[32:33, 5, :]), "arrow", 1)
                        for m in range(8):
                            v.wait_ge(s.h("bcap"), b["bcap"] + 8 * q2 + m + 1)
                            v.tensor_tensor(TMPB[0:64, :], OT[0:64, q2 % 2, m, :], PS[6][0:64, :],
                                            OP.mult)
                            v.drain()
                            v.tensor_add(OT[0:64, q2 % 2, m, :], TMPB[0:64, :], PS[7][0:64, :])
                            s.inc(v.drain(), "b67r", 1)
                            s.inc(v.drain(), "apm", 1)
                        for dch in range(NCH):
                            v.wait_ge(s.h("wod"), b["wod"] + 8 * q2 + dch + 1)
                            kk = 8 * q2 + dch
                            if kk >= 2:
                                v.wait_ge(s.h(f"sty{kk % 2}"), b[f"sty{kk % 2}"] + (kk // 2) * 16)
                            v.tensor_copy(YPS[:, dch % 2, :], PS[6 + dch % 2][:, :])
                            s.inc(v.drain(), "b67r", 1)
                            s.inc(v.drain(), "wo_ev", 1)
                if True:
                    v.wait_ge(s.h("rb_y"), b["rb_y"] + NCH * 16)
                    for c in range(NCH):
                        s.inc(v.tensor_scalar(self.X[:, c, :], self.X[:, c, :],
                                              BO[:, c:c + 1], None, OP.add), "xadd", 1)

            @blk.sync
            def _(sy):
                for qc in range(8):
                    sy.wait_ge(s.h("strow"), b["strow"] + qc + 1)
                    gidx, qoff = qc // 2, (qc % 2) * QC
                    arn = f"star{qc % 2}"
                    s.inc(sy.dma_start(out=io["ar_in"][gidx][0:1, qoff:qoff + QC],
                                       in_=R[32:33, 0, :]), arn, 16)
                    s.inc(sy.dma_start(out=io["ar_in"][gidx][1:2, qoff:qoff + QC],
                                       in_=R[32:33, 1, :]), arn, 16)
                    for q2 in epi_range(qc):
                        rr, cbase = RSQ[q2]
                        for dch in range(NCH):
                            sy.wait_ge(s.h("wo_ev"), b["wo_ev"] + 8 * q2 + dch + 1)
                            kk = 8 * q2 + dch
                            row0 = rr * 1024 + dch * 128
                            s.inc(sy.dma_start(out=io["rs_in"][row0:row0 + 128, cbase:cbase + QC],
                                               in_=YPS[:, dch % 2, :]), f"sty{kk % 2}", 16)

            @blk.gpsimd
            def _(g):
                for gidx in range(4):
                    g.wait_ge(s.h("star0"), b["star0"] + (gidx + 1) * 32)
                    g.wait_ge(s.h("star1"), b["star1"] + (gidx + 1) * 32)
                    s.inc(g.collective_compute("AllReduce", OP.add, replica_groups=PAIRS,
                                               ins=[io["ar_in"][gidx].ap().opt()],
                                               outs=[io["ar_out"][gidx].ap().opt()]), "cc_ar", 1)
                    g.wait_ge(s.h("cc_ar"), b["cc_ar"] + gidx + 1)
                    for qq in range(2):
                        qc = gidx * 2 + qq
                        qoff = qq * QC
                        if qc >= 1:
                            g.wait_ge(s.h("arrow"), b["arrow"] + 3 * (qc - 1) + 3)
                        g.wait_ge(s.h("lnabr"), b["lnabr"] + 2)
                        s.inc(g.dma_start(out=R[32:33, 4, :],
                                          in_=io["ar_out"][gidx][0:1, qoff:qoff + QC]), "rb_ar", 16)
                        s.inc(g.dma_start(out=R[32:33, 5, :],
                                          in_=io["ar_out"][gidx][1:2, qoff:qoff + QC]), "rb_ar", 16)
                g.wait_ge(s.h("sty0"), b["sty0"] + 32 * 16)
                g.wait_ge(s.h("sty1"), b["sty1"] + 32 * 16)
                s.inc(g.collective_compute("ReduceScatter", OP.add, replica_groups=PAIRS,
                                           ins=[io["rs_in"].ap().opt()],
                                           outs=[io["rs_out"].ap().opt()]), "cc_rs", 1)
                s.wait(g, "cc_rs")
                for c in range(NCH):
                    s.inc(g.dma_start(out=self.X[:, c, :],
                                      in_=io["rs_out"][c * 128:(c + 1) * 128, :],
                                      accum_op=OP.add), "rb_y", 16)
        if "o_lbc" in io:
            with nc.Block() as blkd2:
                @blkd2.gpsimd
                def _(g):
                    bst = s.val("st")
                    s.inc(g.dma_start(out=io["o_lbc"].ap()[0:64, :],
                                      in_=bass.AP(LBC, 0, [[2 * QC, 64], [1, 2 * QC]])), "st", 16)
                    s.inc(g.dma_start(out=bass.AP(io["o_r32"], 0, [[3584, 1], [512, 2], [1, 512]]),
                                      in_=self.R32[64:65, 0:2, :]), "st", 16)
                    s.inc(g.dma_start(out=io["o_ot"].ap()[0:64, :],
                                      in_=bass.AP(OT, 0, [[16 * QC, 64], [1, 16 * QC]])), "st", 16)
                    g.wait_ge(s.h("st"), bst + 3 * 16)
        lctx.close()
        return "xadd"

    def attn_b(self, io, ln_ap, ln_base, upto=3):
        nc, s = self.nc, self.s
        R, R32 = self.ROWS, self.R32
        PS, G = self.PS, self.G
        GBLK = {(0, 0): 0, (0, 1): 3, (1, 0): 1, (1, 1): 2}
        RSQ = [(0, 0), (0, 512), (1, 0), (1, 512), (1, 1024), (1, 1536), (0, 1024), (0, 1536)]

        ctxA = ExitStack()
        E = ctxA.enter_context
        HSTG = E(nc.sbuf_tensor("bHSTG", [128, 2, 2048], BF16))
        OTB = E(nc.sbuf_tensor("bOT", [128, 4, 8, QC], BF16))
        FLG = E(nc.sbuf_tensor("bFLG", [128, 2], F32))

        ctxP = ExitStack()
        EP = ctxP.enter_context
        WSTB = EP(nc.sbuf_tensor("bWST", [128, 2, NCH, 128], BF16))
        BQB = EP(nc.sbuf_tensor("bBQ", [128, 16], F32))
        BVRB = EP(nc.sbuf_tensor("bBVR", [1, 1024], BF16))
        TB = EP(nc.sbuf_tensor("bTB", [128, 2, T], BF16))
        ROTB = EP(nc.sbuf_tensor("bROT", [128, 128], BF16))
        STG = EP(nc.sbuf_tensor("bSTG", [128, 2, QC], BF16))
        STG2 = EP(nc.sbuf_tensor("bSG2", [128, 2, QC], BF16))
        TMPT = EP(nc.sbuf_tensor("bTMT", [128, QC], BF16))
        STGV = EP(nc.sbuf_tensor("bSGV", [128, 2, 2, 128], BF16))

        with nc.Block() as blk:
            b1 = {k: s.val(k) for k in ("aw awl mm_qk mm_v ev_qk ev_v sq2 exprm expm s2m "
                                        "stq0 stq1 stv0 stv1 cc_agf cc_agv lbc rb_q").split()}

            @blk.gpsimd
            def _(g):
                s.inc(g.dma_start(out=BQB[:], in_=io["b_bq"][:, :]), "aw", 16)
                s.inc(g.dma_start(out=BVRB[:], in_=io["bvr_b"][:, :]), "aw", 16)
                s.inc(g.dma_start(out=TB[:, 0, :], in_=io["b_tab"][:, 0:T]), "aw", 16)
                s.inc(g.dma_start(out=TB[:, 1, :], in_=io["b_tab"][:, T:2 * T]), "aw", 16)
                s.inc(g.dma_start(out=ROTB[:], in_=io["rotp"][:, :]), "aw", 16)
                s.inc(g.dma_start(out=FLG[:], in_=io["hflag"][:, :]), "aw", 16)
                for oc in range(16):
                    if oc >= 2:
                        g.wait_ge(s.h("mm_qk"), b1["mm_qk"] + (oc - 1) * 4)
                    g.wait_ge(s.h("awl"), b1["awl"] + oc * 16)
                    s.inc(g.dma_start(out=WSTB[:, oc % 2, :, :],
                                      in_=bass.AP(io["w_b"], oc * 128,
                                                  [[3072, 128], [3072 * 128, 8], [1, 128]])), "awl", 16)
                g.wait_ge(s.h("mm_qk"), b1["mm_qk"] + 64)
                for vbp in range(4):
                    if vbp >= 1:
                        g.wait_ge(s.h("mm_v"), b1["mm_v"] + vbp * 16)
                    for sl in range(2):
                        g.wait_ge(s.h("awl"), b1["awl"] + (16 + vbp * 2 + sl) * 16)
                        s.inc(g.dma_start(out=WSTB[:, sl, :, :],
                                          in_=bass.AP(io["w_b"], (16 + vbp * 2 + sl) * 128,
                                                      [[3072, 128], [3072 * 128, 8], [1, 128]])), "awl", 16)
                g.wait_ge(s.h("stq0"), b1["stq0"] + 32 * 16)
                g.wait_ge(s.h("stq1"), b1["stq1"] + 32 * 16)
                g.wait_ge(s.h("stv0"), b1["stv0"] + 32 * 16)
                g.wait_ge(s.h("stv1"), b1["stv1"] + 32 * 16)
                s.inc(g.collective_compute("AllGather", OP.bypass, replica_groups=PAIRS,
                                           ins=[io["agin_fb"].ap().opt()],
                                           outs=[io["agout_fb"].ap().opt()]), "cc_agf", 1)
                s.inc(g.collective_compute("AllGather", OP.bypass, replica_groups=PAIRS,
                                           ins=[io["agin_vb"].ap().opt()],
                                           outs=[io["agout_vb"].ap().opt()]), "cc_agv", 1)
                g.wait_ge(s.h("cc_agf"), b1["cc_agf"] + 1)
                for nu in range(8):
                    hp, r = nu // 2, nu % 2
                    g.wait_ge(s.h("lbc"), b1["lbc"] + nu)
                    for hh in range(2):
                        row = r * 2048 + hh * 1024 + hp * 128
                        s.inc(g.dma_start(out=HSTG[:, hh, :],
                                          in_=bass.AP(io["agout_fb"], row * T,
                                                      [[2048, 128], [1024, 2], [1, 1024]])), "rb_q", 16)

            @blk.tensor
            def _(t):
                t.wait_ge(s.h("aw"), b1["aw"] + 6 * 16)
                for oc in range(16):
                    t.wait_ge(s.h("awl"), b1["awl"] + (oc + 1) * 16)
                    for tq in range(NQC):
                        u = oc * 4 + tq
                        t.wait_ge(s.h(ln_ap), ln_base + tq + 1)
                        if u >= 2:
                            t.wait_ge(s.h("ev_qk"), b1["ev_qk"] + u - 1)
                        for kc in range(NCH):
                            i = t.matmul(PS[6 + u % 2][:, :], WSTB[:, oc % 2, kc, :],
                                         self.XN(kc, tq * QC, QC),
                                         start=(kc == 0), stop=(kc == NCH - 1))
                        s.inc(i, "mm_qk", 1)
                        for uu in ([u - 1] if u >= 1 else []) + ([u] if u == 63 else []):
                            t.wait_ge(s.h("ev_qk"), b1["ev_qk"] + uu + 1)
                            if uu >= 2:
                                t.wait_ge(s.h("exprm"), b1["exprm"] + uu - 1)
                            s.inc(t.matmul(PS[2 + uu % 2][:, :], ROTB[:, :], STG[:, uu % 2, :],
                                           start=True, stop=True), "sq2", 1)
                t.wait_ge(s.h("awl"), b1["awl"] + 18 * 16)
                for vbp in range(4):
                    for tt in range(16):
                        vt = vbp * 16 + tt
                        if vbp >= 1:
                            t.wait_ge(s.h("awl"), b1["awl"] + (16 + vbp * 2 + 2) * 16)
                        t.wait_ge(s.h(ln_ap), ln_base + tt // 4 + 1)
                        if vt >= 2:
                            t.wait_ge(s.h("ev_v"), b1["ev_v"] + vt - 1)
                        for kc in range(NCH):
                            t.matmul(PS[4 + vt % 2][:, 0:256],
                                     self.XN(kc, tt * 128, 128),
                                     WSTB[:, :, kc, :], start=(kc == 0), stop=False)
                        i = t.matmul(PS[4 + vt % 2][:, 0:256], self.ONE1B[:],
                                     BVRB[0:1, vbp * 256:(vbp + 1) * 256],
                                     start=False, stop=True)
                        s.inc(i, "mm_v", 1)

            @blk.vector
            def _(v):
                v.wait_ge(s.h("aw"), b1["aw"] + 6 * 16)
                for u in range(64):
                    oc, tq = u // 4, u % 4
                    v.wait_ge(s.h("mm_qk"), b1["mm_qk"] + u + 1)
                    if u >= 2:
                        v.wait_ge(s.h("sq2"), b1["sq2"] + u - 1)
                    v.drain()
                    s.inc(v.tensor_scalar(STG[:, u % 2, :], PS[6 + u % 2][:, :],
                                          BQB[:, oc:oc + 1], None, OP.add), "ev_qk", 1)
                    v.wait_ge(s.h("sq2"), b1["sq2"] + u + 1)
                    if u >= 2:
                        v.wait_ge(s.h(f"stq{u % 2}"), b1[f"stq{u % 2}"] + (u // 2) * 16)
                    v.tensor_mul(TMPT[:, :], STG[:, u % 2, :], TB[:, 0, tq * QC:(tq + 1) * QC])
                    v.scalar_tensor_tensor(STG2[:, u % 2, :], PS[2 + u % 2][:, :], 1.0,
                                           TB[:, 1, tq * QC:(tq + 1) * QC], OP.mult, OP.mult)
                    s.inc(v.drain(), "exprm", 1)
                    v.drain()
                    s.inc(v.tensor_add(STG2[:, u % 2, :], STG2[:, u % 2, :], TMPT[:, :]),
                          "expm", 1)
                for vt in range(64):
                    v.wait_ge(s.h("mm_v"), b1["mm_v"] + vt + 1)
                    if vt >= 2:
                        v.wait_ge(s.h(f"stv{vt % 2}"), b1[f"stv{vt % 2}"] + (vt // 2) * 16)
                    v.drain()
                    s.inc(v.tensor_copy(STGV[:, vt % 2, :, :],
                                        PS[4 + vt % 2][:, 0:256]), "ev_v", 1)
                for nu in range(8):
                    hp, r = nu // 2, nu % 2
                    dst = bass.AP(self.BIGA, hp * NT + GBLK[(r, 0)] * 1024,
                                  [[16384, 128], [(GBLK[(r, 1)] - GBLK[(r, 0)]) * 1024, 2], [1, 1024]])
                    v.wait_ge(s.h("rb_q"), b1["rb_q"] + (nu + 1) * 32)
                    v.tensor_scalar_mul(HSTG[:, 0, :], HSTG[:, 0, :], FLG[:, 0:1])
                    v.drain()
                    s.inc(v.scalar_tensor_tensor(dst, HSTG[:, 1, :], FLG[:, 1:2],
                                                 HSTG[:, 0, :], OP.mult, OP.add), "lbc", 1)
                    v.drain()

            @blk.sync
            def _(sy):
                for u in range(64):
                    oc, tq = u // 4, u % 4
                    sy.wait_ge(s.h("expm"), b1["expm"] + u + 1)
                    s.inc(sy.dma_start(out=io["agin_fb"][oc * 128:(oc + 1) * 128,
                                                         tq * QC:(tq + 1) * QC],
                                       in_=STG2[:, u % 2, :]), f"stq{u % 2}", 16)
                for vt in range(64):
                    vbp, tt = vt // 16, vt % 16
                    sy.wait_ge(s.h("ev_v"), b1["ev_v"] + vt + 1)
                    dst = bass.AP(io["agin_vb"], (vbp * 2 * 2048 + tt * 128) * 128,
                                  [[128, 128], [2048 * 128, 2], [1, 128]])
                    s.inc(sy.dma_start(out=dst, in_=STGV[:, vt % 2, :, :]), f"stv{vt % 2}", 16)
        ctxP.close()
        if upto == 1:
            ctxA.close()
            return "lbc"
        ctxB2 = ExitStack()
        EB = ctxB2.enter_context
        LBCB = EB(nc.sbuf_tensor("bLBC", [128, 2, QC], F32))
        TMPB2 = EB(nc.sbuf_tensor("bTM2", [128, QC], BF16))
        IDB = EB(nc.sbuf_tensor("bIDB", [64, 64], BF16))
        PT = EB(nc.sbuf_tensor("bPT", [128, 2, 2, QC], BF16))
        VS = EB(nc.sbuf_tensor("bVS", [128, 32, 2, 65], BF16))
        KB = EB(nc.sbuf_tensor("bKB", [128, NT], BF16))
        HSTGV = EB(nc.sbuf_tensor("bHSV", [128, 2, 8, 128], BF16))

        with nc.Block() as blk:
            b2 = {k: s.val(k) for k in ("s2m rb_k rb_v st_d exprm av_d lrow lbc otd b67r "
                                        "cc_agv lnabr awc aw tb1 ssum").split()}

            @blk.gpsimd
            def _(g):
                s.inc(g.dma_start(out=IDB[:, :], in_=io["eye64"][:, :]), "aw", 16)
                s.inc(g.memset(VS[:, :, :, 64:65], 1.0), "awc", 1)
                g.wait_ge(s.h("cc_agv"), b2["cc_agv"])
                for hp in range(4):
                    for r in range(2):
                        uk = hp * 6 + r
                        g.wait_ge(s.h("s2m"), b2["s2m"] + uk)
                        for hh in range(2):
                            row = r * 2048 + hh * 1024 + 512 + hp * 128
                            s.inc(g.dma_start(out=HSTG[:, hh, :],
                                              in_=bass.AP(io["agout_fb"], row * T,
                                                          [[2048, 128], [1024, 2], [1, 1024]])), "rb_k", 16)
                    for r in range(2):
                        for pos in range(2):
                            uk = hp * 6 + 2 + r * 2 + pos
                            g.wait_ge(s.h("s2m"), b2["s2m"] + uk)
                            for hh in range(2):
                                fc = 4 * hh + hp
                                off = (r * 16384 + fc * 2048 + pos * 1024) * 128
                                s.inc(g.dma_start(out=HSTGV[:, hh, :, :],
                                                  in_=bass.AP(io["agout_vb"], off,
                                                              [[128, 128], [128 * 128, 8], [1, 128]])), "rb_v", 16)

            @blk.vector
            def _(v):
                nun = 0
                for hp in range(4):
                    for r in range(2):
                        nun += 1
                        dst = bass.AP(KB, GBLK[(r, 0)] * 1024,
                                      [[NT, 128], [(GBLK[(r, 1)] - GBLK[(r, 0)]) * 1024, 2], [1, 1024]])
                        v.wait_ge(s.h("rb_k"), b2["rb_k"] + nun * 32)
                        if hp >= 1 and r == 0:
                            v.wait_ge(s.h("st_d"), b2["st_d"] + hp * 256)
                        v.tensor_scalar_mul(HSTG[:, 0, :], HSTG[:, 0, :], FLG[:, 0:1])
                        v.drain()
                        s.inc(v.scalar_tensor_tensor(dst, HSTG[:, 1, :], FLG[:, 1:2],
                                                     HSTG[:, 0, :], OP.mult, OP.add), "s2m", 1)
                        v.drain()
                    for r in range(2):
                        for pos in range(2):
                            nvv = hp * 4 + r * 2 + pos
                            v.wait_ge(s.h("rb_v"), b2["rb_v"] + (nvv + 1) * 32)
                            if hp >= 1 and r == 0 and pos == 0:
                                v.wait_ge(s.h("av_d"), b2["av_d"] + hp * 264)
                            gb8 = GBLK[(r, pos)] * 8
                            v.tensor_scalar_mul(HSTGV[:, 0, :, :], HSTGV[:, 0, :, :], FLG[:, 0:1])
                            v.drain()
                            for lh in range(2):
                                lo = lh * 65
                                dstv = bass.AP(VS, gb8 * 130 + lo, [[32 * 130, 128], [130, 8], [1, 64]])
                                i = v.scalar_tensor_tensor(
                                    dstv,
                                    bass.AP(HSTGV, 1024 + lh * 64, [[2048, 128], [128, 8], [1, 64]]),
                                    FLG[:, 1:2],
                                    bass.AP(HSTGV, lh * 64, [[2048, 128], [128, 8], [1, 64]]),
                                    OP.mult, OP.add)
                            s.inc(i, "s2m", 1)
                            v.drain()
                    for qc in range(8):
                        idx = hp * 8 + qc
                        v.wait_ge(s.h("av_d"), b2["av_d"] + (idx + 1) * 33)
                        v.tensor_copy(R[64:65, 0, :], PS[4][64:65, :])
                        v.tensor_copy(R[64:65, 1, :], PS[5][64:65, :])
                        v.drain()
                        v.reciprocal(R[64:65, 2, :], R[64:65, 0, :])
                        v.reciprocal(R[64:65, 3, :], R[64:65, 1, :])
                        v.drain()
                        v.tensor_copy(R32[64:65, 0, :], R[64:65, 2, :])
                        s.inc(v.tensor_copy(R32[64:65, 1, :], R[64:65, 3, :]), "lrow", 1)
                        v.wait_ge(s.h("lbc"), b2["lbc"] + idx + 1)
                        v.tensor_copy(LBCB[0:64, 0, :], PS[6][0:64, :])
                        v.tensor_copy(LBCB[0:64, 1, :], PS[7][0:64, :])
                        s.inc(v.drain(), "b67r", 1)
                        s.inc(v.scalar_tensor_tensor(OTB[0:64, hp, qc, :], PS[4][0:64, :],
                                                     1.0, LBCB[0:64, 0, :], OP.mult, OP.mult), "otd", 1)
                        s.inc(v.scalar_tensor_tensor(TMPB2[0:64, :], PS[5][0:64, :],
                                                     1.0, LBCB[0:64, 1, :], OP.mult, OP.mult), "tb1", 1)
                        v.wait_ge(s.h("ssum"), b2["ssum"] + idx + 1)
                        v.drain()
                        v.tensor_copy(OTB[64:128, hp, qc, :], PS[7][64:128, :])
                        s.inc(v.drain(), "b67r", 1)

            @blk.tensor
            def _(t):
                t.wait_ge(s.h("lbc"), b2["lbc"])
                t.wait_ge(s.h("lnabr"), b2["lnabr"])
                t.wait_ge(s.h("awc"), b2["awc"] + 1)
                for hp in range(4):
                    t.wait_ge(s.h("s2m"), b2["s2m"] + (hp + 1) * 6)
                    for qc in range(8):
                        idx = hp * 8 + qc
                        for j in range(32):
                            u = idx * 32 + j
                            if u >= 2:
                                t.wait_ge(s.h("exprm"), b2["exprm"] + u - 1)
                            t.matmul(PS[2 * (u % 2)][:, :], KB[0:64, j * 128:(j + 1) * 128],
                                     self.BIGA[0:64, hp * NT + qc * QC: hp * NT + (qc + 1) * QC],
                                     start=True, stop=True, tile_position=(0, 0))
                            s.inc(t.matmul(PS[2 * (u % 2) + 1][:, :],
                                           KB[64:128, j * 128:(j + 1) * 128],
                                           self.BIGA[64:128, hp * NT + qc * QC: hp * NT + (qc + 1) * QC],
                                           start=True, stop=True, tile_position=(64, 0)), "st_d", 1)
                            for jj in ([j - 1] if j >= 1 else []) + ([j] if j == 31 else []):
                                uu = idx * 32 + jj
                                t.wait_ge(s.h("exprm"), b2["exprm"] + uu + 1)
                                if jj == 0:
                                    t.wait_ge(s.h("otd"), b2["otd"] + idx)
                                    t.wait_ge(s.h("tb1"), b2["tb1"] + idx)
                                t.matmul(PS[4][0:65, :], VS[:, jj, 0, 0:65],
                                         PT[:, uu % 2, 0, :], start=(jj == 0), stop=(jj == 31))
                                i = t.matmul(PS[5][0:65, :], VS[:, jj, 1, 0:65],
                                             PT[:, uu % 2, 1, :], start=(jj == 0), stop=(jj == 31))
                                s.inc(i, "av_d", 2 if jj == 31 else 1)
                        t.wait_ge(s.h("lrow"), b2["lrow"] + idx + 1)
                        t.wait_ge(s.h("b67r"), b2["b67r"] + 2 * idx)
                        t.matmul(PS[6][0:64, :], R32[64:65, 2, 0:64], R32[64:65, 0, :],
                                 start=True, stop=True)
                        s.inc(t.matmul(PS[7][0:64, :], R32[64:65, 2, 0:64], R32[64:65, 1, :],
                                       start=True, stop=True), "lbc", 1)
                        t.wait_ge(s.h("tb1"), b2["tb1"] + idx + 1)
                        t.wait_ge(s.h("aw"), b2["aw"] + 16)
                        s.inc(t.matmul(PS[7][64:128, :], IDB[0:64, :], TMPB2[0:64, :],
                                       start=True, stop=True), "ssum", 1)

            @blk.scalar
            def _(sc):
                for idx in range(32):
                    for j in range(32):
                        u = idx * 32 + j
                        sc.wait_ge(s.h("st_d"), b2["st_d"] + u + 1)
                        if j >= 2:
                            sc.wait_ge(s.h("av_d"), b2["av_d"] + idx * 33 + j - 1)
                        elif idx >= 1:
                            sc.wait_ge(s.h("av_d"), b2["av_d"] + idx * 33)
                        s.inc(sc.activation(PT[:, u % 2, :, :], G[u % 2], AF.Exp,
                                            scale=self.C0125[:]), "exprm", 1)
        ctxB2.close()
        if upto == 2:
            ctxA.close()
            return "otd"
        ctxW = ExitStack()
        EW = ctxW.enter_context
        WOB = EW(nc.sbuf_tensor("bWO", [128, 4, NCH, 128], BF16))
        YPS = EW(nc.sbuf_tensor("bYPS", [128, 2, QC], F32))
        YB = EW(nc.sbuf_tensor("bYB", [128, QC], F32))
        BOB = EW(nc.sbuf_tensor("bBO", [128, NCH], F32))
        with nc.Block() as blk:
            b3 = {k: s.val(k) for k in ("aw wod wo_ev sty0 sty1 cc_rs rb_y xadd st_y").split()}

            @blk.gpsimd
            def _(g):
                s.inc(g.dma_start(out=WOB[:], in_=io["wo_b"][:, :]), "aw", 16)
                s.inc(g.dma_start(out=BOB[:], in_=io["bo_b"][:, :]), "aw", 16)
                g.wait_ge(s.h("sty0"), b3["sty0"] + 32 * 16)
                g.wait_ge(s.h("sty1"), b3["sty1"] + 32 * 16)
                if upto < 3:
                    return
                s.inc(g.collective_compute("ReduceScatter", OP.add, replica_groups=PAIRS,
                                           ins=[io["rs_in"].ap().opt()],
                                           outs=[io["rs_out"].ap().opt()]), "cc_rs", 1)
                g.wait_ge(s.h("cc_rs"), b3["cc_rs"] + 1)
                for c in range(NCH):
                    s.inc(g.dma_start(out=self.X[:, c, :],
                                      in_=io["rs_out"][c * 128:(c + 1) * 128, :],
                                      accum_op=OP.add), "rb_y", 16)

            @blk.tensor
            def _(t):
                t.wait_ge(s.h("aw"), b3["aw"] + 32)
                for q2 in range(8):
                    for dch in range(NCH):
                        kk = q2 * NCH + dch
                        if kk >= 2:
                            t.wait_ge(s.h("wo_ev"), b3["wo_ev"] + kk - 1)
                        for hp in range(4):
                            i = t.matmul(PS[6 + dch % 2][:, :], WOB[:, hp, dch, :],
                                         OTB[:, hp, q2, :], start=(hp == 0), stop=(hp == 3))
                        s.inc(i, "wod", 1)

            @blk.vector
            def _(v):
                for q2 in range(8):
                    for dch in range(NCH):
                        kk = q2 * NCH + dch
                        v.wait_ge(s.h("wod"), b3["wod"] + kk + 1)
                        if kk >= 2:
                            v.wait_ge(s.h(f"sty{kk % 2}"), b3[f"sty{kk % 2}"] + (kk // 2) * 16)
                        v.drain()
                        v.tensor_copy(YPS[:, dch % 2, :], PS[6 + dch % 2][:, :])
                        s.inc(v.drain(), "wo_ev", 1)
                if upto < 3:
                    return
                v.wait_ge(s.h("rb_y"), b3["rb_y"] + NCH * 16)
                for c in range(NCH):
                    s.inc(v.tensor_scalar(self.X[:, c, :], self.X[:, c, :],
                                          BOB[:, c:c + 1], None, OP.add), "xadd", 1)

            @blk.sync
            def _(sy):
                for q2 in range(8):
                    rr, cbase = RSQ[q2]
                    for dch in range(NCH):
                        kk = q2 * NCH + dch
                        sy.wait_ge(s.h("wo_ev"), b3["wo_ev"] + kk + 1)
                        row0 = rr * 1024 + dch * 128
                        s.inc(sy.dma_start(out=io["rs_in"][row0:row0 + 128, cbase:cbase + QC],
                                           in_=YPS[:, dch % 2, :]), f"sty{kk % 2}", 16)
        ctxW.close()
        ctxA.close()
        return "xadd"


    def ffn_call(self, io, cc):
        nc, s = self.nc, self.s
        with nc.Block() as blk:
            bsp = s.val("xld")

            @blk.sync
            def _(sy):
                for c in range(NCH):
                    s.inc(sy.dma_start(out=io["xspill"][c * 128:(c + 1) * 128, :],
                                       in_=self.X[:, c, :]), "xld", 16)
                sy.wait_ge(s.h("xld"), bsp + 128)
        self.topln(f"lnf{cc}", "xf")
        lctx = ExitStack()
        E = lctx.enter_context
        F = {}
        F["KF"] = E(nc.sbuf_tensor(f"KF{cc}", [128, 4, 1536], BF16))
        F["VF"] = E(nc.sbuf_tensor(f"VF{cc}", [128, 12, 8, 65], BF16))
        F["QF"] = E(nc.sbuf_tensor(f"QF{cc}", [128, 4, 1024], BF16))

        F["WSTF"] = E(nc.sbuf_tensor(f"WSTF{cc}", [128, 2, 8, 128], BF16))
        F["W1ST"] = E(nc.sbuf_tensor(f"W1S{cc}", [128, 2, 8, 256], BF16))
        F["PT"] = E(nc.sbuf_tensor(f"PTF{cc}", [128, 2, 2, QC], BF16))
        F["OTF"] = E(nc.sbuf_tensor(f"OTF{cc}", [128, 8, QC], BF16))
        F["XFH"] = E(nc.sbuf_tensor(f"XFH{cc}", [128, 2, 2, NCH], F32))
        F["QSQ"] = E(nc.sbuf_tensor(f"QSQ{cc}", [128, 2, QC], BF16))
        F["TMP2"] = E(nc.sbuf_tensor(f"TMP2{cc}", [128, QC], BF16))
        F["SEL2"] = E(nc.sbuf_tensor(f"SEL2{cc}", [128, 2], BF16))
        F["SEL2R"] = E(nc.sbuf_tensor(f"S2R{cc}", [2, 128], mybir.dt.float32r))
        F["QSK"] = E(nc.sbuf_tensor(f"QSK{cc}", [128, 2], F32))
        F["B1T"] = E(nc.sbuf_tensor(f"B1T{cc}", [128, 2, 22], F32))
        F["B2F"] = E(nc.sbuf_tensor(f"B2F{cc}", [128, NCH], F32))
        F["AMF"] = E(nc.sbuf_tensor(f"AMF{cc}", [128, 2, 128], BF16))
        F["ROTP"] = E(nc.sbuf_tensor(f"ROTP{cc}", [128, 128], BF16))
        F["FLG"] = E(nc.sbuf_tensor(f"FLGF{cc}", [128, 2], F32))
        with nc.Block() as blk:
            bi = s.val("aw")

            @blk.gpsimd
            def _(g):
                s.inc(g.dma_start(out=F["SEL2"][:], in_=io["sel2c"][:, :]), "aw", 16)
                s.inc(g.dma_start(out=self.ROWS[0:2, 0, 0:128], in_=io["sel2r"][0:2, :]), "aw", 16)
                s.inc(g.dma_start(out=F["AMF"][:], in_=io["amf"][:, :]), "aw", 16)
                s.inc(g.dma_start(out=F["ROTP"][:], in_=io["rotp"][:, :]), "aw", 16)
                s.inc(g.dma_start(out=F["FLG"][:], in_=io["hflag"][:, :]), "aw", 16)
                g.memset(F["VF"][:, :, :, 64:65], 1.0)
                s.inc(g.memset(F["OTF"][64:128, :, :], 0.0), "awc", 1)

            @blk.vector
            def _(v):
                v.memset(self.ROWS[32:33, 0, :], 1.0)
                v.wait_ge(s.h("aw"), bi + 80)
                v.drain()
                v.tensor_copy(F["SEL2R"][:], self.ROWS[0:2, 0, 0:128])
                s.inc(v.tensor_copy(self.R32[32:33, 6, :], self.ROWS[32:33, 0, :]), "awc", 1)
        for l in range(2):
            self.ffn_layer(io, F, cc, l)
        with nc.Block() as blk:
            brb = s.val("rb_y")

            @blk.gpsimd
            def _(g):
                for c in range(NCH):
                    s.inc(g.dma_start(out=self.X[:, c, :],
                                      in_=io["xspill"][c * 128:(c + 1) * 128, :],
                                      accum_op=OP.add), "rb_y", 16)
                g.wait_ge(s.h("rb_y"), brb + NCH * 16)
                s.inc(g.nop(), "xadd2", 1)
        lctx.close()
        return "xadd2"

    def ffn_layer(self, io, F, cc, l):
        nc, s = self.nc, self.s
        R, R32 = self.ROWS, self.R32
        XF = self.X
        with nc.Block() as blk:
            bb = {k: s.val(k) for k in ("stq0 fsl cc_agf rb_q lnabr strow").split()}

            @blk.vector
            def _(v):
                v.tensor_copy(R[:, 0, 0:NCH], XF[:, :, 1023:1024])
                s.inc(v.tensor_copy(R[:, 0, NCH:2 * NCH], XF[:, :, 2047:2048]), "strow", 1)

            @blk.sync
            def _(sy):
                sy.wait_ge(s.h("strow"), bb["strow"] + 1)
                s.inc(sy.dma_start(out=bass.AP(io["agx_in"], 0, [[NCH, 128], [1, NCH]]),
                                   in_=R[:, 0, 0:NCH]), "stq0", 16)
                s.inc(sy.dma_start(out=bass.AP(io["agx_in"], 1024, [[NCH, 128], [1, NCH]]),
                                   in_=R[:, 0, NCH:2 * NCH]), "stq0", 16)

            @blk.gpsimd
            def _(g):
                s.inc(g.dma_start(out=F["QSK"][:], in_=io[f"f_qsk{l}"][:, :]), "fsl", 16)
                s.inc(g.dma_start(out=F["B1T"][:], in_=io[f"f_b1{l}"][:, :]), "fsl", 16)
                s.inc(g.dma_start(out=F["B2F"][:], in_=io[f"f_b2{l}"][:, :]), "fsl", 16)
                s.inc(g.dma_start(out=R[32:33, 2:6, :], in_=io[f"f_lnab{l}"][0:1, :]), "fsl", 16)
                g.wait_ge(s.h("stq0"), bb["stq0"] + 32)
                s.inc(g.collective_compute("AllGather", OP.bypass, replica_groups=PAIRS,
                                           ins=[io["agx_in"].ap().opt()],
                                           outs=[io["agx_out"].ap().opt()]), "cc_agf", 1)
                g.wait_ge(s.h("cc_agf"), bb["cc_agf"] + 1)
                s.inc(g.dma_start(out=F["XFH"][:, 0, 0, :],
                                  in_=bass.AP(io["agx_out"], 0, [[NCH, 128], [1, NCH]])), "rb_q", 16)
                s.inc(g.dma_start(out=F["XFH"][:, 1, 0, :],
                                  in_=bass.AP(io["agx_out"], 3 * 1024, [[NCH, 128], [1, NCH]])), "rb_q", 16)
                s.inc(g.dma_start(out=R[:, 1, 0:NCH],
                                  in_=bass.AP(io["agx_out"], 2 * 1024, [[NCH, 128], [1, NCH]])), "rb_q", 16)

            @blk.vector
            def _(v2):
                v2.wait_ge(s.h("fsl"), bb["fsl"] + 4 * 16)
                s.inc(v2.tensor_copy(R32[32:33, 0:4, :], R[32:33, 2:6, :]), "lnabr", 1)
                v2.wait_ge(s.h("rb_q"), bb["rb_q"] + 48)
                v2.tensor_scalar_mul(F["XFH"][:, 0, 0, :], F["XFH"][:, 0, 0, :], F["FLG"][:, 1:2])
                v2.tensor_scalar_mul(F["XFH"][:, 1, 0, :], F["XFH"][:, 1, 0, :], F["FLG"][:, 0:1])
                v2.drain()
                s.inc(v2.scalar_tensor_tensor(F["XFH"][:, 1, 0, :], R[:, 1, 0:NCH], F["FLG"][:, 1:2],
                                              F["XFH"][:, 1, 0, :], OP.mult, OP.add), "lnabr", 1)
                v2.drain()
        for bki in range(2):
            self.ffn_block(io, F, cc, l, bki)

    def ffn_block(self, io, F, cc, l, bki):
        nc, s = self.nc, self.s
        R, R32, PS, G = self.ROWS, self.R32, self.PS, self.G
        XF = self.X
        KF, VF, QF = F["KF"], F["VF"], F["QF"]
        b1024 = bki * 1024

        def XSH(c, t0, n):
            return self.BIGA[:, c * T + b1024 + t0: c * T + b1024 + t0 + n]

        wq = io[f"f_wqkv{l}"]
        agk_in = io["agka_in"] if bki == 0 else io["agkb_in"]
        agk_out = io["agka_out"] if bki == 0 else io["agkb_out"]
        agm_in = io["agm1_in"] if bki == 0 else io["agm2_in"]
        agm_out = io["agm1_out"] if bki == 0 else io["agm2_out"]

        ctxF = ExitStack()
        F["FTAB"] = ctxF.enter_context(nc.sbuf_tensor(f"FTAB{cc}_{l}_{bki}", [128, 2, 1024], BF16))
        F["QNN"] = ctxF.enter_context(nc.sbuf_tensor(f"QNN{cc}_{l}_{bki}", [128, 2, QC], BF16))
        F["QRAW"] = ctxF.enter_context(nc.sbuf_tensor(f"QRAW{cc}_{l}_{bki}", [128, 2, QC], BF16))
        F["LBCF"] = ctxF.enter_context(nc.sbuf_tensor(f"LBCFp{cc}_{l}_{bki}", [128, QC], BF16))
        with nc.Block() as blk:
            bb = {k: s.val(k) for k in ("awc ftb fwq mm_qk mm_v lbc ev_qk ev_v st_d "
                                        "exprm expm av_d lrow otd sq2 s2m").split()}

            @blk.gpsimd
            def _(g):
                for c in range(NCH):
                    if c < 4:
                        i = g.tensor_copy(XSH(c, 0, 1024), XF[:, c, b1024:b1024 + 1024])
                    else:
                        g.tensor_copy(XSH(c, 0, 1), F["XFH"][:, bki, 0, c:c + 1])
                        i = g.tensor_copy(XSH(c, 1, 1023), XF[:, c, b1024:b1024 + 1023])
                s.inc(i, "awc", 1)
                for kind in range(2):
                    g.wait_ge(s.h("ftb"), bb["ftb"] + kind * 16)
                    s.inc(g.dma_start(out=F["FTAB"][:, kind, :],
                                      in_=io["f_tab"][:, kind * T + b1024: kind * T + b1024 + 1024]),
                          "ftb", 16)
                for oc in range(8):
                    if oc == 4:
                        g.wait_ge(s.h("otd"), bb["otd"] + 8)
                        for kind in range(2):
                            g.wait_ge(s.h("ftb"), bb["ftb"] + (2 + kind) * 16)
                            s.inc(g.dma_start(out=F["FTAB"][:, kind, :],
                                              in_=io["f_tab"][:, (2 + kind) * T + b1024: (2 + kind) * T + b1024 + 1024]),
                                  "ftb", 16)
                    if oc >= 2:
                        g.wait_ge(s.h("mm_qk"), bb["mm_qk"] + (oc - 1) * 2)
                    g.wait_ge(s.h("fwq"), bb["fwq"] + oc * 16)
                    s.inc(g.dma_start(out=F["WSTF"][:, oc % 2, :, :],
                                      in_=bass.AP(wq, oc * 128, [[1536, 128], [1536 * 128, 8], [1, 128]])),
                          "fwq", 16)
                g.wait_ge(s.h("mm_qk"), bb["mm_qk"] + 16)
                for vb in range(2):
                    g.wait_ge(s.h("fwq"), bb["fwq"] + (8 + vb) * 16)
                    s.inc(g.dma_start(out=F["W1ST"][:, vb, :, :],
                                      in_=bass.AP(wq, 1024 + vb * 256, [[1536, 128], [1536 * 128, 8], [1, 256]])),
                          "fwq", 16)

            @blk.tensor
            def _(t):
                t.wait_ge(s.h("awc"), bb["awc"] + 1)
                t.wait_ge(s.h("lnabr"), s.val("lnabr"))
                for oc in range(8):
                    t.wait_ge(s.h("fwq"), bb["fwq"] + (oc + 1) * 16)
                    for qcb in range(2):
                        u = oc * 2 + qcb
                        if u >= 2:
                            t.wait_ge(s.h("exprm"), bb["exprm"] + u - 1)
                        for kc in range(NCH):
                            i = t.matmul(PS[6 + u % 2][:, :], F["WSTF"][:, oc % 2, kc, :],
                                         XSH(kc, qcb * QC, QC), start=(kc == 0), stop=(kc == NCH - 1))
                        s.inc(i, "mm_qk", 1)
                        t.wait_ge(s.h("s2m"), bb["s2m"] + u + 1)
                        if u >= 2:
                            t.wait_ge(s.h("expm"), bb["expm"] + u - 1)
                        s.inc(t.matmul(PS[u % 2][0:2, :], F["SEL2"][:, :], F["QSQ"][:, u % 2, :],
                                       start=True, stop=True), "mm_v", 1)
                        t.wait_ge(s.h("lrow"), bb["lrow"] + u + 1)
                        if u >= 2:
                            t.wait_ge(s.h("otd"), bb["otd"] + u - 1)
                        s.inc(t.matmul(PS[2 + u % 2][:, :], F["SEL2R"][:, :], R32[0:2, 0, :],
                                       start=True, stop=True), "lbc", 1)
                        t.wait_ge(s.h("ev_qk"), bb["ev_qk"] + u + 1)
                        s.inc(t.matmul(PS[2 + u % 2][:, :], F["ROTP"][:, :], F["QNN"][:, u % 2, :],
                                       start=True, stop=True), "ev_v", 1)
                t.wait_ge(s.h("fwq"), bb["fwq"] + 10 * 16)
                for uv in range(16):
                    tt, vs = uv // 2, uv % 2
                    if uv >= 2:
                        t.wait_ge(s.h("sq2"), bb["sq2"] + uv - 1)
                    for kc in range(NCH):
                        i = t.matmul(PS[4 + uv % 2][:, 0:256], XSH(kc, tt * 128, 128),
                                     F["W1ST"][:, vs, kc, :], start=(kc == 0), stop=(kc == NCH - 1))
                    s.inc(i, "st_d", 1)

            @blk.scalar
            def _(sc):
                for u in range(16):
                    sc.wait_ge(s.h("exprm"), bb["exprm"] + u + 1)
                    if u >= 2:
                        sc.wait_ge(s.h("mm_v"), bb["mm_v"] + u - 1)
                    s.inc(sc.activation(F["QSQ"][:, u % 2, :], F["QRAW"][:, u % 2, :], AF.Square),
                          "s2m", 1)
                    sc.wait_ge(s.h("expm"), bb["expm"] + u + 1)
                    sc.activation(R[0:2, 3, :], R[0:2, 3, :], AF.Ln, bias=self.EPS[0:2, :])
                    sc.drain()
                    s.inc(sc.activation(R[0:2, 4, :], R[0:2, 3, :], AF.Exp, scale=self.NHALF[0:2, :]),
                          "av_d", 1)

            @blk.vector
            def _(v):
                for u in range(16):
                    oc, qcb = u // 2, u % 2
                    isq = oc < 4
                    v.wait_ge(s.h("mm_qk"), bb["mm_qk"] + u + 1)
                    if u >= 2:
                        v.wait_ge(s.h("s2m"), bb["s2m"] + u - 1)
                    v.drain()
                    s.inc(v.tensor_copy(F["QRAW"][:, u % 2, :], PS[6 + u % 2][:, :]), "exprm", 1)
                    v.wait_ge(s.h("mm_v"), bb["mm_v"] + u + 1)
                    v.drain()
                    s.inc(v.tensor_copy(R[0:2, 3, :], PS[u % 2][0:2, :]), "expm", 1)
                    v.wait_ge(s.h("av_d"), bb["av_d"] + u + 1)
                    v.drain()
                    s.inc(v.tensor_copy(R32[0:2, 0, :], R[0:2, 4, :]), "lrow", 1)
                    v.wait_ge(s.h("lbc"), bb["lbc"] + u + 1)
                    if u >= 2:
                        v.wait_ge(s.h("ev_v"), bb["ev_v"] + u - 1)
                    s.inc(v.scalar_tensor_tensor(F["QNN"][:, u % 2, :], F["QRAW"][:, u % 2, :],
                                                 F["QSK"][:, (0 if isq else 1):(1 if isq else 2)],
                                                 PS[2 + u % 2][:, :], OP.mult, OP.mult), "ev_qk", 1)
                    v.wait_ge(s.h("ev_v"), bb["ev_v"] + u + 1)
                    v.wait_ge(s.h("ftb"), bb["ftb"] + (32 if isq else 64))
                    tabc = F["FTAB"][:, 0, qcb * QC:(qcb + 1) * QC]
                    tabs = F["FTAB"][:, 1, qcb * QC:(qcb + 1) * QC]
                    v.drain()
                    v.tensor_mul(F["TMP2"][:, :], F["QNN"][:, u % 2, :], tabc)
                    v.scalar_tensor_tensor(F["LBCF"][:, :], PS[2 + u % 2][:, :], 1.0,
                                           tabs, OP.mult, OP.mult)
                    v.drain()
                    if isq:
                        dst = QF[:, oc, qcb * QC:(qcb + 1) * QC]
                    else:
                        dst = KF[:, oc - 4, 512 + qcb * QC: 512 + (qcb + 1) * QC]
                    s.inc(v.tensor_add(dst, F["TMP2"][:, :], F["LBCF"][:, :]), "otd", 1)
                for uv in range(16):
                    tt, vs = uv // 2, uv % 2
                    v.wait_ge(s.h("st_d"), bb["st_d"] + uv + 1)
                    s.inc(v.tensor_copy(VF[:, 4 + tt, vs * 4:(vs + 1) * 4, 0:64],
                                        PS[4 + uv % 2][:, 0:256]), "sq2", 1)

        ctxF.close()
        with nc.Block() as blk:
            bb = {k: s.val(k) for k in ("star0 cc_agv rb_k").split()}

            @blk.sync
            def _(sy):
                s.inc(sy.dma_start(out=agk_in[0:128, 0:2048], in_=KF[:, :, 1024:1536]), "star0", 16)
                s.inc(sy.dma_start(out=bass.AP(agk_in, 2048, [[4128, 128], [520, 4], [65, 8], [1, 65]]),
                                   in_=VF[:, 8:12, :, :]), "star0", 16)
                if bki == 0:
                    s.inc(sy.dma_start(out=io["agkb_in"][128:256, 0:2048], in_=KF[:, :, 1024:1536]),
                          "star0", 16)
                    s.inc(sy.dma_start(out=bass.AP(io["agkb_in"], 128 * 4128 + 2048,
                                                   [[4128, 128], [520, 4], [65, 8], [1, 65]]),
                                       in_=VF[:, 8:12, :, :]), "star0", 16)

            @blk.gpsimd
            def _(g):
                g.wait_ge(s.h("star0"), bb["star0"] + (64 if bki == 0 else 32))
                s.inc(g.collective_compute("AllGather", OP.bypass, replica_groups=PAIRS,
                                           ins=[agk_in.ap().opt()], outs=[agk_out.ap().opt()]), "cc_agv", 1)
                g.wait_ge(s.h("cc_agv"), bb["cc_agv"] + 1)
                if bki == 0:
                    s.inc(g.dma_start(out=KF[:, :, 0:512],
                                      in_=bass.AP(agk_out, 0, [[4128, 128], [512, 4], [1, 512]])), "rb_k", 16)
                    s.inc(g.dma_start(out=VF[:, 0:4, :, :],
                                      in_=bass.AP(agk_out, 2048, [[4128, 128], [520, 4], [65, 8], [1, 65]])), "rb_k", 16)
                else:
                    s.inc(g.dma_start(out=KF[:, :, 0:512],
                                      in_=bass.AP(agk_out, 256 * 4128, [[4128, 128], [512, 4], [1, 512]])), "rb_k", 16)
                    s.inc(g.dma_start(out=VF[:, 0:4, :, :],
                                      in_=bass.AP(agk_out, 256 * 4128 + 2048, [[4128, 128], [520, 4], [65, 8], [1, 65]])), "rb_k", 16)
                    s.inc(g.dma_start(out=bass.AP(F["PT"], 0, [[2048, 128], [512, 4], [1, 512]]),
                                      in_=bass.AP(agk_out, 384 * 4128, [[4128, 128], [512, 4], [1, 512]])), "rb_k", 16)
                    s.inc(g.dma_start(out=bass.AP(F["W1ST"], 0, [[4096, 128], [520, 4], [65, 8], [1, 65]]),
                                      in_=bass.AP(agk_out, 384 * 4128 + 2048, [[4128, 128], [520, 4], [65, 8], [1, 65]])), "rb_k", 16)

            @blk.vector
            def _(v):
                n_dma = 2 if bki == 0 else 4
                v.wait_ge(s.h("rb_k"), bb["rb_k"] + n_dma * 16)
                fsel = F["FLG"][:, 1:2]
                if bki == 0:
                    v.tensor_scalar_mul(KF[:, :, 0:512], KF[:, :, 0:512], fsel)
                    v.drain()
                    s.inc(v.tensor_scalar_mul(VF[:, 0:4, :, :], VF[:, 0:4, :, :], fsel), "s2m", 1)
                    v.drain()
                else:
                    v.tensor_scalar_mul(KF[:, :, 0:512], KF[:, :, 0:512], F["FLG"][:, 0:1])
                    v.tensor_scalar_mul(VF[:, 0:4, :, :], VF[:, 0:4, :, :], F["FLG"][:, 0:1])
                    v.drain()
                    v.scalar_tensor_tensor(KF[:, :, 0:512],
                                           bass.AP(F["PT"], 0, [[2048, 128], [512, 4], [1, 512]]),
                                           fsel, KF[:, :, 0:512], OP.mult, OP.add)
                    v.drain()
                    s.inc(v.scalar_tensor_tensor(VF[:, 0:4, :, :],
                                                 bass.AP(F["W1ST"], 0, [[4096, 128], [520, 4], [65, 8], [1, 65]]),
                                                 fsel, VF[:, 0:4, :, :], OP.mult, OP.add), "s2m", 1)
                    v.drain()

        if "o_qf" in io and cc == 0 and l == 0 and bki == 0:
            with nc.Block() as blkd:
                @blkd.gpsimd
                def _(g):
                    bst = s.val("st")
                    s.inc(g.dma_start(out=io["o_qf"].ap()[:, :],
                                      in_=bass.AP(QF, 0, [[4 * 1024, 128], [1, 4 * 1024]])), "st", 16)
                    s.inc(g.dma_start(out=io["o_kf"].ap()[:, :],
                                      in_=bass.AP(KF, 0, [[4 * 1536, 128], [1, 4 * 1536]])), "st", 16)
                    s.inc(g.dma_start(out=io["o_vf"].ap()[:, :],
                                      in_=bass.AP(VF, 0, [[12 * 8 * 65, 128], [1, 12 * 8 * 65]])), "st", 16)
                    g.wait_ge(s.h("st"), bst + 3 * 16)

        ctxW2 = ExitStack()
        F["WOF"] = ctxW2.enter_context(nc.sbuf_tensor(f"WOF{cc}_{l}_{bki}", [64, 8, 2, 128], BF16))
        F["LBCF"] = ctxW2.enter_context(nc.sbuf_tensor(f"LBCF{cc}_{l}_{bki}", [128, QC], BF16))
        with nc.Block() as blk:
            bq = {k: s.val(k) for k in ("st_d expm exprm av_d lrow lbc otd wod b67r apm s2m aw").split()}

            def live_i(j):
                return [i for i in range(4) if (j < 4 and j >= i) or (j >= 4 and i >= j - 4)]

            @blk.gpsimd
            def _(g):
                for w in range(2):
                    for dch in range(NCH):
                        idx = w * 8 + dch
                        if idx >= 2:
                            g.wait_ge(s.h("wod"), bq["wod"] + idx - 1)
                        g.wait_ge(s.h("aw"), bq["aw"] + idx * 16)
                        s.inc(g.dma_start(out=F["WOF"][0:64, :, idx % 2, :],
                                          in_=bass.AP(io[f"f_wo{l}"], dch * 128,
                                                      [[8192, 64], [1024, 8], [1, 128]])), "aw", 16)

            @blk.tensor
            def _(t):
                t.wait_ge(s.h("s2m"), bq["s2m"])
                nj = 0
                nf67 = 0
                for w in range(2):
                    for hp in range(4):
                        for j in range(8):
                            p2 = nj % 2
                            if nj >= 2:
                                t.wait_ge(s.h("expm"), bq["expm"] + nj - 1)
                            t.matmul(PS[2 * p2][:, :],
                                     KF[0:64, hp, (w * 4 + j) * 128:(w * 4 + j + 1) * 128],
                                     QF[0:64, hp, w * QC: (w + 1) * QC],
                                     start=True, stop=True, tile_position=(0, 0))
                            i2 = t.matmul(PS[2 * p2 + 1][:, :],
                                          KF[64:128, hp, (w * 4 + j) * 128:(w * 4 + j + 1) * 128],
                                          QF[64:128, hp, w * QC: (w + 1) * QC],
                                          start=True, stop=True, tile_position=(64, 0))
                            s.inc(i2, "st_d", 1)
                            for jj in ([j - 1] if j >= 1 else []) + ([j] if j == 7 else []):
                                pj = (nj - (1 if jj == j - 1 else 0)) % 2
                                t.wait_ge(s.h("expm"), bq["expm"] + nj + (1 if jj == j else 0))
                                if jj == 0:
                                    t.wait_ge(s.h("otd"), bq["otd"] + 2 * (w * 4 + hp))
                                for lh in range(2):
                                    i2 = t.matmul(PS[4 + lh][0:65, :], VF[:, w * 4 + jj, hp * 2 + lh, :],
                                                  F["PT"][:, pj, lh, :], start=(jj == 0), stop=(jj == 7))
                                s.inc(i2, "av_d", 1)
                            nj += 1
                        t.wait_ge(s.h("lrow"), bq["lrow"] + w * 4 + hp + 1)
                        t.wait_ge(s.h("b67r"), bq["b67r"] + nf67)
                        nf67 += 1
                        t.matmul(PS[6][0:64, :], R32[64:65, 2, 0:64], R32[64:65, 0, :],
                                 start=True, stop=True)
                        s.inc(t.matmul(PS[7][0:64, :], R32[64:65, 2, 0:64], R32[64:65, 1, :],
                                       start=True, stop=True), "lbc", 1)
                    t.wait_ge(s.h("otd"), bq["otd"] + (w + 1) * 8)
                    for dch in range(NCH):
                        idx = w * 8 + dch
                        t.wait_ge(s.h("aw"), bq["aw"] + (idx + 1) * 16)
                        t.wait_ge(s.h("b67r"), bq["b67r"] + nf67)
                        nf67 += 1
                        for m in range(8):
                            i2 = t.matmul(PS[6 + dch % 2][:, :], F["WOF"][0:64, m, idx % 2, :],
                                          F["OTF"][0:64, m, :], start=(m == 0), stop=(m == 7))
                        s.inc(i2, "wod", 1)

            @blk.scalar
            def _(sc):
                nj = 0
                for w in range(2):
                    for hp in range(4):
                        for j in range(8):
                            p2 = nj % 2
                            sc.wait_ge(s.h("st_d"), bq["st_d"] + nj + 1)
                            if nj >= 2:
                                sc.wait_ge(s.h("av_d"), bq["av_d"] + nj - 1)
                            s.inc(sc.activation(F["PT"][:, p2, :, :], G[p2], AF.Exp), "exprm", 1)
                            nj += 1

            @blk.vector
            def _(v):
                nj = 0
                for w in range(2):
                    for hp in range(4):
                        for j in range(8):
                            p2 = nj % 2
                            v.wait_ge(s.h("exprm"), bq["exprm"] + nj + 1)
                            for lh in range(2):
                                if j < 4:
                                    v.tensor_mul(F["PT"][:, p2, lh, j * 128:(j + 1) * 128],
                                                 F["PT"][:, p2, lh, j * 128:(j + 1) * 128],
                                                 F["AMF"][:, 1, :])
                                    if j < 3:
                                        v.memset(F["PT"][:, p2, lh, (j + 1) * 128:512], 0.0)
                                else:
                                    jd = j - 4
                                    v.tensor_mul(F["PT"][:, p2, lh, jd * 128:(jd + 1) * 128],
                                                 F["PT"][:, p2, lh, jd * 128:(jd + 1) * 128],
                                                 F["AMF"][:, 0, :])
                                    if jd > 0:
                                        v.memset(F["PT"][:, p2, lh, 0:jd * 128], 0.0)
                            s.inc(v.drain(), "expm", 1)
                            nj += 1
                        v.wait_ge(s.h("av_d"), bq["av_d"] + (w * 4 + hp + 1) * 8)
                        v.tensor_copy(R[64:65, 0, :], PS[4][64:65, :])
                        v.tensor_copy(R[64:65, 1, :], PS[5][64:65, :])
                        v.drain()
                        v.reciprocal(R[64:65, 2, :], R[64:65, 0, :])
                        v.reciprocal(R[64:65, 3, :], R[64:65, 1, :])
                        v.drain()
                        v.tensor_copy(R32[64:65, 0, :], R[64:65, 2, :])
                        s.inc(v.tensor_copy(R32[64:65, 1, :], R[64:65, 3, :]), "lrow", 1)
                        v.wait_ge(s.h("lbc"), bq["lbc"] + w * 4 + hp + 1)
                        if w == 1 and hp == 0:
                            v.wait_ge(s.h("wod"), bq["wod"] + 8)
                        v.tensor_copy(F["LBCF"][0:64, :], PS[6][0:64, :])
                        v.drain()
                        s.inc(v.scalar_tensor_tensor(F["OTF"][0:64, hp * 2, :],
                                                     PS[4][0:64, :], 1.0,
                                                     F["LBCF"][0:64, :], OP.mult, OP.mult), "otd", 1)
                        v.drain()
                        v.tensor_copy(F["LBCF"][0:64, :], PS[7][0:64, :])
                        s.inc(v.drain(), "b67r", 1)
                        s.inc(v.scalar_tensor_tensor(F["OTF"][0:64, hp * 2 + 1, :],
                                                     PS[5][0:64, :], 1.0,
                                                     F["LBCF"][0:64, :], OP.mult, OP.mult), "otd", 1)
                    for dch in range(NCH):
                        v.wait_ge(s.h("wod"), bq["wod"] + w * 8 + dch + 1)
                        v.scalar_tensor_tensor(XF[:, dch, b1024 + w * QC: b1024 + (w + 1) * QC],
                                                    PS[6 + dch % 2][:, :], 1.0,
                                                    XF[:, dch, b1024 + w * QC: b1024 + (w + 1) * QC],
                                                    OP.mult, OP.add)
                        s.inc(v.drain(), "b67r", 1)
                        s.inc(v.drain(), "apm", 1)

        if "o_xa" in io and cc == 0 and l == 0 and bki == 0:
            with nc.Block() as blkd:
                @blkd.gpsimd
                def _(g):
                    bst = s.val("st")
                    for c in range(NCH):
                        s.inc(g.dma_start(out=io["o_xa"].ap()[c * 128:(c + 1) * 128, :],
                                          in_=XF[:, c, 0:1024]), "st", 16)
                    g.wait_ge(s.h("st"), bst + 8 * 16)

        ctxW2.close()
        with nc.Block() as blk:
            bb = {k: s.val(k) for k in ("star1 cc_ar rb_v strow").split()}

            @blk.vector
            def _(v):
                s.inc(v.tensor_copy(R[:, 0, 0:NCH], XF[:, :, b1024 + 1023:b1024 + 1024]), "strow", 1)

            @blk.sync
            def _(sy):
                sy.wait_ge(s.h("strow"), bb["strow"] + 1)
                s.inc(sy.dma_start(out=bass.AP(agm_in, 0, [[NCH, 128], [1, NCH]]),
                                   in_=R[:, 0, 0:NCH]), "star1", 16)
                if bki == 0:
                    s.inc(sy.dma_start(out=bass.AP(io["agm2_in"], 1024, [[NCH, 128], [1, NCH]]),
                                       in_=R[:, 0, 0:NCH]), "star1", 16)

            @blk.gpsimd
            def _(g):
                g.wait_ge(s.h("star1"), bb["star1"] + (32 if bki == 0 else 16))
                s.inc(g.collective_compute("AllGather", OP.bypass, replica_groups=PAIRS,
                                           ins=[agm_in.ap().opt()], outs=[agm_out.ap().opt()]), "cc_ar", 1)
                g.wait_ge(s.h("cc_ar"), bb["cc_ar"] + 1)
                if bki == 0:
                    s.inc(g.dma_start(out=F["XFH"][:, 0, 1, :],
                                      in_=bass.AP(agm_out, 0, [[NCH, 128], [1, NCH]])), "rb_v", 16)
                else:
                    s.inc(g.dma_start(out=F["XFH"][:, 1, 1, :],
                                      in_=bass.AP(agm_out, 2 * 1024, [[NCH, 128], [1, NCH]])), "rb_v", 16)
                    s.inc(g.dma_start(out=R[:, 1, 0:NCH],
                                      in_=bass.AP(agm_out, 3 * 1024, [[NCH, 128], [1, NCH]])), "rb_v", 16)

            @blk.vector
            def _(v):
                n_dma = 1 if bki == 0 else 2
                v.wait_ge(s.h("rb_v"), bb["rb_v"] + n_dma * 16)
                if bki == 0:
                    s.inc(v.tensor_scalar_mul(F["XFH"][:, 0, 1, :], F["XFH"][:, 0, 1, :],
                                              F["FLG"][:, 1:2]), "strow", 1)
                    v.drain()
                else:
                    v.tensor_scalar_mul(F["XFH"][:, 1, 1, :], F["XFH"][:, 1, 1, :], F["FLG"][:, 0:1])
                    v.drain()
                    s.inc(v.scalar_tensor_tensor(F["XFH"][:, 1, 1, :], R[:, 1, 0:NCH], F["FLG"][:, 1:2],
                                                 F["XFH"][:, 1, 1, :], OP.mult, OP.add), "strow", 1)
                    v.drain()

        ctxH = ExitStack()
        F["H2"] = ctxH.enter_context(nc.sbuf_tensor(f"H2{cc}_{l}_{bki}", [128, 22, 512], BF16))
        with nc.Block() as blk:
            bm = {k: s.val(k) for k in ("awc sq2 ssum arrow bcap apm lrow mm_qk mm_v st_d "
                                        "fw1s fw2s xadd rb_v").split()}

            @blk.gpsimd
            def _(g):
                g.wait_ge(s.h("strow"), s.val("strow"))
                for c in range(NCH):
                    if c < 4:
                        i = g.tensor_copy(XSH(c, 0, 1024), XF[:, c, b1024:b1024 + 1024])
                    else:
                        g.tensor_copy(XSH(c, 0, 1), F["XFH"][:, bki, 1, c:c + 1])
                        i = g.tensor_copy(XSH(c, 1, 1023), XF[:, c, b1024:b1024 + 1023])
                s.inc(i, "awc", 1)
                nsl = 0
                for qc2 in range(2):
                    for p in range(22):
                        gidx = qc2 * 22 + p
                        if gidx >= 2:
                            g.wait_ge(s.h("mm_qk"), bm["mm_qk"] + (gidx - 1) * 2)
                        g.wait_ge(s.h("fw1s"), bm["fw1s"] + gidx * 16)
                        s.inc(g.dma_start(out=F["W1ST"][:, p % 2, :, :],
                                          in_=bass.AP(io[f"f_w1{l}"], p * 256, [[5632, 128], [5632 * 128, 8], [1, 256]])),
                              "fw1s", 16)
                    for pas in range(2):
                        for kc in range(22):
                            if nsl >= 2:
                                g.wait_ge(s.h("st_d"), bm["st_d"] + nsl - 1)
                            g.wait_ge(s.h("fw2s"), bm["fw2s"] + nsl * 16)
                            s.inc(g.dma_start(out=F["WSTF"][:, nsl % 2, 0:4, :],
                                              in_=bass.AP(io[f"f_w2{l}"], kc * 128 * 1024 + pas * 512,
                                                          [[1024, 128], [128, 4], [1, 128]])),
                                  "fw2s", 16)
                            nsl += 1

            @blk.tensor
            def _(t):
                t.wait_ge(s.h("awc"), bm["awc"] + 1)
                for qc in range(2):
                    if qc == 1:
                        t.wait_ge(s.h("apm"), bm["apm"] + NCH)
                    for c in range(NCH):
                        i = t.matmul(PS[6][32:33, :], self.ONECB[:], XSH(c, qc * QC, QC),
                                     start=(c == 0), stop=(c == NCH - 1))
                    s.inc(i, "ssum", 1)
                    for rnd in range(2):
                        t.wait_ge(s.h("sq2"), bm["sq2"] + qc * 2 + rnd + 1)
                        for c4 in range(4):
                            i = t.matmul(PS[7][32:33, :], self.ONECB[:],
                                         F["PT"][:, c4 // 2, c4 % 2, :],
                                         start=(rnd == 0 and c4 == 0), stop=(rnd == 1 and c4 == 3))
                        s.inc(i, "ssum", 1)
                    t.wait_ge(s.h("arrow"), bm["arrow"] + 3 * qc + 3)
                    for c in range(NCH):
                        lgc = R32[32:33, 0 + c // 4, (c % 4) * 128:(c % 4) * 128 + 128]
                        lbc_ = R32[32:33, 2 + c // 4, (c % 4) * 128:(c % 4) * 128 + 128]
                        if qc * NCH + c >= 1:
                            t.wait_ge(s.h("apm"), bm["apm"] + qc * NCH + c)
                        t.matmul(PS[6][0:128, :], lgc, R32[32:33, 4, :], start=True, stop=True)
                        t.matmul(PS[7][0:128, :], lgc, R32[32:33, 5, :], start=True, stop=False)
                        s.inc(t.matmul(PS[7][0:128, :], lbc_, R32[32:33, 6, 0:QC],
                                       start=False, stop=True), "bcap", 1)
                for qc2 in range(2):
                    t.wait_ge(s.h("apm"), bm["apm"] + (qc2 + 1) * NCH)
                    for p in range(22):
                        gidx = qc2 * 22 + p
                        t.wait_ge(s.h("fw1s"), bm["fw1s"] + (gidx + 1) * 16)
                        if gidx >= 1:
                            t.wait_ge(s.h("lrow"), bm["lrow"] + gidx - 1)
                        for kc in range(NCH):
                            i = t.matmul(PS[0 + p % 2][:, :], F["W1ST"][:, p % 2, kc, 0:128],
                                         XSH(kc, qc2 * 512, 512), start=(kc == 0), stop=(kc == NCH - 1))
                        s.inc(i, "mm_qk", 1)
                        for kc in range(NCH):
                            i = t.matmul(PS[2 + p % 2][:, :], F["W1ST"][:, p % 2, kc, 128:256],
                                         XSH(kc, qc2 * 512, 512), start=(kc == 0), stop=(kc == NCH - 1))
                        s.inc(i, "mm_qk", 1)
                    if qc2 == 0:
                        t.wait_ge(s.h("apm"), bm["apm"] + 16)
                    for pas in range(2):
                        for kc in range(22):
                            nsl = qc2 * 44 + pas * 22 + kc
                            t.wait_ge(s.h("fw2s"), bm["fw2s"] + (nsl + 1) * 16)
                            t.wait_ge(s.h("lrow"), bm["lrow"] + qc2 * 22 + kc + 1)
                            if kc == 0:
                                if pas == 0 and qc2 >= 1:
                                    t.wait_ge(s.h("xadd"), bm["xadd"] + qc2 * 8)
                                if pas == 1:
                                    t.wait_ge(s.h("xadd"), bm["xadd"] + qc2 * 8 + 4)
                            for dc in range(4):
                                i = t.matmul(PS[4 + dc][:, :],
                                             F["WSTF"][:, nsl % 2, dc, :],
                                             F["H2"][:, kc, :], start=(kc == 0), stop=(kc == 21))
                            s.inc(i, "st_d", 1)

            @blk.scalar
            def _(sc):
                sc.wait_ge(s.h("awc"), bm["awc"] + 1)
                for qc in range(2):
                    for rnd in range(2):
                        ridx = qc * 2 + rnd
                        if ridx >= 1:
                            prev = ridx - 1
                            sc.wait_ge(s.h("ssum"), bm["ssum"] + 3 * (prev // 2) + 2 + (prev % 2))
                        i = sc.activation(F["PT"][:, :, :, :],
                                          bass.AP(self.BIGA, b1024 + rnd * 4 * T + qc * QC,
                                                  [[16384, 128], [T, 4], [1, QC]]),
                                          AF.Square)
                        s.inc(i, "sq2", 1)
                    sc.wait_ge(s.h("arrow"), bm["arrow"] + 3 * qc + 1)
                    sc.activation(R[32:33, 3, :], R[32:33, 3, :], AF.Ln, bias=self.EPS[32:33, :])
                    sc.drain()
                    s.inc(sc.activation(R[32:33, 4, :], R[32:33, 3, :], AF.Exp,
                                        scale=self.NHALF[32:33, :]), "arrow", 1)
                for qc2 in range(2):
                    for p in range(22):
                        gidx = qc2 * 22 + p
                        sc.wait_ge(s.h("mm_qk"), bm["mm_qk"] + (gidx + 1) * 2)
                        if gidx >= 2:
                            sc.wait_ge(s.h("lrow"), bm["lrow"] + gidx - 1)
                        s.inc(sc.activation(F["QSQ"][:, gidx % 2, :], PS[2 + p % 2][:, :], AF.Gelu,
                                            bias=F["B1T"][:, 1, p:p + 1]), "mm_v", 1)

            @blk.vector
            def _(v):
                for qc in range(2):
                    v.wait_ge(s.h("ssum"), bm["ssum"] + 3 * qc + 3)
                    v.tensor_scalar_mul(R[32:33, 2, :], PS[6][32:33, :], 1.0 / D)
                    v.tensor_scalar_mul(R[32:33, 3, :], PS[7][32:33, :], 1.0 / D)
                    v.drain()
                    v.tensor_tensor(self.TMP[32:33, 0, 0:QC], R[32:33, 2, :], R[32:33, 2, :], OP.mult)
                    v.drain()
                    s.inc(v.tensor_sub(R[32:33, 3, :], R[32:33, 3, :], self.TMP[32:33, 0, 0:QC]),
                          "arrow", 1)
                    v.wait_ge(s.h("arrow"), bm["arrow"] + 3 * qc + 2)
                    v.tensor_mul(R[32:33, 5, :], R[32:33, 2, :], R[32:33, 4, :])
                    v.drain()
                    v.tensor_scalar_mul(R[32:33, 5, :], R[32:33, 5, :], -1.0)
                    v.drain()
                    v.tensor_copy(R32[32:33, 4, :], R[32:33, 4, :])
                    s.inc(v.tensor_copy(R32[32:33, 5, :], R[32:33, 5, :]), "arrow", 1)
                    for c in range(NCH):
                        v.wait_ge(s.h("bcap"), bm["bcap"] + qc * NCH + c + 1)
                        v.tensor_tensor(F["TMP2"][:, :], XSH(c, qc * QC, QC), PS[6][:, :], OP.mult)
                        v.drain()
                        i = v.tensor_add(XSH(c, qc * QC, QC), F["TMP2"][:, :], PS[7][:, :])
                        s.inc(i, "apm", 1)
                        v.drain()
                for qc2 in range(2):
                    for p in range(22):
                        gidx = qc2 * 22 + p
                        v.wait_ge(s.h("mm_v"), bm["mm_v"] + gidx + 1)
                        if qc2 >= 1:
                            v.wait_ge(s.h("st_d"), bm["st_d"] + (qc2 - 1) * 44 + 22 + p + 1)
                        i = v.scalar_tensor_tensor(F["H2"][:, p, :], PS[0 + p % 2][:, :],
                                                   F["B1T"][:, 0, p:p + 1], F["QSQ"][:, gidx % 2, :],
                                                   OP.add, OP.mult)
                        s.inc(i, "lrow", 1)
                    for pas in range(2):
                        for dc in range(4):
                            dch = pas * 4 + dc
                            v.wait_ge(s.h("st_d"), bm["st_d"] + qc2 * 44 + pas * 22 + 22)
                            i = v.scalar_tensor_tensor(XF[:, dch, b1024 + qc2 * 512: b1024 + (qc2 + 1) * 512],
                                                       PS[4 + dc][:, :],
                                                       F["B2F"][:, dch:dch + 1],
                                                       XF[:, dch, b1024 + qc2 * 512: b1024 + (qc2 + 1) * 512],
                                                       OP.add, OP.add)
                            s.inc(i, "xadd", 1)
                            v.drain()
        ctxH.close()
        if "o_xa" in io and cc == 0 and l == 0 and bki == 0:
            with nc.Block() as blkd:
                @blkd.gpsimd
                def _(g):
                    bst = s.val("st")
                    for c in range(NCH):
                        s.inc(g.dma_start(out=io["o_xm"].ap()[c * 128:(c + 1) * 128, :],
                                          in_=XF[:, c, 0:1024]), "st", 16)
                        s.inc(g.dma_start(out=io["o_xn2"].ap()[c * 128:(c + 1) * 128, :],
                                          in_=XSH(c, 0, 1024)), "st", 16)
                    g.wait_ge(s.h("st"), bst + 16 * 16)

    def dump_xn(self, o_xn, o_xf, ap_xn, ap_xf):
        nc, s = self.nc, self.s
        with nc.Block() as blk:
            @blk.gpsimd
            def _(g):
                s.wait(g, ap_xn)
                s.wait(g, ap_xf)
                for c in range(NCH):
                    s.inc(g.dma_start(out=o_xn[c * 128:(c + 1) * 128, :],
                                      in_=self.XN(c, 0, T)), "st", 16)
                    s.inc(g.dma_start(out=o_xf[c * 128:(c + 1) * 128, :],
                                      in_=self.X[:, c, :]), "st", 16)
                s.wait(g, "st")


def build(dbg="x1", races=True):
    nc = bass.Bass("TRN2", target_bir_lowering=False, detect_race_conditions=races)
    x_fm = nc.dram_tensor("x_fm", [D, T], F32, kind="ExternalInput").ap()
    io = {}
    io["w_a"] = nc.dram_tensor("w_a", [D, 1536], F32, kind="ExternalInput")
    io["b_a"] = nc.dram_tensor("b_a", [128, 10], F32, kind="ExternalInput")
    io["bv_row"] = nc.dram_tensor("bv_row", [1, 256], F32, kind="ExternalInput")
    io["amask"] = nc.dram_tensor("amask", [128, 4 * QC], F32, kind="ExternalInput")
    io["hflag"] = nc.dram_tensor("hflag", [128, 2], F32, kind="ExternalInput")
    io["lnab_a"] = nc.dram_tensor("lnab_a", [1, 1024], F32, kind="ExternalInput")
    io["wo_a"] = nc.dram_tensor("wo_a", [64, 8192], F32, kind="ExternalInput")
    io["bo_fm"] = nc.dram_tensor("bo_fm", [128, NCH], F32, kind="ExternalInput")
    io["agin_f"] = nc.dram_tensor("agin_f", [1280, T], BF16)
    io["agout_f"] = nc.dram_tensor("agout_f", [2560, T], BF16)
    io["agin_v"] = nc.dram_tensor("agin_v", [NT, 128], BF16)
    io["agout_v"] = nc.dram_tensor("agout_v", [2 * NT, 128], BF16)
    io["ar_in"] = [nc.dram_tensor(f"ar_in{g}", [2, 1024], F32) for g in range(4)]
    io["ar_out"] = [nc.dram_tensor(f"ar_out{g}", [2, 1024], F32) for g in range(4)]
    io["rs_in"] = nc.dram_tensor("rs_in", [2048, T], F32)
    io["rs_out"] = nc.dram_tensor("rs_out", [1024, T], F32)
    for l in range(2):
        io[f"f_wqkv{l}"] = nc.dram_tensor(f"f_wqkv{l}", [D, 1536], F32, kind="ExternalInput")
        io[f"f_qsk{l}"] = nc.dram_tensor(f"f_qsk{l}", [128, 2], F32, kind="ExternalInput")
        io[f"f_b1{l}"] = nc.dram_tensor(f"f_b1{l}", [128, 44], F32, kind="ExternalInput")
        io[f"f_b2{l}"] = nc.dram_tensor(f"f_b2{l}", [128, NCH], F32, kind="ExternalInput")
        io[f"f_lnab{l}"] = nc.dram_tensor(f"f_lnab{l}", [1, 2048], F32, kind="ExternalInput")
        io[f"f_wo{l}"] = nc.dram_tensor(f"f_wo{l}", [64, 8192], F32, kind="ExternalInput")
        io[f"f_w1{l}"] = nc.dram_tensor(f"f_w1{l}", [D, 5632], F32, kind="ExternalInput")
        io[f"f_w2{l}"] = nc.dram_tensor(f"f_w2{l}", [2816, D], F32, kind="ExternalInput")
    io["f_tab"] = nc.dram_tensor("f_tab", [128, 8192], F32, kind="ExternalInput")
    io["sel2c"] = nc.dram_tensor("sel2c", [128, 2], F32, kind="ExternalInput")
    io["sel2r"] = nc.dram_tensor("sel2r", [2, 128], F32, kind="ExternalInput")
    io["amf"] = nc.dram_tensor("amf", [128, 256], F32, kind="ExternalInput")
    io["rotp"] = nc.dram_tensor("rotp", [128, 128], F32, kind="ExternalInput")
    io["xspill"] = nc.dram_tensor("xspill", [D, T], F32)
    io["agx_in"] = nc.dram_tensor("agx_in", [2, 1024], F32)
    io["agx_out"] = nc.dram_tensor("agx_out", [4, 1024], F32)
    io["agka_in"] = nc.dram_tensor("agka_in", [128, 4128], BF16)
    io["agka_out"] = nc.dram_tensor("agka_out", [256, 4128], BF16)
    io["agkb_in"] = nc.dram_tensor("agkb_in", [256, 4128], BF16)
    io["agkb_out"] = nc.dram_tensor("agkb_out", [512, 4128], BF16)
    io["agm1_in"] = nc.dram_tensor("agm1_in", [1, 1024], F32)
    io["agm1_out"] = nc.dram_tensor("agm1_out", [2, 1024], F32)
    io["agm2_in"] = nc.dram_tensor("agm2_in", [2, 1024], F32)
    io["agm2_out"] = nc.dram_tensor("agm2_out", [4, 1024], F32)
    io["w_b"] = nc.dram_tensor("w_b", [D, 3072], F32, kind="ExternalInput")
    io["b_bq"] = nc.dram_tensor("b_bq", [128, 16], F32, kind="ExternalInput")
    io["bvr_b"] = nc.dram_tensor("bvr_b", [1, 1024], F32, kind="ExternalInput")
    io["b_tab"] = nc.dram_tensor("b_tab", [128, 2 * T], F32, kind="ExternalInput")
    io["wo_b"] = nc.dram_tensor("wo_b", [128, 4096], F32, kind="ExternalInput")
    io["bo_b"] = nc.dram_tensor("bo_b", [128, NCH], F32, kind="ExternalInput")
    io["eye64"] = nc.dram_tensor("eye64", [64, 64], F32, kind="ExternalInput")
    io["agin_fb"] = nc.dram_tensor("agin_fb", [2048, T], BF16)
    io["agout_fb"] = nc.dram_tensor("agout_fb", [4096, T], BF16)
    io["agin_vb"] = nc.dram_tensor("agin_vb", [16384, 128], BF16)
    io["agout_vb"] = nc.dram_tensor("agout_vb", [32768, 128], BF16)
    o_y = nc.dram_tensor("o_y", [D, T], F32, kind="ExternalOutput").ap()
    if dbg == "kqv":
        io["o_ks"] = nc.dram_tensor("o_ks", [128, NT], F32, kind="ExternalOutput").ap()
        io["o_qs"] = nc.dram_tensor("o_qs", [512, NT], F32, kind="ExternalOutput").ap()
        io["o_vs"] = nc.dram_tensor("o_vs", [128, 32 * 130], F32, kind="ExternalOutput").ap()
        io["o_xn"] = nc.dram_tensor("o_xn", [D, T], F32, kind="ExternalOutput").ap()
        io["o_xf"] = nc.dram_tensor("o_xf", [D, T], F32, kind="ExternalOutput").ap()
        io["o_lbc"] = nc.dram_tensor("o_lbc", [64, 2 * QC], F32, kind="ExternalOutput")
    if dbg == "fkv":
        io["o_qf"] = nc.dram_tensor("o_qf", [128, 4096], F32, kind="ExternalOutput")
        io["o_xa"] = nc.dram_tensor("o_xa", [D, 1024], F32, kind="ExternalOutput")
        io["o_xm"] = nc.dram_tensor("o_xm", [D, 1024], F32, kind="ExternalOutput")
        io["o_xn2"] = nc.dram_tensor("o_xn2", [D, 1024], F32, kind="ExternalOutput")
        io["o_h2"] = nc.dram_tensor("o_h2", [128, 22 * 256], F32, kind="ExternalOutput")
        io["o_kf"] = nc.dram_tensor("o_kf", [128, 6144], F32, kind="ExternalOutput")
        io["o_vf"] = nc.dram_tensor("o_vf", [128, 6240], F32, kind="ExternalOutput")
        io["o_r32"] = nc.dram_tensor("o_r32", [1, 7 * QC], F32, kind="ExternalOutput")
        io["o_ot"] = nc.dram_tensor("o_ot", [64, 16 * QC], F32, kind="ExternalOutput")
    with ExitStack() as ctx:
        p = Prog(nc, ctx, dbg)
        p.p0_load(x_fm)
        a1 = p.topln("ln0", "xn")
        ln_base = p.s.val(a1) - NQC
        if dbg == "kqv":
            p.dump_xn(io["o_xn"], io["o_xf"], a1, "xld")
        done = p.attn_a(io, a1, ln_base)
        if dbg in ("x2", "full", "fkv", "x3"):
            done = p.ffn_call(io, 0)
        if dbg in ("x3", "full"):
            a3 = p.topln("ln2", "xn")
            done = p.attn_b(io, a3, p.s.val(a3) - NQC)
        if dbg == "full":
            done = p.ffn_call(io, 1)
        if dbg == "x3o":
            done = p.attn_b(io, a1, ln_base)
        if dbg in ("x3p1", "x3p2"):
            done = p.attn_b(io, a1, ln_base, upto=int(dbg[-1]))
        if dbg == "x3pw":
            done = p.attn_b(io, a1, ln_base, upto=25)
        s = p.s
        with nc.Block() as blk:
            @blk.sync
            def _(sy):
                s.wait(sy, done)
                for c in range(NCH):
                    s.inc(sy.dma_start(out=o_y[c * 128:(c + 1) * 128, :],
                                       in_=p.X[:, c, :]), "st_out", 16)
                s.wait(sy, "st_out")
    return nc


def host_shards(x):
    out = []
    for c in range(8):
        b, h = c // 2, c % 2
        blocks = [0, 3] if h == 0 else [1, 2]
        toks = np.concatenate([x[b, blk * 1024:(blk + 1) * 1024] for blk in blocks], axis=0)
        out.append(np.ascontiguousarray(toks.T.astype(np.float32)))
    return out


def host_attn_a(I):
    Wq, bq = I["L0_attn_a_Wq"], I["L0_attn_a_bq"]
    Wk, bk = I["L0_attn_a_Wk"], I["L0_attn_a_bk"]
    Wv, bv = I["L0_attn_a_Wv"], I["L0_attn_a_bv"]
    Wo, bo = I["L0_attn_a_Wo"], I["L0_attn_a_bo"]
    lng, lnb = I["L0_attn_a_ln_g"], I["L0_attn_a_ln_b"]
    qcols = []
    for r in range(2):
        for g in range(4):
            for lh in range(2):
                hk = 2 * r + lh
                qcols.append((g * 256 + hk * 64, g * 256 + hk * 64 + 64))
    w_q = np.concatenate([Wq[:, a:bb] for a, bb in qcols], axis=1)
    b_q = np.concatenate([bq[a:bb] for a, bb in qcols])
    w_a = np.ascontiguousarray(np.concatenate([w_q, Wk, Wv], axis=1).astype(np.float32))
    b_chunks = np.concatenate([b_q, bk])
    b_a = np.zeros((128, 10), np.float32)
    for ch in range(10):
        b_a[:, ch] = b_chunks[ch * 128:(ch + 1) * 128]
    bv_row = bv.reshape(1, 256).astype(np.float32)
    amask = np.zeros((128, 4, QC), np.float32)
    for k in range(4):
        for i in range(4):
            if i > k:
                amask[:, k, i * 128:(i + 1) * 128] = 1.0
            elif i == k:
                sl = np.arange(128)
                amask[:, k, i * 128:(i + 1) * 128] = (sl[None, :] >= sl[:, None]).astype(np.float32)
    amask = amask.reshape(128, 4 * QC)
    bo_fm = np.zeros((128, NCH), np.float32)
    for ch in range(NCH):
        bo_fm[:, ch] = bo[ch * 128:(ch + 1) * 128]
    per_core = []
    for c in range(8):
        h = c % 2
        hflag = np.zeros((128, 2), np.float32)
        hflag[:, 0] = 1.0 if h == 0 else 0.0
        hflag[:, 1] = 0.0 if h == 0 else 1.0
        lg_s, lb_s, wo_s = [], [], []
        for g in range(4):
            for lh in range(2):
                hk = 2 * h + lh
                a0 = g * 256 + hk * 64
                lg_s.append(lng[a0:a0 + 64])
                lb_s.append(lnb[a0:a0 + 64])
                wo_s.append(Wo[a0:a0 + 64, :])
        lnab = np.concatenate([np.concatenate(lg_s), np.concatenate(lb_s)]).reshape(1, 1024)
        wo_my = np.stack(wo_s, axis=1)
        wo_my = wo_my.reshape(64, 8, 8, 128).reshape(64, 8192)
        per_core.append(dict(
            w_a=w_a, b_a=b_a, bv_row=bv_row, amask=amask,
            hflag=hflag,
            lnab_a=np.ascontiguousarray(lnab.astype(np.float32)),
            wo_a=np.ascontiguousarray(wo_my.astype(np.float32)),
            bo_fm=bo_fm))
    return per_core


def host_ffn(I):
    DH, H, W, NP = 64, 8, 512, 22
    shared = {}
    for l in range(2):
        wqkv = I["L0_ffn_wqkv"][l].astype(np.float32)
        shared[f"f_wqkv{l}"] = np.ascontiguousarray(wqkv)
        qs, ks = I["L0_ffn_q_scale"][l], I["L0_ffn_k_scale"][l]
        qsk = np.zeros((128, 2), np.float32)
        qsk[:, 0] = np.tile(qs, 2)
        qsk[:, 1] = np.tile(ks, 2)
        shared[f"f_qsk{l}"] = qsk
        w1 = I["L0_ffn_w1"][l].astype(np.float32)
        b1 = I["L0_ffn_b1"][l].astype(np.float32)
        w2 = I["L0_ffn_w2"][l].astype(np.float32)
        b2 = I["L0_ffn_b2"][l].astype(np.float32)
        HID = 2730
        w1p = np.zeros((1024, NP * 256), np.float32)
        b1t = np.zeros((128, 2, NP), np.float32)
        for p_ in range(NP):
            a0, a1_ = p_ * 128, min((p_ + 1) * 128, HID)
            nw = a1_ - a0
            w1p[:, p_ * 256: p_ * 256 + nw] = w1[:, a0:a1_]
            w1p[:, p_ * 256 + 128: p_ * 256 + 128 + nw] = w1[:, HID + a0: HID + a1_]
            b1t[0:nw, 0, p_] = b1[a0:a1_]
            b1t[0:nw, 1, p_] = b1[HID + a0: HID + a1_]
        shared[f"f_w1{l}"] = np.ascontiguousarray(w1p)
        shared[f"f_b1{l}"] = np.ascontiguousarray(b1t.reshape(128, 44))
        w2p = np.zeros((2816, 1024), np.float32)
        w2p[0:HID] = w2
        shared[f"f_w2{l}"] = w2p
        b2f = np.zeros((128, NCH), np.float32)
        for ch in range(NCH):
            b2f[:, ch] = b2[ch * 128:(ch + 1) * 128]
        shared[f"f_b2{l}"] = b2f
        lg, lb = I["L0_ffn_ln_g"][l], I["L0_ffn_ln_b"][l]
        shared[f"f_lnab{l}"] = np.concatenate([lg, lb]).reshape(1, 2048).astype(np.float32)
        wo = I["L0_ffn_wo"][l].astype(np.float32)
        wof = np.zeros((64, 8, 8, 128), np.float32)
        for h in range(H):
            for dch in range(NCH):
                wof[:, h, dch, :] = wo[h * 64:(h + 1) * 64, dch * 128:(dch + 1) * 128]
        shared[f"f_wo{l}"] = np.ascontiguousarray(wof.reshape(64, 8192))
    sel2c = np.zeros((128, 2), np.float32)
    sel2c[0:64, 0] = 1.0
    sel2c[64:128, 1] = 1.0
    shared["sel2c"] = sel2c
    sel2r = np.zeros((2, 128), np.float32)
    sel2r[0, 0:64] = 1.0
    sel2r[1, 64:128] = 1.0
    shared["sel2r"] = sel2r
    sl = np.arange(128)
    amf = np.zeros((128, 256), np.float32)
    amf[:, 0:128] = (sl[None, :] >= sl[:, None]).astype(np.float32)
    amf[:, 128:256] = (sl[:, None] >= sl[None, :]).astype(np.float32)
    shared["amf"] = amf
    rotp = np.zeros((128, 128), np.float32)
    for hb in range(2):
        o = hb * 64
        for m in range(32):
            rotp[o + m + 32, o + m] = -1.0
            rotp[o + m, o + m + 32] = 1.0
    shared["rotp"] = rotp
    half = np.arange(0, DH, 2, dtype=np.float32)
    inv_freq = 1.0 / 10000.0 ** (half / DH)
    base = (half + 0.4 * DH) / (1.4 * DH)
    per_core = []
    for c in range(8):
        h = c % 2
        blocks = [0, 3] if h == 0 else [1, 2]
        tpos = np.concatenate([np.arange(blk * 1024, (blk + 1) * 1024) for blk in blocks]).astype(np.float32)
        freqs = np.concatenate([tpos[:, None] * inv_freq, tpos[:, None] * inv_freq], axis=1)
        power = (tpos - NT // 2) / (W // 2)
        scale = np.concatenate([base[None, :] ** power[:, None]] * 2, axis=1)
        cosf, sinf = np.cos(freqs), np.sin(freqs)
        tab = np.zeros((128, 4, 2048), np.float32)
        for dd in range(64):
            tab[dd, 0, :] = 8.0 * cosf[:, dd] * scale[:, dd]
            tab[dd, 1, :] = 8.0 * sinf[:, dd] * scale[:, dd]
            tab[dd, 2, :] = cosf[:, dd] / scale[:, dd]
            tab[dd, 3, :] = sinf[:, dd] / scale[:, dd]
        tab[64:128] = tab[0:64]
        per_core.append(dict(f_tab=np.ascontiguousarray(tab.reshape(128, 8192)), **shared))
    return per_core


def host_attn_b(I):
    qkv_w, qkv_b = I["L0_attn_b_qkv_w"], I["L0_attn_b_qkv_b"]
    out_w, out_b = I["L0_attn_b_out_w"], I["L0_attn_b_out_b"]
    W = qkv_w.astype(np.float32).T
    bq = qkv_b.astype(np.float32)
    perm64 = np.concatenate([np.arange(0, 64, 2), np.arange(1, 64, 2)])
    qcols = []
    for hb in range(2):
        for qk in range(2):
            for slot in range(4):
                for hd2 in range(2):
                    head = 8 * hb + 2 * slot + hd2
                    base = qk * 1024 + head * 64
                    qcols.append(base + perm64)
    colidx = np.concatenate(qcols)
    w_b = np.concatenate([W[:, colidx], W[:, 2048:3072]], axis=1)
    b_qk = bq[colidx]
    b_bq = np.zeros((128, 16), np.float32)
    for oc in range(16):
        b_bq[:, oc] = b_qk[oc * 128:(oc + 1) * 128]
    bvr_b = np.ascontiguousarray(bq[2048:3072].reshape(1, 1024))
    hd = 64
    inv_freq = 1.0 / 10000.0 ** (np.arange(0, hd, 2, dtype=np.float32) / hd)
    Weff = out_w.astype(np.float32).T
    bo_b = np.zeros((128, NCH), np.float32)
    for ch in range(NCH):
        bo_b[:, ch] = out_b[ch * 128:(ch + 1) * 128]
    per_core = []
    for c in range(8):
        h = c % 2
        blocks = [0, 3] if h == 0 else [1, 2]
        tpos = np.concatenate([np.arange(blk * 1024, (blk + 1) * 1024) for blk in blocks]).astype(np.float32)
        freqs = tpos[:, None] * inv_freq[None, :]
        emb = np.concatenate([np.sin(freqs), np.cos(freqs)], -1)
        Sp = emb[:, ::2].T
        Cp = emb[:, 1::2].T
        tab = np.zeros((128, 2, 2048), np.float32)
        for b4 in range(4):
            tab[b4 * 32:(b4 + 1) * 32, 0, :] = Cp
            tab[b4 * 32:(b4 + 1) * 32, 1, :] = Sp
        wo2 = np.zeros((128, 4, NCH, 128), np.float32)
        for hp in range(4):
            for lh in range(2):
                head = 8 * h + 2 * hp + lh
                strip = Weff[head * 64:(head + 1) * 64, :]
                for dch in range(NCH):
                    wo2[lh * 64:(lh + 1) * 64, hp, dch, :] = strip[:, dch * 128:(dch + 1) * 128]
        per_core.append(dict(w_b=np.ascontiguousarray(w_b), b_bq=b_bq, bvr_b=bvr_b,
                             b_tab=np.ascontiguousarray(tab.reshape(128, 4096)),
                             wo_b=np.ascontiguousarray(wo2.reshape(128, 4096)),
                             bo_b=bo_b, eye64=np.eye(64, dtype=np.float32)))
    return per_core



def make_inmaps(I, dbg):
    shards = host_shards(I["L0_attn_a_x"])
    aa = host_attn_a(I)
    ff = host_ffn(I)
    bb = host_attn_b(I)
    return [dict(x_fm=shards[c], **aa[c], **ff[c], **bb[c]) for c in range(8)]



def kernel(**inputs):
    nc = build("full", races=False)
    in_maps = make_inmaps(inputs, "full")
    res = run_bass_kernel_spmd(nc, in_maps, core_ids=list(range(8)))
    out = np.zeros((4, NT, D), np.float32)
    for c in range(8):
        b, h = c // 2, c % 2
        blocks = [0, 3] if h == 0 else [1, 2]
        oy = res.results[c]["o_y"]
        for i, blk in enumerate(blocks):
            out[b, blk * 1024:(blk + 1) * 1024, :] = oy[:, i * 1024:(i + 1) * 1024].T
    return out


# version 05
# speedup vs baseline: 1.0028x; 1.0028x over previous
import numpy as np
import ml_dtypes
import concourse.bass as bass
import concourse.mybir as mybir
from concourse.bass_utils import run_bass_kernel_spmd
from contextlib import ExitStack

F32 = mybir.dt.float32
BF16 = mybir.dt.bfloat16
I32 = mybir.dt.int32
AF = mybir.ActivationFunctionType
OP = mybir.AluOpType
PAIRS = [[0, 1], [2, 3], [4, 5], [6, 7]]
BF = ml_dtypes.bfloat16

D = 1024
T = 2048
NT = 4096
QC = 512
NQC = T // QC
NCH = 8


class Sems:
    def __init__(self, ctx, nc):
        object.__setattr__(self, "ctx", ctx)
        object.__setattr__(self, "nc", nc)
        object.__setattr__(self, "d", {})

    def h(self, name):
        if name not in self.d:
            self.d[name] = [self.ctx.enter_context(self.nc.semaphore(name)), 0]
        return self.d[name][0]

    def inc(self, instr, name, k=1):
        h = self.h(name)
        instr.then_inc(h, k)
        self.d[name][1] += k
        return self.d[name][1]

    def val(self, name):
        self.h(name)
        return self.d[name][1]

    def wait(self, eng, name, v=None):
        eng.wait_ge(self.h(name), self.val(name) if v is None else v)


class Prog:

    def __init__(self, nc, ctx, dbg):
        self.nc, self.ctx, self.dbg = nc, ctx, dbg
        self.s = Sems(ctx, nc)
        E = ctx.enter_context
        self.X = E(nc.sbuf_tensor("X", [128, NCH, T], F32))
        self.BIGA = E(nc.sbuf_tensor("BIGA", [128, 16384], BF16))
        self.ROWS = E(nc.sbuf_tensor("ROWS", [128, 6, QC], F32))
        self.TMP = E(nc.sbuf_tensor("TMP", [128, 1, QC], F32))
        self.R32 = E(nc.sbuf_tensor("R32", [128, 7, QC], mybir.dt.float32r))
        self.ONE1B = E(nc.sbuf_tensor("ONE1B", [1, 128], BF16))
        self.ONE1F = E(nc.sbuf_tensor("ONE1F", [1, 128], F32))
        self.ONECB = E(nc.sbuf_tensor("ONECB", [128, 1], BF16))
        self.EPS = E(nc.sbuf_tensor("EPS", [128, 1], F32))
        self.NHALF = E(nc.sbuf_tensor("NHALF", [128, 1], F32))
        self.C0125 = E(nc.sbuf_tensor("C0125", [128, 1], F32))
        self.GR = E(nc.gpsimd.register("GRA"))
        self.GR2 = E(nc.gpsimd.register("GRB"))
        self.PSALL = E(nc.psum_tensor("PSALL", [128, 8, QC], F32))
        self.PS = [self.PSALL[:, i, :] for i in range(8)]
        self.G = [bass.AP(self.PSALL, i * 2 * QC, [[8 * QC, 128], [1, 2 * QC]])
                  for i in range(2)]

    def XN(self, c, t0, n):
        return self.BIGA[:, c * T + t0: c * T + t0 + n]

    def p0_load(self, x_fm):
        nc, s = self.nc, self.s
        with nc.Block() as blk:
            @blk.gpsimd
            def _(g):
                g.memset(self.ONE1B[:], 1.0)
                g.memset(self.ONE1F[:], 1.0)
                g.memset(self.ONECB[:], 1.0)
                g.memset(self.EPS[:], 1e-5)
                g.memset(self.C0125[:], 0.125)
                s.inc(g.memset(self.NHALF[:], -0.5), "consts", 1)

            @blk.vector
            def _(v):
                s.wait(v, "consts")
                v.memset(self.ROWS[32:33, 0, :], 1.0)
                v.drain()
                s.inc(v.tensor_copy(self.R32[32:33, 2, :], self.ROWS[32:33, 0, :]), "consts", 1)

            @blk.sync
            def _(sy):
                for c in range(NCH):
                    s.inc(sy.dma_start(out=self.X[:, c, :],
                                       in_=x_fm[c * 128:(c + 1) * 128, :]), "xld", 16)

    def topln(self, tag, dst, after=()):
        nc, s = self.nc, self.s
        after = [(name, s.val(name) if v is None else v) for name, v in after]

        def gate(eng):
            for name, v in after:
                eng.wait_ge(s.h(name), v)
        exact = dst == "xf"
        cast_d = "tl_cast"
        sq_d = "tl_sq"
        sum_d = "tl_sum"
        row_d = "tl_row"
        bc_d = "tl_bc"
        ap_d = "tl_ap"
        b = {k: s.val(k) for k in (cast_d, sq_d, sum_d, row_d, bc_d, ap_d)}
        lctx = ExitStack()
        TC = lctx.enter_context(nc.sbuf_tensor(f"TC{tag}", [128, NCH, QC], BF16))
        SQB = lctx.enter_context(nc.sbuf_tensor(f"SQB{tag}", [128, NCH, QC], BF16))
        RBx = lctx.enter_context(nc.sbuf_tensor(f"RB{tag}", [1, 4, QC], BF16))
        self.TC, self.RB = TC, RBx
        R = self.ROWS

        with nc.Block() as blk:
            @blk.gpsimd
            def _(g):
                gate(g)
                s.wait(g, "xld")
                s.wait(g, "consts")
                for qc in range(NQC):
                    if qc >= 1:
                        g.wait_ge(s.h(sum_d), b[sum_d] + qc)
                    for c in range(NCH):
                        i = g.tensor_copy(self.TC[:, c, :],
                                          self.X[:, c, qc * QC:(qc + 1) * QC])
                    s.inc(i, cast_d, 1)

            @blk.scalar
            def _(sc):
                gate(sc)
                for qc in range(NQC):
                    sc.wait_ge(s.h(cast_d), b[cast_d] + qc + 1)
                    s.inc(sc.activation(SQB[:, :, :], self.TC[:, :, :], AF.Square), sq_d, 1)
                    sc.wait_ge(s.h(row_d), b[row_d] + 3 * qc + 1)
                    sc.activation(R[0:1, 3, :], R[0:1, 3, :], AF.Ln, bias=self.EPS[0:1, :])
                    sc.drain()
                    s.inc(sc.activation(R[0:1, 4, :], R[0:1, 3, :], AF.Exp,
                                        scale=self.NHALF[0:1, :]), row_d, 1)

            @blk.tensor
            def _(t):
                gate(t)
                for qc in range(NQC):
                    t.wait_ge(s.h(cast_d), b[cast_d] + qc + 1)
                    if qc >= 1:
                        t.wait_ge(s.h(row_d), b[row_d] + 3 * (qc - 1) + 1)
                    for c in range(NCH):
                        t.matmul(self.PS[6][0:1, :], self.ONECB[:], self.TC[:, c, :],
                                 start=(c == 0), stop=(c == NCH - 1))
                    t.wait_ge(s.h(sq_d), b[sq_d] + qc + 1)
                    for c in range(NCH):
                        i = t.matmul(self.PS[7][0:1, :], self.ONECB[:], SQB[:, c, :],
                                     start=(c == 0), stop=(c == NCH - 1))
                    s.inc(i, sum_d, 1)
                    t.wait_ge(s.h(row_d), b[row_d] + 3 * qc + 3)
                    if qc >= 1:
                        t.wait_ge(s.h(ap_d), b[ap_d] + qc)
                    if exact:
                        t.matmul(self.PS[4][:, :], self.ONE1F[:], R[0:1, 4, :], start=True, stop=True)
                        i = t.matmul(self.PS[5][:, :], self.ONE1F[:], R[0:1, 5, :], start=True, stop=True)
                    else:
                        t.matmul(self.PS[4][:, :], self.ONE1B[:], self.RB[0:1, 0, :], start=True, stop=True)
                        i = t.matmul(self.PS[5][:, :], self.ONE1B[:], self.RB[0:1, 1, :], start=True, stop=True)
                    s.inc(i, bc_d, 1)

            @blk.vector
            def _(v):
                gate(v)
                for qc in range(NQC):
                    v.wait_ge(s.h(sum_d), b[sum_d] + qc + 1)
                    v.tensor_scalar_mul(R[0:1, 2, :], self.PS[6][0:1, :], 1.0 / D)
                    v.tensor_scalar_mul(R[0:1, 3, :], self.PS[7][0:1, :], 1.0 / D)
                    v.drain()
                    v.tensor_tensor(self.TMP[0:1, 0, 0:QC], R[0:1, 2, :], R[0:1, 2, :], OP.mult)
                    v.drain()
                    s.inc(v.tensor_sub(R[0:1, 3, :], R[0:1, 3, :], self.TMP[0:1, 0, 0:QC]), row_d, 1)
                    v.wait_ge(s.h(row_d), b[row_d] + 3 * qc + 2)
                    v.tensor_mul(R[0:1, 5, :], R[0:1, 2, :], R[0:1, 4, :])
                    v.drain()
                    v.tensor_scalar_mul(R[0:1, 5, :], R[0:1, 5, :], -1.0)
                    v.drain()
                    if not exact:
                        v.tensor_copy(self.RB[0:1, 0, :], R[0:1, 4, :])
                        i = v.tensor_copy(self.RB[0:1, 1, :], R[0:1, 5, :])
                    else:
                        i = v.engine_nop()
                    s.inc(i, row_d, 1)
                    v.wait_ge(s.h(bc_d), b[bc_d] + qc + 1)
                    for c in range(NCH):
                        v.scalar_tensor_tensor(self.TMP[:, 0, :], self.X[:, c, qc * QC:(qc + 1) * QC],
                                               1.0, self.PS[4][:, :], OP.mult, OP.mult)
                        v.drain()
                        if dst == "xn":
                            out = self.XN(c, qc * QC, QC)
                        else:
                            out = self.X[:, c, qc * QC:(qc + 1) * QC]
                        i = v.tensor_add(out, self.TMP[:, 0, :], self.PS[5][:, :])
                        v.drain()
                    s.inc(i, ap_d, 1)
        lctx.close()
        return ap_d


    def attn_a(self, io, ln_ap, ln_base):
        nc, s = self.nc, self.s
        R, R32 = self.ROWS, self.R32
        NJ = [4 * (qc + 1) for qc in range(8)]
        UPREF = [0]
        for qc in range(8):
            for gp in range(4):
                UPREF.append(UPREF[-1] + NJ[qc])
        NU = UPREF[-1]

        lctx = ExitStack()
        E = lctx.enter_context
        WST = E(nc.sbuf_tensor("WST", [128, 2, NCH, 128], BF16))
        BATT = E(nc.sbuf_tensor("BATT", [128, 10], F32))
        BVR = E(nc.sbuf_tensor("BVR", [1, 256], BF16))
        STG = E(nc.sbuf_tensor("STG", [128, 2, QC], BF16))
        STGV = E(nc.sbuf_tensor("STGV", [128, 2, 2, 128], BF16))
        KS = E(nc.sbuf_tensor("KS", [128, NT], BF16))
        VS = E(nc.sbuf_tensor("VS", [128, 32, 2, 65], BF16))
        PT = E(nc.sbuf_tensor("PT", [128, 2, 2, QC], BF16))
        OT = E(nc.sbuf_tensor("OT", [128, 2, 8, QC], BF16))
        TMPB = E(nc.sbuf_tensor("TMPB", [128, QC], BF16))
        LBC = E(nc.sbuf_tensor("LBC", [128, 2, QC], F32))
        AM = E(nc.sbuf_tensor("AM", [128, 4, QC], BF16))
        WOA = E(nc.sbuf_tensor("WOA", [64, 8, NCH, 128], BF16))
        YPS = E(nc.sbuf_tensor("YPS", [128, 2, QC], F32))
        YB = E(nc.sbuf_tensor("YB", [128, QC], F32))
        BO = E(nc.sbuf_tensor("BO", [128, NCH], F32))
        HSTG = E(nc.sbuf_tensor("HSTG", [128, 2, 2048], BF16))
        FLG = E(nc.sbuf_tensor("FLG", [128, 2], F32))

        PS, G = self.PS, self.G
        QSv = lambda gp, c0, n: self.BIGA[:, gp * NT + c0: gp * NT + c0 + n]

        names = ("aw awc st_qk st_v ev_qk ev_v mm_qk mm_v cc_agf cc_agv rb_k rb_q rb_v "
                 "st_d exprm expm av_d lrow lbc otd b67r sq2 s2m ssum strow st_ar "
                 "cc_ar rb_ar arrow bcap apm wod wo_ev st_y cc_rs rb_y xadd lnabr awl stq0 stq1 stv0 stv1 sty0 sty1 star0 star1").split()
        b = {k: s.val(k) for k in names}

        def b67_use(qc, kind, i=0):
            g0 = (qc // 2) * 42
            odd = qc % 2
            if kind == "lbc":
                return g0 + odd * 5 + i
            if kind == "stats":
                return g0 + odd * 5 + 4
            if kind == "bc":
                return g0 + 10 + odd * 16 + i
            return g0 + 18 + odd * 16 + i

        def agf_row(oc):
            return (oc // 4) * 640 + (oc % 4) * 128 if oc < 8 else (oc - 8) * 640 + 512
        GBLK = {(0, 0): 0, (0, 1): 3, (1, 0): 1, (1, 1): 2}
        RSQ = [(0, 0), (0, 512), (1, 0), (1, 512), (1, 1024), (1, 1536), (0, 1024), (0, 1536)]

        with nc.Block() as blk:
            @blk.gpsimd
            def _(g):
                s.wait(g, "consts")
                s.inc(g.dma_start(out=BATT[:], in_=io["b_a"][:, :]), "aw", 16)
                s.inc(g.dma_start(out=BVR[:], in_=io["bv_row"][:, :]), "aw", 16)
                s.inc(g.dma_start(out=AM[:], in_=io["amask"][:, :]), "aw", 16)
                s.inc(g.dma_start(out=WOA[:], in_=io["wo_a"][0:64, :]), "aw", 16)
                s.inc(g.dma_start(out=R[32:33, 4:6, :], in_=io["lnab_a"][0:1, :]), "aw", 16)
                s.inc(g.dma_start(out=BO[:], in_=io["bo_fm"][:, :]), "aw", 16)
                s.inc(g.dma_start(out=FLG[:], in_=io["hflag"][:, :]), "aw", 16)
                for oc in range(10):
                    if oc >= 2:
                        g.wait_ge(s.h("mm_qk"), b["mm_qk"] + (oc - 1) * 4)
                    g.wait_ge(s.h("awl"), b["awl"] + oc * 16)
                    s.inc(g.dma_start(out=WST[:, oc % 2, :, :],
                                      in_=bass.AP(io["w_a"], oc * 128,
                                                  [[1536, 128], [1536 * 128, 8], [1, 128]])), "awl", 16)
                g.wait_ge(s.h("mm_qk"), b["mm_qk"] + 40)
                for vb in range(2):
                    g.wait_ge(s.h("awl"), b["awl"] + (10 + vb) * 16)
                    s.inc(g.dma_start(out=WST[:, vb, :, :],
                                      in_=bass.AP(io["w_a"], (10 + vb) * 128,
                                                  [[1536, 128], [1536 * 128, 8], [1, 128]])), "awl", 16)
                s.inc(g.memset(VS[:, :, :, 64:65], 1.0), "awc", 1)
                g.wait_ge(s.h("stq0"), b["stq0"] + 20 * 16)
                g.wait_ge(s.h("stq1"), b["stq1"] + 20 * 16)
                g.wait_ge(s.h("stv0"), b["stv0"] + 8 * 16)
                g.wait_ge(s.h("stv1"), b["stv1"] + 8 * 16)
                s.inc(g.collective_compute("AllGather", OP.bypass, replica_groups=PAIRS,
                                           ins=[io["agin_f"].ap().opt()],
                                           outs=[io["agout_f"].ap().opt()]), "cc_agf", 1)
                s.inc(g.collective_compute("AllGather", OP.bypass, replica_groups=PAIRS,
                                           ins=[io["agin_v"].ap().opt()],
                                           outs=[io["agout_v"].ap().opt()]), "cc_agv", 1)
                s.wait(g, "cc_agf")
                nu = 0
                for r in range(2):
                    g.wait_ge(s.h("lbc"), b["lbc"] + nu)
                    for hh in range(2):
                        row = r * 1280 + hh * 640 + 512
                        s.inc(g.dma_start(out=HSTG[:, hh, :],
                                          in_=bass.AP(io["agout_f"], row * T,
                                                      [[2048, 128], [1024, 2], [1, 1024]])), "rb_k", 16)
                    nu += 1
                for gp in range(4):
                    for r in range(2):
                        g.wait_ge(s.h("lbc"), b["lbc"] + nu)
                        for hh in range(2):
                            row = r * 1280 + hh * 640 + gp * 128
                            s.inc(g.dma_start(out=HSTG[:, hh, :],
                                              in_=bass.AP(io["agout_f"], row * T,
                                                          [[2048, 128], [1024, 2], [1, 1024]])), "rb_q", 16)
                        nu += 1
                s.wait(g, "cc_agv")
                for r in range(2):
                    for pos in range(2):
                        for lh in range(2):
                            g.wait_ge(s.h("lbc"), b["lbc"] + nu)
                            for hh in range(2):
                                off = (r * NT + hh * T + pos * 1024) * 128 + lh * 64
                                s.inc(g.dma_start(out=bass.AP(HSTG, hh * 2048,
                                                              [[2 * 2048, 128], [64, 8], [1, 64]]),
                                                  in_=bass.AP(io["agout_v"], off,
                                                              [[128, 128], [128 * 128, 8], [1, 64]])), "rb_v", 16)
                            nu += 1

            @blk.tensor
            def _(t):
                for oc in range(10):
                    t.wait_ge(s.h("awl"), b["awl"] + (oc + 1) * 16)
                    for tq in range(NQC):
                        u = oc * 4 + tq
                        t.wait_ge(s.h(ln_ap), ln_base + tq + 1)
                        if u >= 2:
                            t.wait_ge(s.h("ev_qk"), b["ev_qk"] + u - 1)
                        for kc in range(NCH):
                            i = t.matmul(PS[6 + u % 2][:, :], WST[:, oc % 2, kc, :],
                                         self.XN(kc, tq * QC, QC),
                                         start=(kc == 0), stop=(kc == NCH - 1))
                        s.inc(i, "mm_qk", 1)
                t.wait_ge(s.h("awl"), b["awl"] + 12 * 16)
                for tt in range(16):
                    t.wait_ge(s.h(ln_ap), ln_base + tt // 4 + 1)
                    if tt >= 2:
                        t.wait_ge(s.h("ev_v"), b["ev_v"] + tt - 1)
                    for kc in range(NCH):
                        t.matmul(PS[4 + tt % 2][:, 0:256],
                                 self.XN(kc, tt * 128, 128),
                                 WST[:, :, kc, :], start=(kc == 0), stop=False)
                    i = t.matmul(PS[4 + tt % 2][:, 0:256], self.ONE1B[:], BVR[:],
                                 start=False, stop=True)
                    s.inc(i, "mm_v", 1)

            @blk.vector
            def _(v):
                v.wait_ge(s.h("aw"), b["aw"] + 7 * 16)
                v.memset(R[64:65, 4, 0:128], 1.0)
                v.drain()
                s.inc(v.tensor_copy(R32[64:65, 2, 0:128], R[64:65, 4, 0:128]), "lnabr", 1)
                for u in range(40):
                    oc = u // 4
                    v.wait_ge(s.h("mm_qk"), b["mm_qk"] + u + 1)
                    if u >= 2:
                        v.wait_ge(s.h(f"stq{u % 2}"), b[f"stq{u % 2}"] + (u // 2) * 16)
                    s.inc(v.tensor_scalar(STG[:, u % 2, :], PS[6 + u % 2][:, :],
                                          BATT[:, oc:oc + 1], None, OP.add), "ev_qk", 1)
                for tt in range(16):
                    v.wait_ge(s.h("mm_v"), b["mm_v"] + tt + 1)
                    if tt >= 2:
                        v.wait_ge(s.h(f"stv{tt % 2}"), b[f"stv{tt % 2}"] + (tt // 2) * 16)
                    s.inc(v.tensor_copy(STGV[:, tt % 2, :, :],
                                        PS[4 + tt % 2][:, 0:256]), "ev_v", 1)
                nu = 0
                for r in range(2):
                    dst = bass.AP(KS, GBLK[(r, 0)] * 1024,
                                  [[NT, 128], [(GBLK[(r, 1)] - GBLK[(r, 0)]) * 1024, 2], [1, 1024]])
                    v.wait_ge(s.h("rb_k"), b["rb_k"] + (r + 1) * 32)
                    v.tensor_scalar_mul(HSTG[:, 0, :], HSTG[:, 0, :], FLG[:, 0:1])
                    v.drain()
                    s.inc(v.scalar_tensor_tensor(dst, HSTG[:, 1, :], FLG[:, 1:2],
                                                 HSTG[:, 0, :], OP.mult, OP.add), "lbc", 1)
                    v.drain()
                    nu += 1
                for gp in range(4):
                    for r in range(2):
                        dst = bass.AP(self.BIGA, gp * NT + GBLK[(r, 0)] * 1024,
                                      [[16384, 128], [(GBLK[(r, 1)] - GBLK[(r, 0)]) * 1024, 2], [1, 1024]])
                        v.wait_ge(s.h("rb_q"), b["rb_q"] + (gp * 2 + r + 1) * 32)
                        v.tensor_scalar_mul(HSTG[:, 0, :], HSTG[:, 0, :], FLG[:, 0:1])
                        v.drain()
                        s.inc(v.scalar_tensor_tensor(dst, HSTG[:, 1, :], FLG[:, 1:2],
                                                     HSTG[:, 0, :], OP.mult, OP.add), "lbc", 1)
                        v.drain()
                        nu += 1
                for r in range(2):
                    for pos in range(2):
                        gb = GBLK[(r, pos)]
                        for lh in range(2):
                            dst = bass.AP(VS, gb * 8 * 130 + lh * 65,
                                          [[32 * 130, 128], [130, 8], [1, 64]])
                            k = (r * 2 + pos) * 2 + lh
                            v.wait_ge(s.h("rb_v"), b["rb_v"] + (k + 1) * 32)
                            h0 = bass.AP(HSTG, 0, [[2 * 2048, 128], [64, 8], [1, 64]])
                            h1 = bass.AP(HSTG, 2048, [[2 * 2048, 128], [64, 8], [1, 64]])
                            v.tensor_scalar_mul(h0, h0, FLG[:, 0:1])
                            v.drain()
                            s.inc(v.scalar_tensor_tensor(dst, h1, FLG[:, 1:2],
                                                         h0, OP.mult, OP.add), "lbc", 1)
                            v.drain()
                s.inc(v.tensor_copy(R32[32:33, 5:7, :], R[32:33, 4:6, :]), "lnabr", 1)


            @blk.sync
            def _(sy):
                for u in range(40):
                    oc, tq = u // 4, u % 4
                    sy.wait_ge(s.h("ev_qk"), b["ev_qk"] + u + 1)
                    s.inc(sy.dma_start(out=io["agin_f"][agf_row(oc):agf_row(oc) + 128,
                                                        tq * QC:(tq + 1) * QC],
                                       in_=STG[:, u % 2, :]), f"stq{u % 2}", 16)
                for tt in range(16):
                    sy.wait_ge(s.h("ev_v"), b["ev_v"] + tt + 1)
                    dst = bass.AP(io["agin_v"], tt * 128 * 128,
                                  [[128, 128], [2048 * 128, 2], [1, 128]])
                    s.inc(sy.dma_start(out=dst, in_=STGV[:, tt % 2, :, :]), f"stv{tt % 2}", 16)

        if "o_ks" in io:
            with nc.Block() as blkd:
                @blkd.gpsimd
                def _(g):
                    bst = s.val("st")
                    s.inc(g.dma_start(out=io["o_ks"][:, :], in_=KS[:, :]), "st", 16)
                    for gp in range(4):
                        s.inc(g.dma_start(out=io["o_qs"][gp * 128:(gp + 1) * 128, :],
                                          in_=self.BIGA[:, gp * NT:(gp + 1) * NT]), "st", 16)
                    s.inc(g.dma_start(out=io["o_vs"][:, :],
                                      in_=bass.AP(VS, 0, [[32 * 130, 128], [1, 32 * 130]])), "st", 16)
                    g.wait_ge(s.h("st"), bst + 6 * 16)

        def epi_range(qc):
            return (qc - 1, qc) if qc % 2 == 1 else ()

        with nc.Block() as blk:
            @blk.tensor
            def _(t):
                t.wait_ge(s.h("lbc"), b["lbc"] + 18)
                t.wait_ge(s.h("awc"), b["awc"] + 1)
                t.wait_ge(s.h("ev_qk"), b["ev_qk"] + 40)
                t.wait_ge(s.h("ev_v"), b["ev_v"] + 16)
                for qc in range(8):
                    for gp in range(4):
                        w = qc * 4 + gp
                        u0, nj = UPREF[w], NJ[qc]
                        for j in range(nj):
                            u = u0 + j
                            if u >= 2:
                                t.wait_ge(s.h("expm"), b["expm"] + u - 1)
                            t.matmul(PS[2 * (u % 2)][:, :], KS[0:64, j * 128:(j + 1) * 128],
                                     QSv(gp, qc * QC, QC)[0:64, :], start=True, stop=True,
                                     tile_position=(0, 0))
                            s.inc(t.matmul(PS[2 * (u % 2) + 1][:, :],
                                           KS[64:128, j * 128:(j + 1) * 128],
                                           QSv(gp, qc * QC, QC)[64:128, :], start=True, stop=True,
                                           tile_position=(64, 0)), "st_d", 1)
                            for jj in ([j - 1] if j >= 1 else []) + ([j] if j == nj - 1 else []):
                                uu = u0 + jj
                                t.wait_ge(s.h("expm"), b["expm"] + uu + 1)
                                if jj == 0:
                                    t.wait_ge(s.h("otd"), b["otd"] + 2 * w)
                                for lh in range(2):
                                    i = t.matmul(PS[4 + lh][0:65, :], VS[:, jj, lh, :],
                                                 PT[:, uu % 2, lh, :],
                                                 start=(jj == 0), stop=(jj == nj - 1))
                                s.inc(i, "av_d", 1)
                        t.wait_ge(s.h("lrow"), b["lrow"] + w + 1)
                        t.wait_ge(s.h("b67r"), b["b67r"] + b67_use(qc, "lbc", gp))
                        t.wait_ge(s.h("lnabr"), b["lnabr"] + 2)
                        t.matmul(PS[6][0:64, :], R32[64:65, 2, 0:64], R32[64:65, 0, :],
                                 start=True, stop=True)
                        s.inc(t.matmul(PS[7][0:64, :], R32[64:65, 2, 0:64], R32[64:65, 1, :],
                                       start=True, stop=True), "lbc", 1)
                    t.wait_ge(s.h("otd"), b["otd"] + 8 * (qc + 1))
                    t.wait_ge(s.h("b67r"), b["b67r"] + b67_use(qc, "stats"))
                    for m in range(8):
                        i = t.matmul(PS[6][32:33, :], self.ONECB[0:64, :], OT[0:64, qc % 2, m, :],
                                     start=(m == 0), stop=(m == 7))
                    s.inc(i, "ssum", 1)
                    for rnd in range(4):
                        t.wait_ge(s.h("sq2"), b["sq2"] + qc * 4 + rnd + 1)
                        for mm2 in range(2):
                            i = t.matmul(PS[7][32:33, :], self.ONECB[0:64, :], STG[0:64, mm2, :],
                                         start=(rnd == 0 and mm2 == 0), stop=(rnd == 3 and mm2 == 1))
                        s.inc(i, "s2m", 1)
                    s.inc(t.drain(), "ssum", 1)
                    for q2 in epi_range(qc):
                        t.wait_ge(s.h("lnabr"), b["lnabr"] + 2)
                        for m in range(8):
                            t.wait_ge(s.h("arrow"), b["arrow"] + 3 * q2 + 3)
                            t.wait_ge(s.h("b67r"), b["b67r"] + b67_use(q2, "bc", m))
                            par = 3 * (q2 % 2)
                            lg_m = R32[32:33, 5, m * 64:(m + 1) * 64]
                            lb_m = R32[32:33, 6, m * 64:(m + 1) * 64]
                            t.matmul(PS[6][0:64, :], lg_m, R32[32:33, 0 + par, :],
                                     start=True, stop=True)
                            t.matmul(PS[7][0:64, :], lg_m, R32[32:33, 1 + par, :],
                                     start=True, stop=False)
                            s.inc(t.matmul(PS[7][0:64, :], lb_m, R32[32:33, 2, :],
                                           start=False, stop=True), "bcap", 1)
                        for dch in range(NCH):
                            t.wait_ge(s.h("apm"), b["apm"] + 8 * (q2 + 1))
                            t.wait_ge(s.h("b67r"), b["b67r"] + b67_use(q2, "wo", dch))
                            for m in range(8):
                                i = t.matmul(PS[6 + dch % 2][:, :], WOA[0:64, m, dch, :],
                                             OT[0:64, q2 % 2, m, :], start=(m == 0), stop=(m == 7))
                            s.inc(i, "wod", 1)

            @blk.scalar
            def _(sc):
                for qc in range(8):
                    for gp in range(4):
                        w = qc * 4 + gp
                        u0, nj = UPREF[w], NJ[qc]
                        for j in range(nj):
                            u = u0 + j
                            sc.wait_ge(s.h("st_d"), b["st_d"] + u + 1)
                            if u >= 2:
                                sc.wait_ge(s.h("av_d"), b["av_d"] + u - 1)
                            i = sc.activation(PT[:, u % 2, :, :], G[u % 2], AF.Exp,
                                              scale=self.C0125[:])
                            if j >= 4 * qc:
                                s.inc(i, "exprm", 1)
                            else:
                                s.inc(i, "expm", 1)
                    sc.wait_ge(s.h("otd"), b["otd"] + 8 * (qc + 1))
                    for rnd in range(4):
                        if qc * 4 + rnd >= 1:
                            sc.wait_ge(s.h("s2m"), b["s2m"] + qc * 4 + rnd)
                        s.inc(sc.activation(STG[0:64, :, :], OT[0:64, qc % 2, 2 * rnd:2 * rnd + 2, :],
                                            AF.Square), "sq2", 1)
                    for q2 in epi_range(qc):
                        sc.wait_ge(s.h("arrow"), b["arrow"] + 3 * q2 + 1)
                        sc.activation(R[32:33, 3, :], R[32:33, 3, :], AF.Ln, bias=self.EPS[32:33, :])
                        sc.drain()
                        s.inc(sc.activation(R[32:33, 4, :], R[32:33, 3, :], AF.Exp,
                                            scale=self.NHALF[32:33, :]), "arrow", 1)

            @blk.vector
            def _(v):
                nmask = 0
                for qc in range(8):
                    for gp in range(4):
                        w = qc * 4 + gp
                        u0, nj = UPREF[w], NJ[qc]
                        for j in range(4 * qc, nj):
                            u = u0 + j
                            k = j - 4 * qc
                            v.wait_ge(s.h("exprm"), b["exprm"] + nmask + 1)
                            nmask += 1
                            for lh in range(2):
                                i = v.tensor_mul(PT[:, u % 2, lh, :], PT[:, u % 2, lh, :],
                                                 AM[:, k, :])
                            s.inc(i, "expm", 1)
                        v.wait_ge(s.h("av_d"), b["av_d"] + u0 + nj)
                        v.tensor_copy(R[64:65, 0, :], PS[4][64:65, :])
                        v.tensor_copy(R[64:65, 1, :], PS[5][64:65, :])
                        v.drain()
                        v.reciprocal(R[64:65, 2, :], R[64:65, 0, :])
                        v.reciprocal(R[64:65, 3, :], R[64:65, 1, :])
                        v.drain()
                        v.tensor_copy(R32[64:65, 0, :], R[64:65, 2, :])
                        s.inc(v.tensor_copy(R32[64:65, 1, :], R[64:65, 3, :]), "lrow", 1)
                        v.wait_ge(s.h("lbc"), b["lbc"] + 18 + w + 1)
                        v.tensor_copy(LBC[0:64, 0, :], PS[6][0:64, :])
                        v.tensor_copy(LBC[0:64, 1, :], PS[7][0:64, :])
                        s.inc(v.drain(), "b67r", 1)
                        if qc >= 2:
                            v.wait_ge(s.h("wod"), b["wod"] + 8 * (qc - 1))
                        for lh in range(2):
                            m = gp * 2 + lh
                            s.inc(v.scalar_tensor_tensor(OT[0:64, qc % 2, m, :], PS[4 + lh][0:64, :],
                                                         1.0, LBC[0:64, lh, :], OP.mult, OP.mult),
                                  "otd", 1)
                    v.wait_ge(s.h("ssum"), b["ssum"] + 2 * (qc + 1))
                    if qc >= 1:
                        pn = f"star{(qc - 1) % 2}"
                        v.wait_ge(s.h(pn), b[pn] + ((qc - 1) // 2 + 1) * 32)
                    v.tensor_copy(R[32:33, 0, :], PS[6][32:33, :])
                    v.tensor_copy(R[32:33, 1, :], PS[7][32:33, :])
                    s.inc(v.drain(), "b67r", 1)
                    s.inc(v.drain(), "strow", 1)
                    for q2 in epi_range(qc):
                        v.wait_ge(s.h("rb_ar"), b["rb_ar"] + (q2 + 1) * 32)
                        v.tensor_scalar_mul(R[32:33, 2, :], R[32:33, 4, :], 1.0 / D)
                        v.tensor_scalar_mul(R[32:33, 3, :], R[32:33, 5, :], 1.0 / D)
                        v.drain()
                        v.tensor_tensor(self.TMP[32:33, 0, 0:QC], R[32:33, 2, :], R[32:33, 2, :], OP.mult)
                        v.drain()
                        s.inc(v.tensor_sub(R[32:33, 3, :], R[32:33, 3, :], self.TMP[32:33, 0, 0:QC]),
                              "arrow", 1)
                        v.wait_ge(s.h("arrow"), b["arrow"] + 3 * q2 + 2)
                        par = 3 * (q2 % 2)
                        v.tensor_mul(R[32:33, 5, :], R[32:33, 2, :], R[32:33, 4, :])
                        v.drain()
                        v.tensor_scalar_mul(R[32:33, 5, :], R[32:33, 5, :], -1.0)
                        v.drain()
                        v.tensor_copy(R32[32:33, 0 + par, :], R[32:33, 4, :])
                        s.inc(v.tensor_copy(R32[32:33, 1 + par, :], R[32:33, 5, :]), "arrow", 1)
                        for m in range(8):
                            v.wait_ge(s.h("bcap"), b["bcap"] + 8 * q2 + m + 1)
                            v.tensor_tensor(TMPB[0:64, :], OT[0:64, q2 % 2, m, :], PS[6][0:64, :],
                                            OP.mult)
                            v.drain()
                            v.tensor_add(OT[0:64, q2 % 2, m, :], TMPB[0:64, :], PS[7][0:64, :])
                            s.inc(v.drain(), "b67r", 1)
                            s.inc(v.drain(), "apm", 1)
                        for dch in range(NCH):
                            v.wait_ge(s.h("wod"), b["wod"] + 8 * q2 + dch + 1)
                            kk = 8 * q2 + dch
                            if kk >= 2:
                                v.wait_ge(s.h(f"sty{kk % 2}"), b[f"sty{kk % 2}"] + (kk // 2) * 16)
                            v.tensor_copy(YPS[:, dch % 2, :], PS[6 + dch % 2][:, :])
                            s.inc(v.drain(), "b67r", 1)
                            s.inc(v.drain(), "wo_ev", 1)
                if True:
                    v.wait_ge(s.h("rb_y"), b["rb_y"] + NCH * 16)
                    for c in range(NCH):
                        s.inc(v.tensor_scalar(self.X[:, c, :], self.X[:, c, :],
                                              BO[:, c:c + 1], None, OP.add), "xadd", 1)

            @blk.sync
            def _(sy):
                for qc in range(8):
                    sy.wait_ge(s.h("strow"), b["strow"] + qc + 1)
                    gidx, qoff = qc // 2, (qc % 2) * QC
                    arn = f"star{qc % 2}"
                    s.inc(sy.dma_start(out=io["ar_in"][gidx][0:1, qoff:qoff + QC],
                                       in_=R[32:33, 0, :]), arn, 16)
                    s.inc(sy.dma_start(out=io["ar_in"][gidx][1:2, qoff:qoff + QC],
                                       in_=R[32:33, 1, :]), arn, 16)
                    for q2 in epi_range(qc):
                        rr, cbase = RSQ[q2]
                        for dch in range(NCH):
                            sy.wait_ge(s.h("wo_ev"), b["wo_ev"] + 8 * q2 + dch + 1)
                            kk = 8 * q2 + dch
                            row0 = rr * 1024 + dch * 128
                            s.inc(sy.dma_start(out=io["rs_in"][row0:row0 + 128, cbase:cbase + QC],
                                               in_=YPS[:, dch % 2, :]), f"sty{kk % 2}", 16)

            @blk.gpsimd
            def _(g):
                for gidx in range(4):
                    g.wait_ge(s.h("star0"), b["star0"] + (gidx + 1) * 32)
                    g.wait_ge(s.h("star1"), b["star1"] + (gidx + 1) * 32)
                    s.inc(g.collective_compute("AllReduce", OP.add, replica_groups=PAIRS,
                                               ins=[io["ar_in"][gidx].ap().opt()],
                                               outs=[io["ar_out"][gidx].ap().opt()]), "cc_ar", 1)
                    g.wait_ge(s.h("cc_ar"), b["cc_ar"] + gidx + 1)
                    for qq in range(2):
                        qc = gidx * 2 + qq
                        qoff = qq * QC
                        if qc >= 1:
                            g.wait_ge(s.h("arrow"), b["arrow"] + 3 * (qc - 1) + 3)
                        g.wait_ge(s.h("lnabr"), b["lnabr"] + 2)
                        s.inc(g.dma_start(out=R[32:33, 4, :],
                                          in_=io["ar_out"][gidx][0:1, qoff:qoff + QC]), "rb_ar", 16)
                        s.inc(g.dma_start(out=R[32:33, 5, :],
                                          in_=io["ar_out"][gidx][1:2, qoff:qoff + QC]), "rb_ar", 16)
                g.wait_ge(s.h("sty0"), b["sty0"] + 32 * 16)
                g.wait_ge(s.h("sty1"), b["sty1"] + 32 * 16)
                s.inc(g.collective_compute("ReduceScatter", OP.add, replica_groups=PAIRS,
                                           ins=[io["rs_in"].ap().opt()],
                                           outs=[io["rs_out"].ap().opt()]), "cc_rs", 1)
                s.wait(g, "cc_rs")
                for c in range(NCH):
                    s.inc(g.dma_start(out=self.X[:, c, :],
                                      in_=io["rs_out"][c * 128:(c + 1) * 128, :],
                                      accum_op=OP.add), "rb_y", 16)
        if "o_lbc" in io:
            with nc.Block() as blkd2:
                @blkd2.gpsimd
                def _(g):
                    bst = s.val("st")
                    s.inc(g.dma_start(out=io["o_lbc"].ap()[0:64, :],
                                      in_=bass.AP(LBC, 0, [[2 * QC, 64], [1, 2 * QC]])), "st", 16)
                    s.inc(g.dma_start(out=bass.AP(io["o_r32"], 0, [[3584, 1], [512, 2], [1, 512]]),
                                      in_=self.R32[64:65, 0:2, :]), "st", 16)
                    s.inc(g.dma_start(out=io["o_ot"].ap()[0:64, :],
                                      in_=bass.AP(OT, 0, [[16 * QC, 64], [1, 16 * QC]])), "st", 16)
                    g.wait_ge(s.h("st"), bst + 3 * 16)
        lctx.close()
        return "xadd"

    def attn_b(self, io, ln_ap, ln_base, upto=3):
        nc, s = self.nc, self.s
        R, R32 = self.ROWS, self.R32
        PS, G = self.PS, self.G
        GBLK = {(0, 0): 0, (0, 1): 3, (1, 0): 1, (1, 1): 2}
        RSQ = [(0, 0), (0, 512), (1, 0), (1, 512), (1, 1024), (1, 1536), (0, 1024), (0, 1536)]

        ctxA = ExitStack()
        E = ctxA.enter_context
        HSTG = E(nc.sbuf_tensor("bHSTG", [128, 2, 2048], BF16))
        OTB = E(nc.sbuf_tensor("bOT", [128, 4, 8, QC], BF16))
        FLG = E(nc.sbuf_tensor("bFLG", [128, 2], F32))

        ctxP = ExitStack()
        EP = ctxP.enter_context
        WSTB = EP(nc.sbuf_tensor("bWST", [128, 2, NCH, 128], BF16))
        BQB = EP(nc.sbuf_tensor("bBQ", [128, 16], F32))
        BVRB = EP(nc.sbuf_tensor("bBVR", [1, 1024], BF16))
        TB = EP(nc.sbuf_tensor("bTB", [128, 2, T], BF16))
        ROTB = EP(nc.sbuf_tensor("bROT", [128, 128], BF16))
        STG = EP(nc.sbuf_tensor("bSTG", [128, 2, QC], BF16))
        STG2 = EP(nc.sbuf_tensor("bSG2", [128, 2, QC], BF16))
        TMPT = EP(nc.sbuf_tensor("bTMT", [128, QC], BF16))
        STGV = EP(nc.sbuf_tensor("bSGV", [128, 2, 2, 128], BF16))

        with nc.Block() as blk:
            b1 = {k: s.val(k) for k in ("aw awl mm_qk mm_v ev_qk ev_v sq2 exprm expm s2m "
                                        "stq0 stq1 stv0 stv1 cc_agf cc_agv lbc rb_q").split()}

            @blk.gpsimd
            def _(g):
                s.inc(g.dma_start(out=BQB[:], in_=io["b_bq"][:, :]), "aw", 16)
                s.inc(g.dma_start(out=BVRB[:], in_=io["bvr_b"][:, :]), "aw", 16)
                s.inc(g.dma_start(out=TB[:, 0, :], in_=io["b_tab"][:, 0:T]), "aw", 16)
                s.inc(g.dma_start(out=TB[:, 1, :], in_=io["b_tab"][:, T:2 * T]), "aw", 16)
                s.inc(g.dma_start(out=ROTB[:], in_=io["rotp"][:, :]), "aw", 16)
                s.inc(g.dma_start(out=FLG[:], in_=io["hflag"][:, :]), "aw", 16)
                for oc in range(16):
                    if oc >= 2:
                        g.wait_ge(s.h("mm_qk"), b1["mm_qk"] + (oc - 1) * 4)
                    g.wait_ge(s.h("awl"), b1["awl"] + oc * 16)
                    s.inc(g.dma_start(out=WSTB[:, oc % 2, :, :],
                                      in_=bass.AP(io["w_b"], oc * 128,
                                                  [[3072, 128], [3072 * 128, 8], [1, 128]])), "awl", 16)
                g.wait_ge(s.h("mm_qk"), b1["mm_qk"] + 64)
                for vbp in range(4):
                    if vbp >= 1:
                        g.wait_ge(s.h("mm_v"), b1["mm_v"] + vbp * 16)
                    for sl in range(2):
                        g.wait_ge(s.h("awl"), b1["awl"] + (16 + vbp * 2 + sl) * 16)
                        s.inc(g.dma_start(out=WSTB[:, sl, :, :],
                                          in_=bass.AP(io["w_b"], (16 + vbp * 2 + sl) * 128,
                                                      [[3072, 128], [3072 * 128, 8], [1, 128]])), "awl", 16)
                g.wait_ge(s.h("stq0"), b1["stq0"] + 32 * 16)
                g.wait_ge(s.h("stq1"), b1["stq1"] + 32 * 16)
                g.wait_ge(s.h("stv0"), b1["stv0"] + 32 * 16)
                g.wait_ge(s.h("stv1"), b1["stv1"] + 32 * 16)
                s.inc(g.collective_compute("AllGather", OP.bypass, replica_groups=PAIRS,
                                           ins=[io["agin_fb"].ap().opt()],
                                           outs=[io["agout_fb"].ap().opt()]), "cc_agf", 1)
                s.inc(g.collective_compute("AllGather", OP.bypass, replica_groups=PAIRS,
                                           ins=[io["agin_vb"].ap().opt()],
                                           outs=[io["agout_vb"].ap().opt()]), "cc_agv", 1)
                g.wait_ge(s.h("cc_agf"), b1["cc_agf"] + 1)
                for nu in range(8):
                    hp, r = nu // 2, nu % 2
                    g.wait_ge(s.h("lbc"), b1["lbc"] + nu)
                    for hh in range(2):
                        row = r * 2048 + hh * 1024 + hp * 128
                        s.inc(g.dma_start(out=HSTG[:, hh, :],
                                          in_=bass.AP(io["agout_fb"], row * T,
                                                      [[2048, 128], [1024, 2], [1, 1024]])), "rb_q", 16)

            @blk.tensor
            def _(t):
                t.wait_ge(s.h("aw"), b1["aw"] + 6 * 16)
                for oc in range(16):
                    t.wait_ge(s.h("awl"), b1["awl"] + (oc + 1) * 16)
                    for tq in range(NQC):
                        u = oc * 4 + tq
                        t.wait_ge(s.h(ln_ap), ln_base + tq + 1)
                        if u >= 2:
                            t.wait_ge(s.h("ev_qk"), b1["ev_qk"] + u - 1)
                        for kc in range(NCH):
                            i = t.matmul(PS[6 + u % 2][:, :], WSTB[:, oc % 2, kc, :],
                                         self.XN(kc, tq * QC, QC),
                                         start=(kc == 0), stop=(kc == NCH - 1))
                        s.inc(i, "mm_qk", 1)
                        for uu in ([u - 1] if u >= 1 else []) + ([u] if u == 63 else []):
                            t.wait_ge(s.h("ev_qk"), b1["ev_qk"] + uu + 1)
                            if uu >= 2:
                                t.wait_ge(s.h("exprm"), b1["exprm"] + uu - 1)
                            s.inc(t.matmul(PS[2 + uu % 2][:, :], ROTB[:, :], STG[:, uu % 2, :],
                                           start=True, stop=True), "sq2", 1)
                t.wait_ge(s.h("awl"), b1["awl"] + 18 * 16)
                for vbp in range(4):
                    for tt in range(16):
                        vt = vbp * 16 + tt
                        if vbp >= 1:
                            t.wait_ge(s.h("awl"), b1["awl"] + (16 + vbp * 2 + 2) * 16)
                        t.wait_ge(s.h(ln_ap), ln_base + tt // 4 + 1)
                        if vt >= 2:
                            t.wait_ge(s.h("ev_v"), b1["ev_v"] + vt - 1)
                        for kc in range(NCH):
                            t.matmul(PS[4 + vt % 2][:, 0:256],
                                     self.XN(kc, tt * 128, 128),
                                     WSTB[:, :, kc, :], start=(kc == 0), stop=False)
                        i = t.matmul(PS[4 + vt % 2][:, 0:256], self.ONE1B[:],
                                     BVRB[0:1, vbp * 256:(vbp + 1) * 256],
                                     start=False, stop=True)
                        s.inc(i, "mm_v", 1)

            @blk.vector
            def _(v):
                v.wait_ge(s.h("aw"), b1["aw"] + 6 * 16)
                for u in range(64):
                    oc, tq = u // 4, u % 4
                    v.wait_ge(s.h("mm_qk"), b1["mm_qk"] + u + 1)
                    if u >= 2:
                        v.wait_ge(s.h("sq2"), b1["sq2"] + u - 1)
                    v.drain()
                    s.inc(v.tensor_scalar(STG[:, u % 2, :], PS[6 + u % 2][:, :],
                                          BQB[:, oc:oc + 1], None, OP.add), "ev_qk", 1)
                    v.wait_ge(s.h("sq2"), b1["sq2"] + u + 1)
                    if u >= 2:
                        v.wait_ge(s.h(f"stq{u % 2}"), b1[f"stq{u % 2}"] + (u // 2) * 16)
                    v.tensor_mul(TMPT[:, :], STG[:, u % 2, :], TB[:, 0, tq * QC:(tq + 1) * QC])
                    v.scalar_tensor_tensor(STG2[:, u % 2, :], PS[2 + u % 2][:, :], 1.0,
                                           TB[:, 1, tq * QC:(tq + 1) * QC], OP.mult, OP.mult)
                    s.inc(v.drain(), "exprm", 1)
                    v.drain()
                    s.inc(v.tensor_add(STG2[:, u % 2, :], STG2[:, u % 2, :], TMPT[:, :]),
                          "expm", 1)
                for vt in range(64):
                    v.wait_ge(s.h("mm_v"), b1["mm_v"] + vt + 1)
                    if vt >= 2:
                        v.wait_ge(s.h(f"stv{vt % 2}"), b1[f"stv{vt % 2}"] + (vt // 2) * 16)
                    v.drain()
                    s.inc(v.tensor_copy(STGV[:, vt % 2, :, :],
                                        PS[4 + vt % 2][:, 0:256]), "ev_v", 1)
                for nu in range(8):
                    hp, r = nu // 2, nu % 2
                    dst = bass.AP(self.BIGA, hp * NT + GBLK[(r, 0)] * 1024,
                                  [[16384, 128], [(GBLK[(r, 1)] - GBLK[(r, 0)]) * 1024, 2], [1, 1024]])
                    v.wait_ge(s.h("rb_q"), b1["rb_q"] + (nu + 1) * 32)
                    v.tensor_scalar_mul(HSTG[:, 0, :], HSTG[:, 0, :], FLG[:, 0:1])
                    v.drain()
                    s.inc(v.scalar_tensor_tensor(dst, HSTG[:, 1, :], FLG[:, 1:2],
                                                 HSTG[:, 0, :], OP.mult, OP.add), "lbc", 1)
                    v.drain()

            @blk.sync
            def _(sy):
                for u in range(64):
                    oc, tq = u // 4, u % 4
                    sy.wait_ge(s.h("expm"), b1["expm"] + u + 1)
                    s.inc(sy.dma_start(out=io["agin_fb"][oc * 128:(oc + 1) * 128,
                                                         tq * QC:(tq + 1) * QC],
                                       in_=STG2[:, u % 2, :]), f"stq{u % 2}", 16)
                for vt in range(64):
                    vbp, tt = vt // 16, vt % 16
                    sy.wait_ge(s.h("ev_v"), b1["ev_v"] + vt + 1)
                    dst = bass.AP(io["agin_vb"], (vbp * 2 * 2048 + tt * 128) * 128,
                                  [[128, 128], [2048 * 128, 2], [1, 128]])
                    s.inc(sy.dma_start(out=dst, in_=STGV[:, vt % 2, :, :]), f"stv{vt % 2}", 16)
        ctxP.close()
        if upto == 1:
            ctxA.close()
            return "lbc"
        ctxB2 = ExitStack()
        EB = ctxB2.enter_context
        LBCB = EB(nc.sbuf_tensor("bLBC", [128, 2, QC], F32))
        TMPB2 = EB(nc.sbuf_tensor("bTM2", [128, QC], BF16))
        IDB = EB(nc.sbuf_tensor("bIDB", [64, 64], BF16))
        PT = EB(nc.sbuf_tensor("bPT", [128, 2, 2, QC], BF16))
        VS = EB(nc.sbuf_tensor("bVS", [128, 32, 2, 65], BF16))
        KB = EB(nc.sbuf_tensor("bKB", [128, NT], BF16))
        HSTGV = EB(nc.sbuf_tensor("bHSV", [128, 2, 8, 128], BF16))

        with nc.Block() as blk:
            b2 = {k: s.val(k) for k in ("s2m rb_k rb_v st_d exprm av_d lrow lbc otd b67r "
                                        "cc_agv lnabr awc aw tb1 ssum").split()}

            @blk.gpsimd
            def _(g):
                s.inc(g.dma_start(out=IDB[:, :], in_=io["eye64"][:, :]), "aw", 16)
                s.inc(g.memset(VS[:, :, :, 64:65], 1.0), "awc", 1)
                g.wait_ge(s.h("cc_agv"), b2["cc_agv"])
                for hp in range(4):
                    for r in range(2):
                        uk = hp * 6 + r
                        g.wait_ge(s.h("s2m"), b2["s2m"] + uk)
                        for hh in range(2):
                            row = r * 2048 + hh * 1024 + 512 + hp * 128
                            s.inc(g.dma_start(out=HSTG[:, hh, :],
                                              in_=bass.AP(io["agout_fb"], row * T,
                                                          [[2048, 128], [1024, 2], [1, 1024]])), "rb_k", 16)
                    for r in range(2):
                        for pos in range(2):
                            uk = hp * 6 + 2 + r * 2 + pos
                            g.wait_ge(s.h("s2m"), b2["s2m"] + uk)
                            for hh in range(2):
                                fc = 4 * hh + hp
                                off = (r * 16384 + fc * 2048 + pos * 1024) * 128
                                s.inc(g.dma_start(out=HSTGV[:, hh, :, :],
                                                  in_=bass.AP(io["agout_vb"], off,
                                                              [[128, 128], [128 * 128, 8], [1, 128]])), "rb_v", 16)

            @blk.vector
            def _(v):
                nun = 0
                for hp in range(4):
                    for r in range(2):
                        nun += 1
                        dst = bass.AP(KB, GBLK[(r, 0)] * 1024,
                                      [[NT, 128], [(GBLK[(r, 1)] - GBLK[(r, 0)]) * 1024, 2], [1, 1024]])
                        v.wait_ge(s.h("rb_k"), b2["rb_k"] + nun * 32)
                        if hp >= 1 and r == 0:
                            v.wait_ge(s.h("st_d"), b2["st_d"] + hp * 256)
                        v.tensor_scalar_mul(HSTG[:, 0, :], HSTG[:, 0, :], FLG[:, 0:1])
                        v.drain()
                        s.inc(v.scalar_tensor_tensor(dst, HSTG[:, 1, :], FLG[:, 1:2],
                                                     HSTG[:, 0, :], OP.mult, OP.add), "s2m", 1)
                        v.drain()
                    for r in range(2):
                        for pos in range(2):
                            nvv = hp * 4 + r * 2 + pos
                            v.wait_ge(s.h("rb_v"), b2["rb_v"] + (nvv + 1) * 32)
                            if hp >= 1 and r == 0 and pos == 0:
                                v.wait_ge(s.h("av_d"), b2["av_d"] + hp * 264)
                            gb8 = GBLK[(r, pos)] * 8
                            v.tensor_scalar_mul(HSTGV[:, 0, :, :], HSTGV[:, 0, :, :], FLG[:, 0:1])
                            v.drain()
                            for lh in range(2):
                                lo = lh * 65
                                dstv = bass.AP(VS, gb8 * 130 + lo, [[32 * 130, 128], [130, 8], [1, 64]])
                                i = v.scalar_tensor_tensor(
                                    dstv,
                                    bass.AP(HSTGV, 1024 + lh * 64, [[2048, 128], [128, 8], [1, 64]]),
                                    FLG[:, 1:2],
                                    bass.AP(HSTGV, lh * 64, [[2048, 128], [128, 8], [1, 64]]),
                                    OP.mult, OP.add)
                            s.inc(i, "s2m", 1)
                            v.drain()
                    for qc in range(8):
                        idx = hp * 8 + qc
                        v.wait_ge(s.h("av_d"), b2["av_d"] + (idx + 1) * 33)
                        v.tensor_copy(R[64:65, 0, :], PS[4][64:65, :])
                        v.tensor_copy(R[64:65, 1, :], PS[5][64:65, :])
                        v.drain()
                        v.reciprocal(R[64:65, 2, :], R[64:65, 0, :])
                        v.reciprocal(R[64:65, 3, :], R[64:65, 1, :])
                        v.drain()
                        v.tensor_copy(R32[64:65, 0, :], R[64:65, 2, :])
                        s.inc(v.tensor_copy(R32[64:65, 1, :], R[64:65, 3, :]), "lrow", 1)
                        v.wait_ge(s.h("lbc"), b2["lbc"] + idx + 1)
                        v.tensor_copy(LBCB[0:64, 0, :], PS[6][0:64, :])
                        v.tensor_copy(LBCB[0:64, 1, :], PS[7][0:64, :])
                        s.inc(v.drain(), "b67r", 1)
                        s.inc(v.scalar_tensor_tensor(OTB[0:64, hp, qc, :], PS[4][0:64, :],
                                                     1.0, LBCB[0:64, 0, :], OP.mult, OP.mult), "otd", 1)
                        s.inc(v.scalar_tensor_tensor(TMPB2[0:64, :], PS[5][0:64, :],
                                                     1.0, LBCB[0:64, 1, :], OP.mult, OP.mult), "tb1", 1)
                        v.wait_ge(s.h("ssum"), b2["ssum"] + idx + 1)
                        v.drain()
                        v.tensor_copy(OTB[64:128, hp, qc, :], PS[7][64:128, :])
                        s.inc(v.drain(), "b67r", 1)

            @blk.tensor
            def _(t):
                t.wait_ge(s.h("lbc"), b2["lbc"])
                t.wait_ge(s.h("lnabr"), b2["lnabr"])
                t.wait_ge(s.h("awc"), b2["awc"] + 1)
                for hp in range(4):
                    t.wait_ge(s.h("s2m"), b2["s2m"] + (hp + 1) * 6)
                    for qc in range(8):
                        idx = hp * 8 + qc
                        for j in range(32):
                            u = idx * 32 + j
                            if u >= 2:
                                t.wait_ge(s.h("exprm"), b2["exprm"] + u - 1)
                            t.matmul(PS[2 * (u % 2)][:, :], KB[0:64, j * 128:(j + 1) * 128],
                                     self.BIGA[0:64, hp * NT + qc * QC: hp * NT + (qc + 1) * QC],
                                     start=True, stop=True, tile_position=(0, 0))
                            s.inc(t.matmul(PS[2 * (u % 2) + 1][:, :],
                                           KB[64:128, j * 128:(j + 1) * 128],
                                           self.BIGA[64:128, hp * NT + qc * QC: hp * NT + (qc + 1) * QC],
                                           start=True, stop=True, tile_position=(64, 0)), "st_d", 1)
                            for jj in ([j - 1] if j >= 1 else []) + ([j] if j == 31 else []):
                                uu = idx * 32 + jj
                                t.wait_ge(s.h("exprm"), b2["exprm"] + uu + 1)
                                if jj == 0:
                                    t.wait_ge(s.h("otd"), b2["otd"] + idx)
                                    t.wait_ge(s.h("tb1"), b2["tb1"] + idx)
                                t.matmul(PS[4][0:65, :], VS[:, jj, 0, 0:65],
                                         PT[:, uu % 2, 0, :], start=(jj == 0), stop=(jj == 31))
                                i = t.matmul(PS[5][0:65, :], VS[:, jj, 1, 0:65],
                                             PT[:, uu % 2, 1, :], start=(jj == 0), stop=(jj == 31))
                                s.inc(i, "av_d", 2 if jj == 31 else 1)
                        t.wait_ge(s.h("lrow"), b2["lrow"] + idx + 1)
                        t.wait_ge(s.h("b67r"), b2["b67r"] + 2 * idx)
                        t.matmul(PS[6][0:64, :], R32[64:65, 2, 0:64], R32[64:65, 0, :],
                                 start=True, stop=True)
                        s.inc(t.matmul(PS[7][0:64, :], R32[64:65, 2, 0:64], R32[64:65, 1, :],
                                       start=True, stop=True), "lbc", 1)
                        t.wait_ge(s.h("tb1"), b2["tb1"] + idx + 1)
                        t.wait_ge(s.h("aw"), b2["aw"] + 16)
                        s.inc(t.matmul(PS[7][64:128, :], IDB[0:64, :], TMPB2[0:64, :],
                                       start=True, stop=True), "ssum", 1)

            @blk.scalar
            def _(sc):
                for idx in range(32):
                    for j in range(32):
                        u = idx * 32 + j
                        sc.wait_ge(s.h("st_d"), b2["st_d"] + u + 1)
                        if j >= 2:
                            sc.wait_ge(s.h("av_d"), b2["av_d"] + idx * 33 + j - 1)
                        elif idx >= 1:
                            sc.wait_ge(s.h("av_d"), b2["av_d"] + idx * 33)
                        s.inc(sc.activation(PT[:, u % 2, :, :], G[u % 2], AF.Exp,
                                            scale=self.C0125[:]), "exprm", 1)
        ctxB2.close()
        if upto == 2:
            ctxA.close()
            return "otd"
        ctxW = ExitStack()
        EW = ctxW.enter_context
        WOB = EW(nc.sbuf_tensor("bWO", [128, 4, NCH, 128], BF16))
        YPS = EW(nc.sbuf_tensor("bYPS", [128, 2, QC], F32))
        YB = EW(nc.sbuf_tensor("bYB", [128, QC], F32))
        BOB = EW(nc.sbuf_tensor("bBO", [128, NCH], F32))
        with nc.Block() as blk:
            b3 = {k: s.val(k) for k in ("aw wod wo_ev sty0 sty1 cc_rs rb_y xadd st_y").split()}

            @blk.gpsimd
            def _(g):
                s.inc(g.dma_start(out=WOB[:], in_=io["wo_b"][:, :]), "aw", 16)
                s.inc(g.dma_start(out=BOB[:], in_=io["bo_b"][:, :]), "aw", 16)
                g.wait_ge(s.h("sty0"), b3["sty0"] + 32 * 16)
                g.wait_ge(s.h("sty1"), b3["sty1"] + 32 * 16)
                if upto < 3:
                    return
                s.inc(g.collective_compute("ReduceScatter", OP.add, replica_groups=PAIRS,
                                           ins=[io["rs_in"].ap().opt()],
                                           outs=[io["rs_out"].ap().opt()]), "cc_rs", 1)
                g.wait_ge(s.h("cc_rs"), b3["cc_rs"] + 1)
                for c in range(NCH):
                    s.inc(g.dma_start(out=self.X[:, c, :],
                                      in_=io["rs_out"][c * 128:(c + 1) * 128, :],
                                      accum_op=OP.add), "rb_y", 16)

            @blk.tensor
            def _(t):
                t.wait_ge(s.h("aw"), b3["aw"] + 32)
                for q2 in range(8):
                    for dch in range(NCH):
                        kk = q2 * NCH + dch
                        if kk >= 2:
                            t.wait_ge(s.h("wo_ev"), b3["wo_ev"] + kk - 1)
                        for hp in range(4):
                            i = t.matmul(PS[6 + dch % 2][:, :], WOB[:, hp, dch, :],
                                         OTB[:, hp, q2, :], start=(hp == 0), stop=(hp == 3))
                        s.inc(i, "wod", 1)

            @blk.vector
            def _(v):
                for q2 in range(8):
                    for dch in range(NCH):
                        kk = q2 * NCH + dch
                        v.wait_ge(s.h("wod"), b3["wod"] + kk + 1)
                        if kk >= 2:
                            v.wait_ge(s.h(f"sty{kk % 2}"), b3[f"sty{kk % 2}"] + (kk // 2) * 16)
                        v.drain()
                        v.tensor_copy(YPS[:, dch % 2, :], PS[6 + dch % 2][:, :])
                        s.inc(v.drain(), "wo_ev", 1)
                if upto < 3:
                    return
                v.wait_ge(s.h("rb_y"), b3["rb_y"] + NCH * 16)
                for c in range(NCH):
                    s.inc(v.tensor_scalar(self.X[:, c, :], self.X[:, c, :],
                                          BOB[:, c:c + 1], None, OP.add), "xadd", 1)

            @blk.sync
            def _(sy):
                for q2 in range(8):
                    rr, cbase = RSQ[q2]
                    for dch in range(NCH):
                        kk = q2 * NCH + dch
                        sy.wait_ge(s.h("wo_ev"), b3["wo_ev"] + kk + 1)
                        row0 = rr * 1024 + dch * 128
                        s.inc(sy.dma_start(out=io["rs_in"][row0:row0 + 128, cbase:cbase + QC],
                                           in_=YPS[:, dch % 2, :]), f"sty{kk % 2}", 16)
        ctxW.close()
        ctxA.close()
        return "xadd"


    def ffn_call(self, io, cc):
        nc, s = self.nc, self.s
        with nc.Block() as blk:
            bsp = s.val("xld")

            @blk.sync
            def _(sy):
                for c in range(NCH):
                    s.inc(sy.dma_start(out=io["xspill"][c * 128:(c + 1) * 128, :],
                                       in_=self.X[:, c, :]), "xld", 16)
                sy.wait_ge(s.h("xld"), bsp + 128)
        self.topln(f"lnf{cc}", "xf")
        lctx = ExitStack()
        E = lctx.enter_context
        F = {}
        F["KF"] = E(nc.sbuf_tensor(f"KF{cc}", [128, 4, 1536], BF16))
        F["VF"] = E(nc.sbuf_tensor(f"VF{cc}", [128, 12, 8, 65], BF16))
        F["QF"] = E(nc.sbuf_tensor(f"QF{cc}", [128, 4, 1024], BF16))

        F["WSTF"] = E(nc.sbuf_tensor(f"WSTF{cc}", [128, 2, 8, 128], BF16))
        F["W1ST"] = E(nc.sbuf_tensor(f"W1S{cc}", [128, 2, 8, 256], BF16))
        F["PT"] = E(nc.sbuf_tensor(f"PTF{cc}", [128, 2, 2, QC], BF16))
        F["OTF"] = E(nc.sbuf_tensor(f"OTF{cc}", [128, 8, QC], BF16))
        F["XFH"] = E(nc.sbuf_tensor(f"XFH{cc}", [128, 2, 2, NCH], F32))
        F["QSQ"] = E(nc.sbuf_tensor(f"QSQ{cc}", [128, 2, QC], BF16))
        F["TMP2"] = E(nc.sbuf_tensor(f"TMP2{cc}", [128, QC], BF16))
        F["SEL2"] = E(nc.sbuf_tensor(f"SEL2{cc}", [128, 2], BF16))
        F["SEL2R"] = E(nc.sbuf_tensor(f"S2R{cc}", [2, 128], mybir.dt.float32r))
        F["QSK"] = E(nc.sbuf_tensor(f"QSK{cc}", [128, 2], F32))
        F["B1T"] = E(nc.sbuf_tensor(f"B1T{cc}", [128, 2, 22], F32))
        F["B2F"] = E(nc.sbuf_tensor(f"B2F{cc}", [128, NCH], F32))
        F["AMF"] = E(nc.sbuf_tensor(f"AMF{cc}", [128, 2, 128], BF16))
        F["ROTP"] = E(nc.sbuf_tensor(f"ROTP{cc}", [128, 128], BF16))
        F["FLG"] = E(nc.sbuf_tensor(f"FLGF{cc}", [128, 2], F32))
        with nc.Block() as blk:
            bi = s.val("aw")

            @blk.gpsimd
            def _(g):
                s.inc(g.dma_start(out=F["SEL2"][:], in_=io["sel2c"][:, :]), "aw", 16)
                s.inc(g.dma_start(out=self.ROWS[0:2, 0, 0:128], in_=io["sel2r"][0:2, :]), "aw", 16)
                s.inc(g.dma_start(out=F["AMF"][:], in_=io["amf"][:, :]), "aw", 16)
                s.inc(g.dma_start(out=F["ROTP"][:], in_=io["rotp"][:, :]), "aw", 16)
                s.inc(g.dma_start(out=F["FLG"][:], in_=io["hflag"][:, :]), "aw", 16)
                g.memset(F["VF"][:, :, :, 64:65], 1.0)
                s.inc(g.memset(F["OTF"][64:128, :, :], 0.0), "awc", 1)

            @blk.vector
            def _(v):
                v.memset(self.ROWS[32:33, 0, :], 1.0)
                v.wait_ge(s.h("aw"), bi + 80)
                v.drain()
                v.tensor_copy(F["SEL2R"][:], self.ROWS[0:2, 0, 0:128])
                s.inc(v.tensor_copy(self.R32[32:33, 6, :], self.ROWS[32:33, 0, :]), "awc", 1)
        for l in range(2):
            self.ffn_layer(io, F, cc, l)
        with nc.Block() as blk:
            brb = s.val("rb_y")

            @blk.gpsimd
            def _(g):
                for c in range(NCH):
                    s.inc(g.dma_start(out=self.X[:, c, :],
                                      in_=io["xspill"][c * 128:(c + 1) * 128, :],
                                      accum_op=OP.add), "rb_y", 16)
                g.wait_ge(s.h("rb_y"), brb + NCH * 16)
                s.inc(g.nop(), "xadd2", 1)
        lctx.close()
        return "xadd2"

    def ffn_layer(self, io, F, cc, l):
        nc, s = self.nc, self.s
        R, R32 = self.ROWS, self.R32
        XF = self.X
        with nc.Block() as blk:
            bb = {k: s.val(k) for k in ("stq0 fsl cc_agf rb_q lnabr strow").split()}

            @blk.vector
            def _(v):
                v.tensor_copy(R[:, 0, 0:NCH], XF[:, :, 1023:1024])
                s.inc(v.tensor_copy(R[:, 0, NCH:2 * NCH], XF[:, :, 2047:2048]), "strow", 1)

            @blk.sync
            def _(sy):
                sy.wait_ge(s.h("strow"), bb["strow"] + 1)
                s.inc(sy.dma_start(out=bass.AP(io["agx_in"], 0, [[NCH, 128], [1, NCH]]),
                                   in_=R[:, 0, 0:NCH]), "stq0", 16)
                s.inc(sy.dma_start(out=bass.AP(io["agx_in"], 1024, [[NCH, 128], [1, NCH]]),
                                   in_=R[:, 0, NCH:2 * NCH]), "stq0", 16)

            @blk.gpsimd
            def _(g):
                s.inc(g.dma_start(out=F["QSK"][:], in_=io[f"f_qsk{l}"][:, :]), "fsl", 16)
                s.inc(g.dma_start(out=F["B1T"][:], in_=io[f"f_b1{l}"][:, :]), "fsl", 16)
                s.inc(g.dma_start(out=F["B2F"][:], in_=io[f"f_b2{l}"][:, :]), "fsl", 16)
                s.inc(g.dma_start(out=R[32:33, 2:6, :], in_=io[f"f_lnab{l}"][0:1, :]), "fsl", 16)
                g.wait_ge(s.h("stq0"), bb["stq0"] + 32)
                s.inc(g.collective_compute("AllGather", OP.bypass, replica_groups=PAIRS,
                                           ins=[io["agx_in"].ap().opt()],
                                           outs=[io["agx_out"].ap().opt()]), "cc_agf", 1)
                g.wait_ge(s.h("cc_agf"), bb["cc_agf"] + 1)
                s.inc(g.dma_start(out=F["XFH"][:, 0, 0, :],
                                  in_=bass.AP(io["agx_out"], 0, [[NCH, 128], [1, NCH]])), "rb_q", 16)
                s.inc(g.dma_start(out=F["XFH"][:, 1, 0, :],
                                  in_=bass.AP(io["agx_out"], 3 * 1024, [[NCH, 128], [1, NCH]])), "rb_q", 16)
                s.inc(g.dma_start(out=R[:, 1, 0:NCH],
                                  in_=bass.AP(io["agx_out"], 2 * 1024, [[NCH, 128], [1, NCH]])), "rb_q", 16)

            @blk.vector
            def _(v2):
                v2.wait_ge(s.h("fsl"), bb["fsl"] + 4 * 16)
                s.inc(v2.tensor_copy(R32[32:33, 0:4, :], R[32:33, 2:6, :]), "lnabr", 1)
                v2.wait_ge(s.h("rb_q"), bb["rb_q"] + 48)
                v2.tensor_scalar_mul(F["XFH"][:, 0, 0, :], F["XFH"][:, 0, 0, :], F["FLG"][:, 1:2])
                v2.tensor_scalar_mul(F["XFH"][:, 1, 0, :], F["XFH"][:, 1, 0, :], F["FLG"][:, 0:1])
                v2.drain()
                s.inc(v2.scalar_tensor_tensor(F["XFH"][:, 1, 0, :], R[:, 1, 0:NCH], F["FLG"][:, 1:2],
                                              F["XFH"][:, 1, 0, :], OP.mult, OP.add), "lnabr", 1)
                v2.drain()
        for bki in range(2):
            self.ffn_block(io, F, cc, l, bki)

    def ffn_block(self, io, F, cc, l, bki):
        nc, s = self.nc, self.s
        R, R32, PS, G = self.ROWS, self.R32, self.PS, self.G
        XF = self.X
        KF, VF, QF = F["KF"], F["VF"], F["QF"]
        b1024 = bki * 1024

        def XSH(c, t0, n):
            return self.BIGA[:, c * T + b1024 + t0: c * T + b1024 + t0 + n]

        wq = io[f"f_wqkv{l}"]
        agk_in = io["agka_in"] if bki == 0 else io["agkb_in"]
        agk_out = io["agka_out"] if bki == 0 else io["agkb_out"]
        agm_in = io["agm1_in"] if bki == 0 else io["agm2_in"]
        agm_out = io["agm1_out"] if bki == 0 else io["agm2_out"]

        ctxF = ExitStack()
        F["FTAB"] = ctxF.enter_context(nc.sbuf_tensor(f"FTAB{cc}_{l}_{bki}", [128, 2, 1024], BF16))
        F["QNN"] = ctxF.enter_context(nc.sbuf_tensor(f"QNN{cc}_{l}_{bki}", [128, 2, QC], BF16))
        F["QRAW"] = ctxF.enter_context(nc.sbuf_tensor(f"QRAW{cc}_{l}_{bki}", [128, 2, QC], BF16))
        F["LBCF"] = ctxF.enter_context(nc.sbuf_tensor(f"LBCFp{cc}_{l}_{bki}", [128, QC], BF16))
        with nc.Block() as blk:
            bb = {k: s.val(k) for k in ("awc ftb fwq mm_qk mm_v lbc ev_qk ev_v st_d "
                                        "exprm expm av_d lrow otd sq2 s2m").split()}

            @blk.gpsimd
            def _(g):
                for c in range(NCH):
                    if c < 4:
                        i = g.tensor_copy(XSH(c, 0, 1024), XF[:, c, b1024:b1024 + 1024])
                    else:
                        g.tensor_copy(XSH(c, 0, 1), F["XFH"][:, bki, 0, c:c + 1])
                        i = g.tensor_copy(XSH(c, 1, 1023), XF[:, c, b1024:b1024 + 1023])
                s.inc(i, "awc", 1)
                for kind in range(2):
                    g.wait_ge(s.h("ftb"), bb["ftb"] + kind * 16)
                    s.inc(g.dma_start(out=F["FTAB"][:, kind, :],
                                      in_=io["f_tab"][:, kind * T + b1024: kind * T + b1024 + 1024]),
                          "ftb", 16)
                for oc in range(8):
                    if oc == 4:
                        g.wait_ge(s.h("otd"), bb["otd"] + 8)
                        for kind in range(2):
                            g.wait_ge(s.h("ftb"), bb["ftb"] + (2 + kind) * 16)
                            s.inc(g.dma_start(out=F["FTAB"][:, kind, :],
                                              in_=io["f_tab"][:, (2 + kind) * T + b1024: (2 + kind) * T + b1024 + 1024]),
                                  "ftb", 16)
                    if oc >= 2:
                        g.wait_ge(s.h("mm_qk"), bb["mm_qk"] + (oc - 1) * 2)
                    g.wait_ge(s.h("fwq"), bb["fwq"] + oc * 16)
                    s.inc(g.dma_start(out=F["WSTF"][:, oc % 2, :, :],
                                      in_=bass.AP(wq, oc * 128, [[1536, 128], [1536 * 128, 8], [1, 128]])),
                          "fwq", 16)
                g.wait_ge(s.h("mm_qk"), bb["mm_qk"] + 16)
                for vb in range(2):
                    g.wait_ge(s.h("fwq"), bb["fwq"] + (8 + vb) * 16)
                    s.inc(g.dma_start(out=F["W1ST"][:, vb, :, :],
                                      in_=bass.AP(wq, 1024 + vb * 256, [[1536, 128], [1536 * 128, 8], [1, 256]])),
                          "fwq", 16)

            @blk.tensor
            def _(t):
                t.wait_ge(s.h("awc"), bb["awc"] + 1)
                t.wait_ge(s.h("lnabr"), s.val("lnabr"))
                for oc in range(8):
                    t.wait_ge(s.h("fwq"), bb["fwq"] + (oc + 1) * 16)
                    for qcb in range(2):
                        u = oc * 2 + qcb
                        if u >= 2:
                            t.wait_ge(s.h("exprm"), bb["exprm"] + u - 1)
                        for kc in range(NCH):
                            i = t.matmul(PS[6 + u % 2][:, :], F["WSTF"][:, oc % 2, kc, :],
                                         XSH(kc, qcb * QC, QC), start=(kc == 0), stop=(kc == NCH - 1))
                        s.inc(i, "mm_qk", 1)
                        t.wait_ge(s.h("s2m"), bb["s2m"] + u + 1)
                        if u >= 2:
                            t.wait_ge(s.h("expm"), bb["expm"] + u - 1)
                        s.inc(t.matmul(PS[u % 2][0:2, :], F["SEL2"][:, :], F["QSQ"][:, u % 2, :],
                                       start=True, stop=True), "mm_v", 1)
                        t.wait_ge(s.h("lrow"), bb["lrow"] + u + 1)
                        if u >= 2:
                            t.wait_ge(s.h("otd"), bb["otd"] + u - 1)
                        s.inc(t.matmul(PS[2 + u % 2][:, :], F["SEL2R"][:, :], R32[0:2, 0, :],
                                       start=True, stop=True), "lbc", 1)
                        t.wait_ge(s.h("ev_qk"), bb["ev_qk"] + u + 1)
                        s.inc(t.matmul(PS[2 + u % 2][:, :], F["ROTP"][:, :], F["QNN"][:, u % 2, :],
                                       start=True, stop=True), "ev_v", 1)
                t.wait_ge(s.h("fwq"), bb["fwq"] + 10 * 16)
                for uv in range(16):
                    tt, vs = uv // 2, uv % 2
                    if uv >= 2:
                        t.wait_ge(s.h("sq2"), bb["sq2"] + uv - 1)
                    for kc in range(NCH):
                        i = t.matmul(PS[4 + uv % 2][:, 0:256], XSH(kc, tt * 128, 128),
                                     F["W1ST"][:, vs, kc, :], start=(kc == 0), stop=(kc == NCH - 1))
                    s.inc(i, "st_d", 1)

            @blk.scalar
            def _(sc):
                for u in range(16):
                    sc.wait_ge(s.h("exprm"), bb["exprm"] + u + 1)
                    if u >= 2:
                        sc.wait_ge(s.h("mm_v"), bb["mm_v"] + u - 1)
                    s.inc(sc.activation(F["QSQ"][:, u % 2, :], F["QRAW"][:, u % 2, :], AF.Square),
                          "s2m", 1)
                    sc.wait_ge(s.h("expm"), bb["expm"] + u + 1)
                    sc.activation(R[0:2, 3, :], R[0:2, 3, :], AF.Ln, bias=self.EPS[0:2, :])
                    sc.drain()
                    s.inc(sc.activation(R[0:2, 4, :], R[0:2, 3, :], AF.Exp, scale=self.NHALF[0:2, :]),
                          "av_d", 1)

            @blk.vector
            def _(v):
                for u in range(16):
                    oc, qcb = u // 2, u % 2
                    isq = oc < 4
                    v.wait_ge(s.h("mm_qk"), bb["mm_qk"] + u + 1)
                    if u >= 2:
                        v.wait_ge(s.h("s2m"), bb["s2m"] + u - 1)
                    v.drain()
                    s.inc(v.tensor_copy(F["QRAW"][:, u % 2, :], PS[6 + u % 2][:, :]), "exprm", 1)
                    v.wait_ge(s.h("mm_v"), bb["mm_v"] + u + 1)
                    v.drain()
                    s.inc(v.tensor_copy(R[0:2, 3, :], PS[u % 2][0:2, :]), "expm", 1)
                    v.wait_ge(s.h("av_d"), bb["av_d"] + u + 1)
                    v.drain()
                    s.inc(v.tensor_copy(R32[0:2, 0, :], R[0:2, 4, :]), "lrow", 1)
                    v.wait_ge(s.h("lbc"), bb["lbc"] + u + 1)
                    if u >= 2:
                        v.wait_ge(s.h("ev_v"), bb["ev_v"] + u - 1)
                    s.inc(v.scalar_tensor_tensor(F["QNN"][:, u % 2, :], F["QRAW"][:, u % 2, :],
                                                 F["QSK"][:, (0 if isq else 1):(1 if isq else 2)],
                                                 PS[2 + u % 2][:, :], OP.mult, OP.mult), "ev_qk", 1)
                    v.wait_ge(s.h("ev_v"), bb["ev_v"] + u + 1)
                    v.wait_ge(s.h("ftb"), bb["ftb"] + (32 if isq else 64))
                    tabc = F["FTAB"][:, 0, qcb * QC:(qcb + 1) * QC]
                    tabs = F["FTAB"][:, 1, qcb * QC:(qcb + 1) * QC]
                    v.drain()
                    v.tensor_mul(F["TMP2"][:, :], F["QNN"][:, u % 2, :], tabc)
                    v.scalar_tensor_tensor(F["LBCF"][:, :], PS[2 + u % 2][:, :], 1.0,
                                           tabs, OP.mult, OP.mult)
                    v.drain()
                    if isq:
                        dst = QF[:, oc, qcb * QC:(qcb + 1) * QC]
                    else:
                        dst = KF[:, oc - 4, 512 + qcb * QC: 512 + (qcb + 1) * QC]
                    s.inc(v.tensor_add(dst, F["TMP2"][:, :], F["LBCF"][:, :]), "otd", 1)
                for uv in range(16):
                    tt, vs = uv // 2, uv % 2
                    v.wait_ge(s.h("st_d"), bb["st_d"] + uv + 1)
                    s.inc(v.tensor_copy(VF[:, 4 + tt, vs * 4:(vs + 1) * 4, 0:64],
                                        PS[4 + uv % 2][:, 0:256]), "sq2", 1)

        ctxF.close()
        with nc.Block() as blk:
            bb = {k: s.val(k) for k in ("star0 cc_agv rb_k").split()}

            @blk.sync
            def _(sy):
                s.inc(sy.dma_start(out=agk_in[0:128, 0:2048], in_=KF[:, :, 1024:1536]), "star0", 16)
                s.inc(sy.dma_start(out=bass.AP(agk_in, 2048, [[4128, 128], [520, 4], [65, 8], [1, 65]]),
                                   in_=VF[:, 8:12, :, :]), "star0", 16)
                if bki == 0:
                    s.inc(sy.dma_start(out=io["agkb_in"][128:256, 0:2048], in_=KF[:, :, 1024:1536]),
                          "star0", 16)
                    s.inc(sy.dma_start(out=bass.AP(io["agkb_in"], 128 * 4128 + 2048,
                                                   [[4128, 128], [520, 4], [65, 8], [1, 65]]),
                                       in_=VF[:, 8:12, :, :]), "star0", 16)

            @blk.gpsimd
            def _(g):
                g.wait_ge(s.h("star0"), bb["star0"] + (64 if bki == 0 else 32))
                s.inc(g.collective_compute("AllGather", OP.bypass, replica_groups=PAIRS,
                                           ins=[agk_in.ap().opt()], outs=[agk_out.ap().opt()]), "cc_agv", 1)
                g.wait_ge(s.h("cc_agv"), bb["cc_agv"] + 1)
                if bki == 0:
                    s.inc(g.dma_start(out=KF[:, :, 0:512],
                                      in_=bass.AP(agk_out, 0, [[4128, 128], [512, 4], [1, 512]])), "rb_k", 16)
                    s.inc(g.dma_start(out=VF[:, 0:4, :, :],
                                      in_=bass.AP(agk_out, 2048, [[4128, 128], [520, 4], [65, 8], [1, 65]])), "rb_k", 16)
                else:
                    s.inc(g.dma_start(out=KF[:, :, 0:512],
                                      in_=bass.AP(agk_out, 256 * 4128, [[4128, 128], [512, 4], [1, 512]])), "rb_k", 16)
                    s.inc(g.dma_start(out=VF[:, 0:4, :, :],
                                      in_=bass.AP(agk_out, 256 * 4128 + 2048, [[4128, 128], [520, 4], [65, 8], [1, 65]])), "rb_k", 16)
                    s.inc(g.dma_start(out=bass.AP(F["PT"], 0, [[2048, 128], [512, 4], [1, 512]]),
                                      in_=bass.AP(agk_out, 384 * 4128, [[4128, 128], [512, 4], [1, 512]])), "rb_k", 16)
                    s.inc(g.dma_start(out=bass.AP(F["W1ST"], 0, [[4096, 128], [520, 4], [65, 8], [1, 65]]),
                                      in_=bass.AP(agk_out, 384 * 4128 + 2048, [[4128, 128], [520, 4], [65, 8], [1, 65]])), "rb_k", 16)

            @blk.vector
            def _(v):
                n_dma = 2 if bki == 0 else 4
                v.wait_ge(s.h("rb_k"), bb["rb_k"] + n_dma * 16)
                fsel = F["FLG"][:, 1:2]
                if bki == 0:
                    v.tensor_scalar_mul(KF[:, :, 0:512], KF[:, :, 0:512], fsel)
                    v.drain()
                    s.inc(v.tensor_scalar_mul(VF[:, 0:4, :, :], VF[:, 0:4, :, :], fsel), "s2m", 1)
                    v.drain()
                else:
                    v.tensor_scalar_mul(KF[:, :, 0:512], KF[:, :, 0:512], F["FLG"][:, 0:1])
                    v.tensor_scalar_mul(VF[:, 0:4, :, :], VF[:, 0:4, :, :], F["FLG"][:, 0:1])
                    v.drain()
                    v.scalar_tensor_tensor(KF[:, :, 0:512],
                                           bass.AP(F["PT"], 0, [[2048, 128], [512, 4], [1, 512]]),
                                           fsel, KF[:, :, 0:512], OP.mult, OP.add)
                    v.drain()
                    s.inc(v.scalar_tensor_tensor(VF[:, 0:4, :, :],
                                                 bass.AP(F["W1ST"], 0, [[4096, 128], [520, 4], [65, 8], [1, 65]]),
                                                 fsel, VF[:, 0:4, :, :], OP.mult, OP.add), "s2m", 1)
                    v.drain()

        if "o_qf" in io and cc == 0 and l == 0 and bki == 0:
            with nc.Block() as blkd:
                @blkd.gpsimd
                def _(g):
                    bst = s.val("st")
                    s.inc(g.dma_start(out=io["o_qf"].ap()[:, :],
                                      in_=bass.AP(QF, 0, [[4 * 1024, 128], [1, 4 * 1024]])), "st", 16)
                    s.inc(g.dma_start(out=io["o_kf"].ap()[:, :],
                                      in_=bass.AP(KF, 0, [[4 * 1536, 128], [1, 4 * 1536]])), "st", 16)
                    s.inc(g.dma_start(out=io["o_vf"].ap()[:, :],
                                      in_=bass.AP(VF, 0, [[12 * 8 * 65, 128], [1, 12 * 8 * 65]])), "st", 16)
                    g.wait_ge(s.h("st"), bst + 3 * 16)

        ctxW2 = ExitStack()
        F["WOF"] = ctxW2.enter_context(nc.sbuf_tensor(f"WOF{cc}_{l}_{bki}", [64, 8, 8, 128], BF16))
        F["LBCF"] = ctxW2.enter_context(nc.sbuf_tensor(f"LBCF{cc}_{l}_{bki}", [128, QC], BF16))
        with nc.Block() as blk:
            bq = {k: s.val(k) for k in ("st_d expm exprm av_d lrow lbc otd wod b67r apm s2m aw").split()}

            def live_i(j):
                return [i for i in range(4) if (j < 4 and j >= i) or (j >= 4 and i >= j - 4)]

            @blk.gpsimd
            def _(g):
                s.inc(g.dma_start(out=F["WOF"][0:64, :, :, :], in_=io[f"f_wo{l}"][0:64, :]),
                      "aw", 16)

            @blk.tensor
            def _(t):
                t.wait_ge(s.h("s2m"), bq["s2m"])
                nj = 0
                nf67 = 0
                for w in range(2):
                    for hp in range(4):
                        for j in range(8):
                            p2 = nj % 2
                            if nj >= 2:
                                t.wait_ge(s.h("expm"), bq["expm"] + nj - 1)
                            t.matmul(PS[2 * p2][:, :],
                                     KF[0:64, hp, (w * 4 + j) * 128:(w * 4 + j + 1) * 128],
                                     QF[0:64, hp, w * QC: (w + 1) * QC],
                                     start=True, stop=True, tile_position=(0, 0))
                            i2 = t.matmul(PS[2 * p2 + 1][:, :],
                                          KF[64:128, hp, (w * 4 + j) * 128:(w * 4 + j + 1) * 128],
                                          QF[64:128, hp, w * QC: (w + 1) * QC],
                                          start=True, stop=True, tile_position=(64, 0))
                            s.inc(i2, "st_d", 1)
                            for jj in ([j - 1] if j >= 1 else []) + ([j] if j == 7 else []):
                                pj = (nj - (1 if jj == j - 1 else 0)) % 2
                                t.wait_ge(s.h("expm"), bq["expm"] + nj + (1 if jj == j else 0))
                                if jj == 0:
                                    t.wait_ge(s.h("otd"), bq["otd"] + 2 * (w * 4 + hp))
                                for lh in range(2):
                                    i2 = t.matmul(PS[4 + lh][0:65, :], VF[:, w * 4 + jj, hp * 2 + lh, :],
                                                  F["PT"][:, pj, lh, :], start=(jj == 0), stop=(jj == 7))
                                s.inc(i2, "av_d", 1)
                            nj += 1
                        t.wait_ge(s.h("lrow"), bq["lrow"] + w * 4 + hp + 1)
                        t.wait_ge(s.h("b67r"), bq["b67r"] + nf67)
                        nf67 += 1
                        t.matmul(PS[6][0:64, :], R32[64:65, 2, 0:64], R32[64:65, 0, :],
                                 start=True, stop=True)
                        s.inc(t.matmul(PS[7][0:64, :], R32[64:65, 2, 0:64], R32[64:65, 1, :],
                                       start=True, stop=True), "lbc", 1)
                    t.wait_ge(s.h("otd"), bq["otd"] + (w + 1) * 8)
                    t.wait_ge(s.h("aw"), bq["aw"] + 16)
                    for dch in range(NCH):
                        t.wait_ge(s.h("b67r"), bq["b67r"] + nf67)
                        nf67 += 1
                        for m in range(8):
                            i2 = t.matmul(PS[6 + dch % 2][:, :], F["WOF"][0:64, m, dch, :],
                                          F["OTF"][0:64, m, :], start=(m == 0), stop=(m == 7))
                        s.inc(i2, "wod", 1)

            @blk.scalar
            def _(sc):
                nj = 0
                for w in range(2):
                    for hp in range(4):
                        for j in range(8):
                            p2 = nj % 2
                            sc.wait_ge(s.h("st_d"), bq["st_d"] + nj + 1)
                            if nj >= 2:
                                sc.wait_ge(s.h("av_d"), bq["av_d"] + nj - 1)
                            s.inc(sc.activation(F["PT"][:, p2, :, :], G[p2], AF.Exp), "exprm", 1)
                            nj += 1

            @blk.vector
            def _(v):
                nj = 0
                for w in range(2):
                    for hp in range(4):
                        for j in range(8):
                            p2 = nj % 2
                            v.wait_ge(s.h("exprm"), bq["exprm"] + nj + 1)
                            for lh in range(2):
                                if j < 4:
                                    v.tensor_mul(F["PT"][:, p2, lh, j * 128:(j + 1) * 128],
                                                 F["PT"][:, p2, lh, j * 128:(j + 1) * 128],
                                                 F["AMF"][:, 1, :])
                                    if j < 3:
                                        v.memset(F["PT"][:, p2, lh, (j + 1) * 128:512], 0.0)
                                else:
                                    jd = j - 4
                                    v.tensor_mul(F["PT"][:, p2, lh, jd * 128:(jd + 1) * 128],
                                                 F["PT"][:, p2, lh, jd * 128:(jd + 1) * 128],
                                                 F["AMF"][:, 0, :])
                                    if jd > 0:
                                        v.memset(F["PT"][:, p2, lh, 0:jd * 128], 0.0)
                            s.inc(v.drain(), "expm", 1)
                            nj += 1
                        v.wait_ge(s.h("av_d"), bq["av_d"] + (w * 4 + hp + 1) * 8)
                        v.tensor_copy(R[64:65, 0, :], PS[4][64:65, :])
                        v.tensor_copy(R[64:65, 1, :], PS[5][64:65, :])
                        v.drain()
                        v.reciprocal(R[64:65, 2, :], R[64:65, 0, :])
                        v.reciprocal(R[64:65, 3, :], R[64:65, 1, :])
                        v.drain()
                        v.tensor_copy(R32[64:65, 0, :], R[64:65, 2, :])
                        s.inc(v.tensor_copy(R32[64:65, 1, :], R[64:65, 3, :]), "lrow", 1)
                        v.wait_ge(s.h("lbc"), bq["lbc"] + w * 4 + hp + 1)
                        if w == 1 and hp == 0:
                            v.wait_ge(s.h("wod"), bq["wod"] + 8)
                        v.tensor_copy(F["LBCF"][0:64, :], PS[6][0:64, :])
                        v.drain()
                        s.inc(v.scalar_tensor_tensor(F["OTF"][0:64, hp * 2, :],
                                                     PS[4][0:64, :], 1.0,
                                                     F["LBCF"][0:64, :], OP.mult, OP.mult), "otd", 1)
                        v.drain()
                        v.tensor_copy(F["LBCF"][0:64, :], PS[7][0:64, :])
                        s.inc(v.drain(), "b67r", 1)
                        s.inc(v.scalar_tensor_tensor(F["OTF"][0:64, hp * 2 + 1, :],
                                                     PS[5][0:64, :], 1.0,
                                                     F["LBCF"][0:64, :], OP.mult, OP.mult), "otd", 1)
                    for dch in range(NCH):
                        v.wait_ge(s.h("wod"), bq["wod"] + w * 8 + dch + 1)
                        v.scalar_tensor_tensor(XF[:, dch, b1024 + w * QC: b1024 + (w + 1) * QC],
                                                    PS[6 + dch % 2][:, :], 1.0,
                                                    XF[:, dch, b1024 + w * QC: b1024 + (w + 1) * QC],
                                                    OP.mult, OP.add)
                        s.inc(v.drain(), "b67r", 1)
                        s.inc(v.drain(), "apm", 1)

        if "o_xa" in io and cc == 0 and l == 0 and bki == 0:
            with nc.Block() as blkd:
                @blkd.gpsimd
                def _(g):
                    bst = s.val("st")
                    for c in range(NCH):
                        s.inc(g.dma_start(out=io["o_xa"].ap()[c * 128:(c + 1) * 128, :],
                                          in_=XF[:, c, 0:1024]), "st", 16)
                    g.wait_ge(s.h("st"), bst + 8 * 16)

        ctxW2.close()
        with nc.Block() as blk:
            bb = {k: s.val(k) for k in ("star1 cc_ar rb_v strow").split()}

            @blk.vector
            def _(v):
                s.inc(v.tensor_copy(R[:, 0, 0:NCH], XF[:, :, b1024 + 1023:b1024 + 1024]), "strow", 1)

            @blk.sync
            def _(sy):
                sy.wait_ge(s.h("strow"), bb["strow"] + 1)
                s.inc(sy.dma_start(out=bass.AP(agm_in, 0, [[NCH, 128], [1, NCH]]),
                                   in_=R[:, 0, 0:NCH]), "star1", 16)
                if bki == 0:
                    s.inc(sy.dma_start(out=bass.AP(io["agm2_in"], 1024, [[NCH, 128], [1, NCH]]),
                                       in_=R[:, 0, 0:NCH]), "star1", 16)

            @blk.gpsimd
            def _(g):
                g.wait_ge(s.h("star1"), bb["star1"] + (32 if bki == 0 else 16))
                s.inc(g.collective_compute("AllGather", OP.bypass, replica_groups=PAIRS,
                                           ins=[agm_in.ap().opt()], outs=[agm_out.ap().opt()]), "cc_ar", 1)
                g.wait_ge(s.h("cc_ar"), bb["cc_ar"] + 1)
                if bki == 0:
                    s.inc(g.dma_start(out=F["XFH"][:, 0, 1, :],
                                      in_=bass.AP(agm_out, 0, [[NCH, 128], [1, NCH]])), "rb_v", 16)
                else:
                    s.inc(g.dma_start(out=F["XFH"][:, 1, 1, :],
                                      in_=bass.AP(agm_out, 2 * 1024, [[NCH, 128], [1, NCH]])), "rb_v", 16)
                    s.inc(g.dma_start(out=R[:, 1, 0:NCH],
                                      in_=bass.AP(agm_out, 3 * 1024, [[NCH, 128], [1, NCH]])), "rb_v", 16)

            @blk.vector
            def _(v):
                n_dma = 1 if bki == 0 else 2
                v.wait_ge(s.h("rb_v"), bb["rb_v"] + n_dma * 16)
                if bki == 0:
                    s.inc(v.tensor_scalar_mul(F["XFH"][:, 0, 1, :], F["XFH"][:, 0, 1, :],
                                              F["FLG"][:, 1:2]), "strow", 1)
                    v.drain()
                else:
                    v.tensor_scalar_mul(F["XFH"][:, 1, 1, :], F["XFH"][:, 1, 1, :], F["FLG"][:, 0:1])
                    v.drain()
                    s.inc(v.scalar_tensor_tensor(F["XFH"][:, 1, 1, :], R[:, 1, 0:NCH], F["FLG"][:, 1:2],
                                                 F["XFH"][:, 1, 1, :], OP.mult, OP.add), "strow", 1)
                    v.drain()

        ctxH = ExitStack()
        F["H2"] = ctxH.enter_context(nc.sbuf_tensor(f"H2{cc}_{l}_{bki}", [128, 22, 512], BF16))
        with nc.Block() as blk:
            bm = {k: s.val(k) for k in ("awc sq2 ssum arrow bcap apm lrow mm_qk mm_v st_d "
                                        "fw1s fw2s xadd rb_v").split()}

            @blk.gpsimd
            def _(g):
                g.wait_ge(s.h("strow"), s.val("strow"))
                for c in range(NCH):
                    if c < 4:
                        i = g.tensor_copy(XSH(c, 0, 1024), XF[:, c, b1024:b1024 + 1024])
                    else:
                        g.tensor_copy(XSH(c, 0, 1), F["XFH"][:, bki, 1, c:c + 1])
                        i = g.tensor_copy(XSH(c, 1, 1023), XF[:, c, b1024:b1024 + 1023])
                s.inc(i, "awc", 1)
                nsl = 0
                for qc2 in range(2):
                    for p in range(22):
                        gidx = qc2 * 22 + p
                        if gidx >= 2:
                            g.wait_ge(s.h("mm_qk"), bm["mm_qk"] + (gidx - 1) * 2)
                        g.wait_ge(s.h("fw1s"), bm["fw1s"] + gidx * 16)
                        s.inc(g.dma_start(out=F["W1ST"][:, p % 2, :, :],
                                          in_=bass.AP(io[f"f_w1{l}"], p * 256, [[5632, 128], [5632 * 128, 8], [1, 256]])),
                              "fw1s", 16)
                    for pas in range(2):
                        for kc in range(22):
                            if nsl >= 2:
                                g.wait_ge(s.h("st_d"), bm["st_d"] + nsl - 1)
                            g.wait_ge(s.h("fw2s"), bm["fw2s"] + nsl * 16)
                            s.inc(g.dma_start(out=F["WSTF"][:, nsl % 2, 0:4, :],
                                              in_=bass.AP(io[f"f_w2{l}"], kc * 128 * 1024 + pas * 512,
                                                          [[1024, 128], [128, 4], [1, 128]])),
                                  "fw2s", 16)
                            nsl += 1

            @blk.tensor
            def _(t):
                t.wait_ge(s.h("awc"), bm["awc"] + 1)
                for qc in range(2):
                    if qc == 1:
                        t.wait_ge(s.h("apm"), bm["apm"] + NCH)
                    for c in range(NCH):
                        i = t.matmul(PS[6][32:33, :], self.ONECB[:], XSH(c, qc * QC, QC),
                                     start=(c == 0), stop=(c == NCH - 1))
                    s.inc(i, "ssum", 1)
                    for rnd in range(2):
                        t.wait_ge(s.h("sq2"), bm["sq2"] + qc * 2 + rnd + 1)
                        for c4 in range(4):
                            i = t.matmul(PS[7][32:33, :], self.ONECB[:],
                                         F["PT"][:, c4 // 2, c4 % 2, :],
                                         start=(rnd == 0 and c4 == 0), stop=(rnd == 1 and c4 == 3))
                        s.inc(i, "ssum", 1)
                    t.wait_ge(s.h("arrow"), bm["arrow"] + 3 * qc + 3)
                    for c in range(NCH):
                        lgc = R32[32:33, 0 + c // 4, (c % 4) * 128:(c % 4) * 128 + 128]
                        lbc_ = R32[32:33, 2 + c // 4, (c % 4) * 128:(c % 4) * 128 + 128]
                        if qc * NCH + c >= 1:
                            t.wait_ge(s.h("apm"), bm["apm"] + qc * NCH + c)
                        t.matmul(PS[6][0:128, :], lgc, R32[32:33, 4, :], start=True, stop=True)
                        t.matmul(PS[7][0:128, :], lgc, R32[32:33, 5, :], start=True, stop=False)
                        s.inc(t.matmul(PS[7][0:128, :], lbc_, R32[32:33, 6, 0:QC],
                                       start=False, stop=True), "bcap", 1)
                for qc2 in range(2):
                    t.wait_ge(s.h("apm"), bm["apm"] + (qc2 + 1) * NCH)
                    for p in range(22):
                        gidx = qc2 * 22 + p
                        t.wait_ge(s.h("fw1s"), bm["fw1s"] + (gidx + 1) * 16)
                        if gidx >= 1:
                            t.wait_ge(s.h("lrow"), bm["lrow"] + gidx - 1)
                        for kc in range(NCH):
                            i = t.matmul(PS[0 + p % 2][:, :], F["W1ST"][:, p % 2, kc, 0:128],
                                         XSH(kc, qc2 * 512, 512), start=(kc == 0), stop=(kc == NCH - 1))
                        s.inc(i, "mm_qk", 1)
                        for kc in range(NCH):
                            i = t.matmul(PS[2 + p % 2][:, :], F["W1ST"][:, p % 2, kc, 128:256],
                                         XSH(kc, qc2 * 512, 512), start=(kc == 0), stop=(kc == NCH - 1))
                        s.inc(i, "mm_qk", 1)
                    if qc2 == 0:
                        t.wait_ge(s.h("apm"), bm["apm"] + 16)
                    for pas in range(2):
                        for kc in range(22):
                            nsl = qc2 * 44 + pas * 22 + kc
                            t.wait_ge(s.h("fw2s"), bm["fw2s"] + (nsl + 1) * 16)
                            t.wait_ge(s.h("lrow"), bm["lrow"] + qc2 * 22 + kc + 1)
                            if kc == 0:
                                if pas == 0 and qc2 >= 1:
                                    t.wait_ge(s.h("xadd"), bm["xadd"] + qc2 * 8)
                                if pas == 1:
                                    t.wait_ge(s.h("xadd"), bm["xadd"] + qc2 * 8 + 4)
                            for dc in range(4):
                                i = t.matmul(PS[4 + dc][:, :],
                                             F["WSTF"][:, nsl % 2, dc, :],
                                             F["H2"][:, kc, :], start=(kc == 0), stop=(kc == 21))
                            s.inc(i, "st_d", 1)

            @blk.scalar
            def _(sc):
                sc.wait_ge(s.h("awc"), bm["awc"] + 1)
                for qc in range(2):
                    for rnd in range(2):
                        ridx = qc * 2 + rnd
                        if ridx >= 1:
                            prev = ridx - 1
                            sc.wait_ge(s.h("ssum"), bm["ssum"] + 3 * (prev // 2) + 2 + (prev % 2))
                        i = sc.activation(F["PT"][:, :, :, :],
                                          bass.AP(self.BIGA, b1024 + rnd * 4 * T + qc * QC,
                                                  [[16384, 128], [T, 4], [1, QC]]),
                                          AF.Square)
                        s.inc(i, "sq2", 1)
                    sc.wait_ge(s.h("arrow"), bm["arrow"] + 3 * qc + 1)
                    sc.activation(R[32:33, 3, :], R[32:33, 3, :], AF.Ln, bias=self.EPS[32:33, :])
                    sc.drain()
                    s.inc(sc.activation(R[32:33, 4, :], R[32:33, 3, :], AF.Exp,
                                        scale=self.NHALF[32:33, :]), "arrow", 1)
                for qc2 in range(2):
                    for p in range(22):
                        gidx = qc2 * 22 + p
                        sc.wait_ge(s.h("mm_qk"), bm["mm_qk"] + (gidx + 1) * 2)
                        if gidx >= 2:
                            sc.wait_ge(s.h("lrow"), bm["lrow"] + gidx - 1)
                        s.inc(sc.activation(F["QSQ"][:, gidx % 2, :], PS[2 + p % 2][:, :], AF.Gelu,
                                            bias=F["B1T"][:, 1, p:p + 1]), "mm_v", 1)

            @blk.vector
            def _(v):
                for qc in range(2):
                    v.wait_ge(s.h("ssum"), bm["ssum"] + 3 * qc + 3)
                    v.tensor_scalar_mul(R[32:33, 2, :], PS[6][32:33, :], 1.0 / D)
                    v.tensor_scalar_mul(R[32:33, 3, :], PS[7][32:33, :], 1.0 / D)
                    v.drain()
                    v.tensor_tensor(self.TMP[32:33, 0, 0:QC], R[32:33, 2, :], R[32:33, 2, :], OP.mult)
                    v.drain()
                    s.inc(v.tensor_sub(R[32:33, 3, :], R[32:33, 3, :], self.TMP[32:33, 0, 0:QC]),
                          "arrow", 1)
                    v.wait_ge(s.h("arrow"), bm["arrow"] + 3 * qc + 2)
                    v.tensor_mul(R[32:33, 5, :], R[32:33, 2, :], R[32:33, 4, :])
                    v.drain()
                    v.tensor_scalar_mul(R[32:33, 5, :], R[32:33, 5, :], -1.0)
                    v.drain()
                    v.tensor_copy(R32[32:33, 4, :], R[32:33, 4, :])
                    s.inc(v.tensor_copy(R32[32:33, 5, :], R[32:33, 5, :]), "arrow", 1)
                    for c in range(NCH):
                        v.wait_ge(s.h("bcap"), bm["bcap"] + qc * NCH + c + 1)
                        v.tensor_tensor(F["TMP2"][:, :], XSH(c, qc * QC, QC), PS[6][:, :], OP.mult)
                        v.drain()
                        i = v.tensor_add(XSH(c, qc * QC, QC), F["TMP2"][:, :], PS[7][:, :])
                        s.inc(i, "apm", 1)
                        v.drain()
                for qc2 in range(2):
                    for p in range(22):
                        gidx = qc2 * 22 + p
                        v.wait_ge(s.h("mm_v"), bm["mm_v"] + gidx + 1)
                        if qc2 >= 1:
                            v.wait_ge(s.h("st_d"), bm["st_d"] + (qc2 - 1) * 44 + 22 + p + 1)
                        i = v.scalar_tensor_tensor(F["H2"][:, p, :], PS[0 + p % 2][:, :],
                                                   F["B1T"][:, 0, p:p + 1], F["QSQ"][:, gidx % 2, :],
                                                   OP.add, OP.mult)
                        s.inc(i, "lrow", 1)
                    for pas in range(2):
                        for dc in range(4):
                            dch = pas * 4 + dc
                            v.wait_ge(s.h("st_d"), bm["st_d"] + qc2 * 44 + pas * 22 + 22)
                            i = v.scalar_tensor_tensor(XF[:, dch, b1024 + qc2 * 512: b1024 + (qc2 + 1) * 512],
                                                       PS[4 + dc][:, :],
                                                       F["B2F"][:, dch:dch + 1],
                                                       XF[:, dch, b1024 + qc2 * 512: b1024 + (qc2 + 1) * 512],
                                                       OP.add, OP.add)
                            s.inc(i, "xadd", 1)
                            v.drain()
        ctxH.close()
        if "o_xa" in io and cc == 0 and l == 0 and bki == 0:
            with nc.Block() as blkd:
                @blkd.gpsimd
                def _(g):
                    bst = s.val("st")
                    for c in range(NCH):
                        s.inc(g.dma_start(out=io["o_xm"].ap()[c * 128:(c + 1) * 128, :],
                                          in_=XF[:, c, 0:1024]), "st", 16)
                        s.inc(g.dma_start(out=io["o_xn2"].ap()[c * 128:(c + 1) * 128, :],
                                          in_=XSH(c, 0, 1024)), "st", 16)
                    g.wait_ge(s.h("st"), bst + 16 * 16)

    def dump_xn(self, o_xn, o_xf, ap_xn, ap_xf):
        nc, s = self.nc, self.s
        with nc.Block() as blk:
            @blk.gpsimd
            def _(g):
                s.wait(g, ap_xn)
                s.wait(g, ap_xf)
                for c in range(NCH):
                    s.inc(g.dma_start(out=o_xn[c * 128:(c + 1) * 128, :],
                                      in_=self.XN(c, 0, T)), "st", 16)
                    s.inc(g.dma_start(out=o_xf[c * 128:(c + 1) * 128, :],
                                      in_=self.X[:, c, :]), "st", 16)
                s.wait(g, "st")


def build(dbg="x1", races=True):
    nc = bass.Bass("TRN2", target_bir_lowering=False, detect_race_conditions=races)
    x_fm = nc.dram_tensor("x_fm", [D, T], F32, kind="ExternalInput").ap()
    io = {}
    io["w_a"] = nc.dram_tensor("w_a", [D, 1536], F32, kind="ExternalInput")
    io["b_a"] = nc.dram_tensor("b_a", [128, 10], F32, kind="ExternalInput")
    io["bv_row"] = nc.dram_tensor("bv_row", [1, 256], F32, kind="ExternalInput")
    io["amask"] = nc.dram_tensor("amask", [128, 4 * QC], F32, kind="ExternalInput")
    io["hflag"] = nc.dram_tensor("hflag", [128, 2], F32, kind="ExternalInput")
    io["lnab_a"] = nc.dram_tensor("lnab_a", [1, 1024], F32, kind="ExternalInput")
    io["wo_a"] = nc.dram_tensor("wo_a", [64, 8192], F32, kind="ExternalInput")
    io["bo_fm"] = nc.dram_tensor("bo_fm", [128, NCH], F32, kind="ExternalInput")
    io["agin_f"] = nc.dram_tensor("agin_f", [1280, T], BF16)
    io["agout_f"] = nc.dram_tensor("agout_f", [2560, T], BF16)
    io["agin_v"] = nc.dram_tensor("agin_v", [NT, 128], BF16)
    io["agout_v"] = nc.dram_tensor("agout_v", [2 * NT, 128], BF16)
    io["ar_in"] = [nc.dram_tensor(f"ar_in{g}", [2, 1024], F32) for g in range(4)]
    io["ar_out"] = [nc.dram_tensor(f"ar_out{g}", [2, 1024], F32) for g in range(4)]
    io["rs_in"] = nc.dram_tensor("rs_in", [2048, T], F32)
    io["rs_out"] = nc.dram_tensor("rs_out", [1024, T], F32)
    for l in range(2):
        io[f"f_wqkv{l}"] = nc.dram_tensor(f"f_wqkv{l}", [D, 1536], F32, kind="ExternalInput")
        io[f"f_qsk{l}"] = nc.dram_tensor(f"f_qsk{l}", [128, 2], F32, kind="ExternalInput")
        io[f"f_b1{l}"] = nc.dram_tensor(f"f_b1{l}", [128, 44], F32, kind="ExternalInput")
        io[f"f_b2{l}"] = nc.dram_tensor(f"f_b2{l}", [128, NCH], F32, kind="ExternalInput")
        io[f"f_lnab{l}"] = nc.dram_tensor(f"f_lnab{l}", [1, 2048], F32, kind="ExternalInput")
        io[f"f_wo{l}"] = nc.dram_tensor(f"f_wo{l}", [64, 8192], F32, kind="ExternalInput")
        io[f"f_w1{l}"] = nc.dram_tensor(f"f_w1{l}", [D, 5632], F32, kind="ExternalInput")
        io[f"f_w2{l}"] = nc.dram_tensor(f"f_w2{l}", [2816, D], F32, kind="ExternalInput")
    io["f_tab"] = nc.dram_tensor("f_tab", [128, 8192], F32, kind="ExternalInput")
    io["sel2c"] = nc.dram_tensor("sel2c", [128, 2], F32, kind="ExternalInput")
    io["sel2r"] = nc.dram_tensor("sel2r", [2, 128], F32, kind="ExternalInput")
    io["amf"] = nc.dram_tensor("amf", [128, 256], F32, kind="ExternalInput")
    io["rotp"] = nc.dram_tensor("rotp", [128, 128], F32, kind="ExternalInput")
    io["xspill"] = nc.dram_tensor("xspill", [D, T], F32)
    io["agx_in"] = nc.dram_tensor("agx_in", [2, 1024], F32)
    io["agx_out"] = nc.dram_tensor("agx_out", [4, 1024], F32)
    io["agka_in"] = nc.dram_tensor("agka_in", [128, 4128], BF16)
    io["agka_out"] = nc.dram_tensor("agka_out", [256, 4128], BF16)
    io["agkb_in"] = nc.dram_tensor("agkb_in", [256, 4128], BF16)
    io["agkb_out"] = nc.dram_tensor("agkb_out", [512, 4128], BF16)
    io["agm1_in"] = nc.dram_tensor("agm1_in", [1, 1024], F32)
    io["agm1_out"] = nc.dram_tensor("agm1_out", [2, 1024], F32)
    io["agm2_in"] = nc.dram_tensor("agm2_in", [2, 1024], F32)
    io["agm2_out"] = nc.dram_tensor("agm2_out", [4, 1024], F32)
    io["w_b"] = nc.dram_tensor("w_b", [D, 3072], F32, kind="ExternalInput")
    io["b_bq"] = nc.dram_tensor("b_bq", [128, 16], F32, kind="ExternalInput")
    io["bvr_b"] = nc.dram_tensor("bvr_b", [1, 1024], F32, kind="ExternalInput")
    io["b_tab"] = nc.dram_tensor("b_tab", [128, 2 * T], F32, kind="ExternalInput")
    io["wo_b"] = nc.dram_tensor("wo_b", [128, 4096], F32, kind="ExternalInput")
    io["bo_b"] = nc.dram_tensor("bo_b", [128, NCH], F32, kind="ExternalInput")
    io["eye64"] = nc.dram_tensor("eye64", [64, 64], F32, kind="ExternalInput")
    io["agin_fb"] = nc.dram_tensor("agin_fb", [2048, T], BF16)
    io["agout_fb"] = nc.dram_tensor("agout_fb", [4096, T], BF16)
    io["agin_vb"] = nc.dram_tensor("agin_vb", [16384, 128], BF16)
    io["agout_vb"] = nc.dram_tensor("agout_vb", [32768, 128], BF16)
    o_y = nc.dram_tensor("o_y", [D, T], F32, kind="ExternalOutput").ap()
    if dbg == "kqv":
        io["o_ks"] = nc.dram_tensor("o_ks", [128, NT], F32, kind="ExternalOutput").ap()
        io["o_qs"] = nc.dram_tensor("o_qs", [512, NT], F32, kind="ExternalOutput").ap()
        io["o_vs"] = nc.dram_tensor("o_vs", [128, 32 * 130], F32, kind="ExternalOutput").ap()
        io["o_xn"] = nc.dram_tensor("o_xn", [D, T], F32, kind="ExternalOutput").ap()
        io["o_xf"] = nc.dram_tensor("o_xf", [D, T], F32, kind="ExternalOutput").ap()
        io["o_lbc"] = nc.dram_tensor("o_lbc", [64, 2 * QC], F32, kind="ExternalOutput")
    if dbg == "fkv":
        io["o_qf"] = nc.dram_tensor("o_qf", [128, 4096], F32, kind="ExternalOutput")
        io["o_xa"] = nc.dram_tensor("o_xa", [D, 1024], F32, kind="ExternalOutput")
        io["o_xm"] = nc.dram_tensor("o_xm", [D, 1024], F32, kind="ExternalOutput")
        io["o_xn2"] = nc.dram_tensor("o_xn2", [D, 1024], F32, kind="ExternalOutput")
        io["o_h2"] = nc.dram_tensor("o_h2", [128, 22 * 256], F32, kind="ExternalOutput")
        io["o_kf"] = nc.dram_tensor("o_kf", [128, 6144], F32, kind="ExternalOutput")
        io["o_vf"] = nc.dram_tensor("o_vf", [128, 6240], F32, kind="ExternalOutput")
        io["o_r32"] = nc.dram_tensor("o_r32", [1, 7 * QC], F32, kind="ExternalOutput")
        io["o_ot"] = nc.dram_tensor("o_ot", [64, 16 * QC], F32, kind="ExternalOutput")
    with ExitStack() as ctx:
        p = Prog(nc, ctx, dbg)
        p.p0_load(x_fm)
        a1 = p.topln("ln0", "xn")
        ln_base = p.s.val(a1) - NQC
        if dbg == "kqv":
            p.dump_xn(io["o_xn"], io["o_xf"], a1, "xld")
        done = p.attn_a(io, a1, ln_base)
        if dbg in ("x2", "full", "fkv", "x3"):
            done = p.ffn_call(io, 0)
        if dbg in ("x3", "full"):
            a3 = p.topln("ln2", "xn")
            done = p.attn_b(io, a3, p.s.val(a3) - NQC)
        if dbg == "full":
            done = p.ffn_call(io, 1)
        if dbg == "x3o":
            done = p.attn_b(io, a1, ln_base)
        if dbg in ("x3p1", "x3p2"):
            done = p.attn_b(io, a1, ln_base, upto=int(dbg[-1]))
        if dbg == "x3pw":
            done = p.attn_b(io, a1, ln_base, upto=25)
        s = p.s
        with nc.Block() as blk:
            @blk.sync
            def _(sy):
                s.wait(sy, done)
                for c in range(NCH):
                    s.inc(sy.dma_start(out=o_y[c * 128:(c + 1) * 128, :],
                                       in_=p.X[:, c, :]), "st_out", 16)
                s.wait(sy, "st_out")
    return nc


def host_shards(x):
    out = []
    for c in range(8):
        b, h = c // 2, c % 2
        blocks = [0, 3] if h == 0 else [1, 2]
        toks = np.concatenate([x[b, blk * 1024:(blk + 1) * 1024] for blk in blocks], axis=0)
        out.append(np.ascontiguousarray(toks.T.astype(np.float32)))
    return out


def host_attn_a(I):
    Wq, bq = I["L0_attn_a_Wq"], I["L0_attn_a_bq"]
    Wk, bk = I["L0_attn_a_Wk"], I["L0_attn_a_bk"]
    Wv, bv = I["L0_attn_a_Wv"], I["L0_attn_a_bv"]
    Wo, bo = I["L0_attn_a_Wo"], I["L0_attn_a_bo"]
    lng, lnb = I["L0_attn_a_ln_g"], I["L0_attn_a_ln_b"]
    qcols = []
    for r in range(2):
        for g in range(4):
            for lh in range(2):
                hk = 2 * r + lh
                qcols.append((g * 256 + hk * 64, g * 256 + hk * 64 + 64))
    w_q = np.concatenate([Wq[:, a:bb] for a, bb in qcols], axis=1)
    b_q = np.concatenate([bq[a:bb] for a, bb in qcols])
    w_a = np.ascontiguousarray(np.concatenate([w_q, Wk, Wv], axis=1).astype(np.float32))
    b_chunks = np.concatenate([b_q, bk])
    b_a = np.zeros((128, 10), np.float32)
    for ch in range(10):
        b_a[:, ch] = b_chunks[ch * 128:(ch + 1) * 128]
    bv_row = bv.reshape(1, 256).astype(np.float32)
    amask = np.zeros((128, 4, QC), np.float32)
    for k in range(4):
        for i in range(4):
            if i > k:
                amask[:, k, i * 128:(i + 1) * 128] = 1.0
            elif i == k:
                sl = np.arange(128)
                amask[:, k, i * 128:(i + 1) * 128] = (sl[None, :] >= sl[:, None]).astype(np.float32)
    amask = amask.reshape(128, 4 * QC)
    bo_fm = np.zeros((128, NCH), np.float32)
    for ch in range(NCH):
        bo_fm[:, ch] = bo[ch * 128:(ch + 1) * 128]
    per_core = []
    for c in range(8):
        h = c % 2
        hflag = np.zeros((128, 2), np.float32)
        hflag[:, 0] = 1.0 if h == 0 else 0.0
        hflag[:, 1] = 0.0 if h == 0 else 1.0
        lg_s, lb_s, wo_s = [], [], []
        for g in range(4):
            for lh in range(2):
                hk = 2 * h + lh
                a0 = g * 256 + hk * 64
                lg_s.append(lng[a0:a0 + 64])
                lb_s.append(lnb[a0:a0 + 64])
                wo_s.append(Wo[a0:a0 + 64, :])
        lnab = np.concatenate([np.concatenate(lg_s), np.concatenate(lb_s)]).reshape(1, 1024)
        wo_my = np.stack(wo_s, axis=1)
        wo_my = wo_my.reshape(64, 8, 8, 128).reshape(64, 8192)
        per_core.append(dict(
            w_a=w_a, b_a=b_a, bv_row=bv_row, amask=amask,
            hflag=hflag,
            lnab_a=np.ascontiguousarray(lnab.astype(np.float32)),
            wo_a=np.ascontiguousarray(wo_my.astype(np.float32)),
            bo_fm=bo_fm))
    return per_core


def host_ffn(I):
    DH, H, W, NP = 64, 8, 512, 22
    shared = {}
    for l in range(2):
        wqkv = I["L0_ffn_wqkv"][l].astype(np.float32)
        shared[f"f_wqkv{l}"] = np.ascontiguousarray(wqkv)
        qs, ks = I["L0_ffn_q_scale"][l], I["L0_ffn_k_scale"][l]
        qsk = np.zeros((128, 2), np.float32)
        qsk[:, 0] = np.tile(qs, 2)
        qsk[:, 1] = np.tile(ks, 2)
        shared[f"f_qsk{l}"] = qsk
        w1 = I["L0_ffn_w1"][l].astype(np.float32)
        b1 = I["L0_ffn_b1"][l].astype(np.float32)
        w2 = I["L0_ffn_w2"][l].astype(np.float32)
        b2 = I["L0_ffn_b2"][l].astype(np.float32)
        HID = 2730
        w1p = np.zeros((1024, NP * 256), np.float32)
        b1t = np.zeros((128, 2, NP), np.float32)
        for p_ in range(NP):
            a0, a1_ = p_ * 128, min((p_ + 1) * 128, HID)
            nw = a1_ - a0
            w1p[:, p_ * 256: p_ * 256 + nw] = w1[:, a0:a1_]
            w1p[:, p_ * 256 + 128: p_ * 256 + 128 + nw] = w1[:, HID + a0: HID + a1_]
            b1t[0:nw, 0, p_] = b1[a0:a1_]
            b1t[0:nw, 1, p_] = b1[HID + a0: HID + a1_]
        shared[f"f_w1{l}"] = np.ascontiguousarray(w1p)
        shared[f"f_b1{l}"] = np.ascontiguousarray(b1t.reshape(128, 44))
        w2p = np.zeros((2816, 1024), np.float32)
        w2p[0:HID] = w2
        shared[f"f_w2{l}"] = w2p
        b2f = np.zeros((128, NCH), np.float32)
        for ch in range(NCH):
            b2f[:, ch] = b2[ch * 128:(ch + 1) * 128]
        shared[f"f_b2{l}"] = b2f
        lg, lb = I["L0_ffn_ln_g"][l], I["L0_ffn_ln_b"][l]
        shared[f"f_lnab{l}"] = np.concatenate([lg, lb]).reshape(1, 2048).astype(np.float32)
        wo = I["L0_ffn_wo"][l].astype(np.float32)
        wof = np.zeros((64, 8, 8, 128), np.float32)
        for h in range(H):
            for dch in range(NCH):
                wof[:, h, dch, :] = wo[h * 64:(h + 1) * 64, dch * 128:(dch + 1) * 128]
        shared[f"f_wo{l}"] = np.ascontiguousarray(wof.reshape(64, 8192))
    sel2c = np.zeros((128, 2), np.float32)
    sel2c[0:64, 0] = 1.0
    sel2c[64:128, 1] = 1.0
    shared["sel2c"] = sel2c
    sel2r = np.zeros((2, 128), np.float32)
    sel2r[0, 0:64] = 1.0
    sel2r[1, 64:128] = 1.0
    shared["sel2r"] = sel2r
    sl = np.arange(128)
    amf = np.zeros((128, 256), np.float32)
    amf[:, 0:128] = (sl[None, :] >= sl[:, None]).astype(np.float32)
    amf[:, 128:256] = (sl[:, None] >= sl[None, :]).astype(np.float32)
    shared["amf"] = amf
    rotp = np.zeros((128, 128), np.float32)
    for hb in range(2):
        o = hb * 64
        for m in range(32):
            rotp[o + m + 32, o + m] = -1.0
            rotp[o + m, o + m + 32] = 1.0
    shared["rotp"] = rotp
    half = np.arange(0, DH, 2, dtype=np.float32)
    inv_freq = 1.0 / 10000.0 ** (half / DH)
    base = (half + 0.4 * DH) / (1.4 * DH)
    per_core = []
    for c in range(8):
        h = c % 2
        blocks = [0, 3] if h == 0 else [1, 2]
        tpos = np.concatenate([np.arange(blk * 1024, (blk + 1) * 1024) for blk in blocks]).astype(np.float32)
        freqs = np.concatenate([tpos[:, None] * inv_freq, tpos[:, None] * inv_freq], axis=1)
        power = (tpos - NT // 2) / (W // 2)
        scale = np.concatenate([base[None, :] ** power[:, None]] * 2, axis=1)
        cosf, sinf = np.cos(freqs), np.sin(freqs)
        tab = np.zeros((128, 4, 2048), np.float32)
        for dd in range(64):
            tab[dd, 0, :] = 8.0 * cosf[:, dd] * scale[:, dd]
            tab[dd, 1, :] = 8.0 * sinf[:, dd] * scale[:, dd]
            tab[dd, 2, :] = cosf[:, dd] / scale[:, dd]
            tab[dd, 3, :] = sinf[:, dd] / scale[:, dd]
        tab[64:128] = tab[0:64]
        per_core.append(dict(f_tab=np.ascontiguousarray(tab.reshape(128, 8192)), **shared))
    return per_core


def host_attn_b(I):
    qkv_w, qkv_b = I["L0_attn_b_qkv_w"], I["L0_attn_b_qkv_b"]
    out_w, out_b = I["L0_attn_b_out_w"], I["L0_attn_b_out_b"]
    W = qkv_w.astype(np.float32).T
    bq = qkv_b.astype(np.float32)
    perm64 = np.concatenate([np.arange(0, 64, 2), np.arange(1, 64, 2)])
    qcols = []
    for hb in range(2):
        for qk in range(2):
            for slot in range(4):
                for hd2 in range(2):
                    head = 8 * hb + 2 * slot + hd2
                    base = qk * 1024 + head * 64
                    qcols.append(base + perm64)
    colidx = np.concatenate(qcols)
    w_b = np.concatenate([W[:, colidx], W[:, 2048:3072]], axis=1)
    b_qk = bq[colidx]
    b_bq = np.zeros((128, 16), np.float32)
    for oc in range(16):
        b_bq[:, oc] = b_qk[oc * 128:(oc + 1) * 128]
    bvr_b = np.ascontiguousarray(bq[2048:3072].reshape(1, 1024))
    hd = 64
    inv_freq = 1.0 / 10000.0 ** (np.arange(0, hd, 2, dtype=np.float32) / hd)
    Weff = out_w.astype(np.float32).T
    bo_b = np.zeros((128, NCH), np.float32)
    for ch in range(NCH):
        bo_b[:, ch] = out_b[ch * 128:(ch + 1) * 128]
    per_core = []
    for c in range(8):
        h = c % 2
        blocks = [0, 3] if h == 0 else [1, 2]
        tpos = np.concatenate([np.arange(blk * 1024, (blk + 1) * 1024) for blk in blocks]).astype(np.float32)
        freqs = tpos[:, None] * inv_freq[None, :]
        emb = np.concatenate([np.sin(freqs), np.cos(freqs)], -1)
        Sp = emb[:, ::2].T
        Cp = emb[:, 1::2].T
        tab = np.zeros((128, 2, 2048), np.float32)
        for b4 in range(4):
            tab[b4 * 32:(b4 + 1) * 32, 0, :] = Cp
            tab[b4 * 32:(b4 + 1) * 32, 1, :] = Sp
        wo2 = np.zeros((128, 4, NCH, 128), np.float32)
        for hp in range(4):
            for lh in range(2):
                head = 8 * h + 2 * hp + lh
                strip = Weff[head * 64:(head + 1) * 64, :]
                for dch in range(NCH):
                    wo2[lh * 64:(lh + 1) * 64, hp, dch, :] = strip[:, dch * 128:(dch + 1) * 128]
        per_core.append(dict(w_b=np.ascontiguousarray(w_b), b_bq=b_bq, bvr_b=bvr_b,
                             b_tab=np.ascontiguousarray(tab.reshape(128, 4096)),
                             wo_b=np.ascontiguousarray(wo2.reshape(128, 4096)),
                             bo_b=bo_b, eye64=np.eye(64, dtype=np.float32)))
    return per_core



def make_inmaps(I, dbg):
    shards = host_shards(I["L0_attn_a_x"])
    aa = host_attn_a(I)
    ff = host_ffn(I)
    bb = host_attn_b(I)
    return [dict(x_fm=shards[c], **aa[c], **ff[c], **bb[c]) for c in range(8)]



def kernel(**inputs):
    nc = build("full", races=False)
    in_maps = make_inmaps(inputs, "full")
    res = run_bass_kernel_spmd(nc, in_maps, core_ids=list(range(8)))
    out = np.zeros((4, NT, D), np.float32)
    for c in range(8):
        b, h = c // 2, c % 2
        blocks = [0, 3] if h == 0 else [1, 2]
        oy = res.results[c]["o_y"]
        for i, blk in enumerate(blocks):
            out[b, blk * 1024:(blk + 1) * 1024, :] = oy[:, i * 1024:(i + 1) * 1024].T
    return out


# version 06
# speedup vs baseline: 1.0179x; 1.0151x over previous
import numpy as np
import ml_dtypes
import concourse.bass as bass
import concourse.mybir as mybir
from concourse.bass_utils import run_bass_kernel_spmd
from contextlib import ExitStack

F32 = mybir.dt.float32
BF16 = mybir.dt.bfloat16
I32 = mybir.dt.int32
AF = mybir.ActivationFunctionType
OP = mybir.AluOpType
PAIRS = [[0, 1], [2, 3], [4, 5], [6, 7]]
BF = ml_dtypes.bfloat16

D = 1024
T = 2048
NT = 4096
QC = 512
NQC = T // QC
NCH = 8


class Sems:
    def __init__(self, ctx, nc):
        object.__setattr__(self, "ctx", ctx)
        object.__setattr__(self, "nc", nc)
        object.__setattr__(self, "d", {})

    def h(self, name):
        if name not in self.d:
            self.d[name] = [self.ctx.enter_context(self.nc.semaphore(name)), 0]
        return self.d[name][0]

    def inc(self, instr, name, k=1):
        h = self.h(name)
        instr.then_inc(h, k)
        self.d[name][1] += k
        return self.d[name][1]

    def val(self, name):
        self.h(name)
        return self.d[name][1]

    def wait(self, eng, name, v=None):
        eng.wait_ge(self.h(name), self.val(name) if v is None else v)


class Prog:

    def __init__(self, nc, ctx, dbg):
        self.nc, self.ctx, self.dbg = nc, ctx, dbg
        self.s = Sems(ctx, nc)
        E = ctx.enter_context
        self.X = E(nc.sbuf_tensor("X", [128, NCH, T], F32))
        self.BIGA = E(nc.sbuf_tensor("BIGA", [128, 16384], BF16))
        self.ROWS = E(nc.sbuf_tensor("ROWS", [128, 6, QC], F32))
        self.TMP = E(nc.sbuf_tensor("TMP", [128, 1, QC], F32))
        self.R32 = E(nc.sbuf_tensor("R32", [128, 7, QC], mybir.dt.float32r))
        self.ONE1B = E(nc.sbuf_tensor("ONE1B", [1, 128], BF16))
        self.ONE1F = E(nc.sbuf_tensor("ONE1F", [1, 128], F32))
        self.ONECB = E(nc.sbuf_tensor("ONECB", [128, 1], BF16))
        self.EPS = E(nc.sbuf_tensor("EPS", [128, 1], F32))
        self.NHALF = E(nc.sbuf_tensor("NHALF", [128, 1], F32))
        self.C0125 = E(nc.sbuf_tensor("C0125", [128, 1], F32))
        self.GR = E(nc.gpsimd.register("GRA"))
        self.GR2 = E(nc.gpsimd.register("GRB"))
        self.PSALL = E(nc.psum_tensor("PSALL", [128, 8, QC], F32))
        self.PS = [self.PSALL[:, i, :] for i in range(8)]
        self.G = [bass.AP(self.PSALL, i * 2 * QC, [[8 * QC, 128], [1, 2 * QC]])
                  for i in range(2)]

    def XN(self, c, t0, n):
        return self.BIGA[:, c * T + t0: c * T + t0 + n]

    def p0_load(self, x_fm):
        nc, s = self.nc, self.s
        with nc.Block() as blk:
            @blk.gpsimd
            def _(g):
                g.memset(self.ONE1B[:], 1.0)
                g.memset(self.ONE1F[:], 1.0)
                g.memset(self.ONECB[:], 1.0)
                g.memset(self.EPS[:], 1e-5)
                g.memset(self.C0125[:], 0.125)
                s.inc(g.memset(self.NHALF[:], -0.5), "consts", 1)

            @blk.vector
            def _(v):
                s.wait(v, "consts")
                v.memset(self.ROWS[32:33, 0, :], 1.0)
                s.inc(v.tensor_copy(self.R32[32:33, 2, :], self.ROWS[32:33, 0, :]), "consts", 1)

            @blk.sync
            def _(sy):
                for c in range(NCH):
                    s.inc(sy.dma_start(out=self.X[:, c, :],
                                       in_=x_fm[c * 128:(c + 1) * 128, :]), "xld", 16)

    def topln(self, tag, dst, after=()):
        nc, s = self.nc, self.s
        after = [(name, s.val(name) if v is None else v) for name, v in after]

        def gate(eng):
            for name, v in after:
                eng.wait_ge(s.h(name), v)
        exact = dst == "xf"
        cast_d = "tl_cast"
        sq_d = "tl_sq"
        sum_d = "tl_sum"
        row_d = "tl_row"
        bc_d = "tl_bc"
        ap_d = "tl_ap"
        b = {k: s.val(k) for k in (cast_d, sq_d, sum_d, row_d, bc_d, ap_d)}
        lctx = ExitStack()
        TC = lctx.enter_context(nc.sbuf_tensor(f"TC{tag}", [128, NCH, QC], BF16))
        SQB = lctx.enter_context(nc.sbuf_tensor(f"SQB{tag}", [128, NCH, QC], BF16))
        RBx = lctx.enter_context(nc.sbuf_tensor(f"RB{tag}", [1, 4, QC], BF16))
        self.TC, self.RB = TC, RBx
        R = self.ROWS

        with nc.Block() as blk:
            @blk.gpsimd
            def _(g):
                gate(g)
                s.wait(g, "xld")
                s.wait(g, "consts")
                for qc in range(NQC):
                    if qc >= 1:
                        g.wait_ge(s.h(sum_d), b[sum_d] + qc)
                    for c in range(NCH):
                        i = g.tensor_copy(self.TC[:, c, :],
                                          self.X[:, c, qc * QC:(qc + 1) * QC])
                    s.inc(i, cast_d, 1)

            @blk.scalar
            def _(sc):
                gate(sc)
                for qc in range(NQC):
                    sc.wait_ge(s.h(cast_d), b[cast_d] + qc + 1)
                    s.inc(sc.activation(SQB[:, :, :], self.TC[:, :, :], AF.Square), sq_d, 1)
                    sc.wait_ge(s.h(row_d), b[row_d] + 3 * qc + 1)
                    sc.activation(R[0:1, 3, :], R[0:1, 3, :], AF.Ln, bias=self.EPS[0:1, :])
                    s.inc(sc.activation(R[0:1, 4, :], R[0:1, 3, :], AF.Exp,
                                        scale=self.NHALF[0:1, :]), row_d, 1)

            @blk.tensor
            def _(t):
                gate(t)
                for qc in range(NQC):
                    t.wait_ge(s.h(cast_d), b[cast_d] + qc + 1)
                    if qc >= 1:
                        t.wait_ge(s.h(row_d), b[row_d] + 3 * (qc - 1) + 1)
                    for c in range(NCH):
                        t.matmul(self.PS[6][0:1, :], self.ONECB[:], self.TC[:, c, :],
                                 start=(c == 0), stop=(c == NCH - 1))
                    t.wait_ge(s.h(sq_d), b[sq_d] + qc + 1)
                    for c in range(NCH):
                        i = t.matmul(self.PS[7][0:1, :], self.ONECB[:], SQB[:, c, :],
                                     start=(c == 0), stop=(c == NCH - 1))
                    s.inc(i, sum_d, 1)
                    t.wait_ge(s.h(row_d), b[row_d] + 3 * qc + 3)
                    if qc >= 1:
                        t.wait_ge(s.h(ap_d), b[ap_d] + qc)
                    if exact:
                        t.matmul(self.PS[4][:, :], self.ONE1F[:], R[0:1, 4, :], start=True, stop=True)
                        i = t.matmul(self.PS[5][:, :], self.ONE1F[:], R[0:1, 5, :], start=True, stop=True)
                    else:
                        t.matmul(self.PS[4][:, :], self.ONE1B[:], self.RB[0:1, 0, :], start=True, stop=True)
                        i = t.matmul(self.PS[5][:, :], self.ONE1B[:], self.RB[0:1, 1, :], start=True, stop=True)
                    s.inc(i, bc_d, 1)

            @blk.vector
            def _(v):
                gate(v)
                for qc in range(NQC):
                    v.wait_ge(s.h(sum_d), b[sum_d] + qc + 1)
                    v.tensor_scalar_mul(R[0:1, 2, :], self.PS[6][0:1, :], 1.0 / D)
                    v.tensor_scalar_mul(R[0:1, 3, :], self.PS[7][0:1, :], 1.0 / D)
                    v.tensor_tensor(self.TMP[0:1, 0, 0:QC], R[0:1, 2, :], R[0:1, 2, :], OP.mult)
                    s.inc(v.tensor_sub(R[0:1, 3, :], R[0:1, 3, :], self.TMP[0:1, 0, 0:QC]), row_d, 1)
                    v.wait_ge(s.h(row_d), b[row_d] + 3 * qc + 2)
                    v.tensor_mul(R[0:1, 5, :], R[0:1, 2, :], R[0:1, 4, :])
                    v.tensor_scalar_mul(R[0:1, 5, :], R[0:1, 5, :], -1.0)
                    if not exact:
                        v.tensor_copy(self.RB[0:1, 0, :], R[0:1, 4, :])
                        i = v.tensor_copy(self.RB[0:1, 1, :], R[0:1, 5, :])
                    else:
                        i = v.engine_nop()
                    s.inc(i, row_d, 1)
                    v.wait_ge(s.h(bc_d), b[bc_d] + qc + 1)
                    for c in range(NCH):
                        v.scalar_tensor_tensor(self.TMP[:, 0, :], self.X[:, c, qc * QC:(qc + 1) * QC],
                                               1.0, self.PS[4][:, :], OP.mult, OP.mult)
                        if dst == "xn":
                            out = self.XN(c, qc * QC, QC)
                        else:
                            out = self.X[:, c, qc * QC:(qc + 1) * QC]
                        i = v.tensor_add(out, self.TMP[:, 0, :], self.PS[5][:, :])
                    s.inc(i, ap_d, 1)
        lctx.close()
        return ap_d


    def attn_a(self, io, ln_ap, ln_base):
        nc, s = self.nc, self.s
        R, R32 = self.ROWS, self.R32
        NJ = [4 * (qc + 1) for qc in range(8)]
        UPREF = [0]
        for qc in range(8):
            for gp in range(4):
                UPREF.append(UPREF[-1] + NJ[qc])
        NU = UPREF[-1]

        lctx = ExitStack()
        E = lctx.enter_context
        WST = E(nc.sbuf_tensor("WST", [128, 2, NCH, 128], BF16))
        BATT = E(nc.sbuf_tensor("BATT", [128, 10], F32))
        BVR = E(nc.sbuf_tensor("BVR", [1, 256], BF16))
        STG = E(nc.sbuf_tensor("STG", [128, 2, QC], BF16))
        STGV = E(nc.sbuf_tensor("STGV", [128, 2, 2, 128], BF16))
        KS = E(nc.sbuf_tensor("KS", [128, NT], BF16))
        VS = E(nc.sbuf_tensor("VS", [128, 32, 2, 65], BF16))
        PT = E(nc.sbuf_tensor("PT", [128, 2, 2, QC], BF16))
        OT = E(nc.sbuf_tensor("OT", [128, 2, 8, QC], BF16))
        TMPB = E(nc.sbuf_tensor("TMPB", [128, QC], BF16))
        LBC = E(nc.sbuf_tensor("LBC", [128, 2, QC], F32))
        AM = E(nc.sbuf_tensor("AM", [128, 4, QC], BF16))
        WOA = E(nc.sbuf_tensor("WOA", [64, 8, NCH, 128], BF16))
        YPS = E(nc.sbuf_tensor("YPS", [128, 2, QC], F32))
        YB = E(nc.sbuf_tensor("YB", [128, QC], F32))
        BO = E(nc.sbuf_tensor("BO", [128, NCH], F32))
        HSTG = E(nc.sbuf_tensor("HSTG", [128, 2, 2048], BF16))
        FLG = E(nc.sbuf_tensor("FLG", [128, 2], F32))

        PS, G = self.PS, self.G
        QSv = lambda gp, c0, n: self.BIGA[:, gp * NT + c0: gp * NT + c0 + n]

        names = ("aw awc st_qk st_v ev_qk ev_v mm_qk mm_v cc_agf cc_agv rb_k rb_q rb_v "
                 "st_d exprm expm av_d lrow lbc otd b67r sq2 s2m ssum strow st_ar "
                 "cc_ar rb_ar arrow bcap apm wod wo_ev st_y cc_rs rb_y xadd lnabr awl stq0 stq1 stv0 stv1 sty0 sty1 star0 star1").split()
        b = {k: s.val(k) for k in names}

        def b67_use(qc, kind, i=0):
            g0 = (qc // 2) * 42
            odd = qc % 2
            if kind == "lbc":
                return g0 + odd * 5 + i
            if kind == "stats":
                return g0 + odd * 5 + 4
            if kind == "bc":
                return g0 + 10 + odd * 16 + i
            return g0 + 18 + odd * 16 + i

        def agf_row(oc):
            return (oc // 4) * 640 + (oc % 4) * 128 if oc < 8 else (oc - 8) * 640 + 512
        GBLK = {(0, 0): 0, (0, 1): 3, (1, 0): 1, (1, 1): 2}
        RSQ = [(0, 0), (0, 512), (1, 0), (1, 512), (1, 1024), (1, 1536), (0, 1024), (0, 1536)]

        with nc.Block() as blk:
            @blk.gpsimd
            def _(g):
                s.wait(g, "consts")
                s.inc(g.dma_start(out=BATT[:], in_=io["b_a"][:, :]), "aw", 16)
                s.inc(g.dma_start(out=BVR[:], in_=io["bv_row"][:, :]), "aw", 16)
                s.inc(g.dma_start(out=AM[:], in_=io["amask"][:, :]), "aw", 16)
                s.inc(g.dma_start(out=WOA[:], in_=io["wo_a"][0:64, :]), "aw", 16)
                s.inc(g.dma_start(out=R[32:33, 4:6, :], in_=io["lnab_a"][0:1, :]), "aw", 16)
                s.inc(g.dma_start(out=BO[:], in_=io["bo_fm"][:, :]), "aw", 16)
                s.inc(g.dma_start(out=FLG[:], in_=io["hflag"][:, :]), "aw", 16)
                for oc in range(10):
                    if oc >= 2:
                        g.wait_ge(s.h("mm_qk"), b["mm_qk"] + (oc - 1) * 4)
                    g.wait_ge(s.h("awl"), b["awl"] + oc * 16)
                    s.inc(g.dma_start(out=WST[:, oc % 2, :, :],
                                      in_=bass.AP(io["w_a"], oc * 128,
                                                  [[1536, 128], [1536 * 128, 8], [1, 128]])), "awl", 16)
                g.wait_ge(s.h("mm_qk"), b["mm_qk"] + 40)
                for vb in range(2):
                    g.wait_ge(s.h("awl"), b["awl"] + (10 + vb) * 16)
                    s.inc(g.dma_start(out=WST[:, vb, :, :],
                                      in_=bass.AP(io["w_a"], (10 + vb) * 128,
                                                  [[1536, 128], [1536 * 128, 8], [1, 128]])), "awl", 16)
                s.inc(g.memset(VS[:, :, :, 64:65], 1.0), "awc", 1)
                g.wait_ge(s.h("stq0"), b["stq0"] + 20 * 16)
                g.wait_ge(s.h("stq1"), b["stq1"] + 20 * 16)
                g.wait_ge(s.h("stv0"), b["stv0"] + 8 * 16)
                g.wait_ge(s.h("stv1"), b["stv1"] + 8 * 16)
                s.inc(g.collective_compute("AllGather", OP.bypass, replica_groups=PAIRS,
                                           ins=[io["agin_f"].ap().opt()],
                                           outs=[io["agout_f"].ap().opt()]), "cc_agf", 1)
                s.inc(g.collective_compute("AllGather", OP.bypass, replica_groups=PAIRS,
                                           ins=[io["agin_v"].ap().opt()],
                                           outs=[io["agout_v"].ap().opt()]), "cc_agv", 1)
                s.wait(g, "cc_agf")
                nu = 0
                for r in range(2):
                    g.wait_ge(s.h("lbc"), b["lbc"] + nu)
                    for hh in range(2):
                        row = r * 1280 + hh * 640 + 512
                        s.inc(g.dma_start(out=HSTG[:, hh, :],
                                          in_=bass.AP(io["agout_f"], row * T,
                                                      [[2048, 128], [1024, 2], [1, 1024]])), "rb_k", 16)
                    nu += 1
                for gp in range(4):
                    for r in range(2):
                        g.wait_ge(s.h("lbc"), b["lbc"] + nu)
                        for hh in range(2):
                            row = r * 1280 + hh * 640 + gp * 128
                            s.inc(g.dma_start(out=HSTG[:, hh, :],
                                              in_=bass.AP(io["agout_f"], row * T,
                                                          [[2048, 128], [1024, 2], [1, 1024]])), "rb_q", 16)
                        nu += 1
                s.wait(g, "cc_agv")
                for r in range(2):
                    for pos in range(2):
                        for lh in range(2):
                            g.wait_ge(s.h("lbc"), b["lbc"] + nu)
                            for hh in range(2):
                                off = (r * NT + hh * T + pos * 1024) * 128 + lh * 64
                                s.inc(g.dma_start(out=bass.AP(HSTG, hh * 2048,
                                                              [[2 * 2048, 128], [64, 8], [1, 64]]),
                                                  in_=bass.AP(io["agout_v"], off,
                                                              [[128, 128], [128 * 128, 8], [1, 64]])), "rb_v", 16)
                            nu += 1

            @blk.tensor
            def _(t):
                for oc in range(10):
                    t.wait_ge(s.h("awl"), b["awl"] + (oc + 1) * 16)
                    for tq in range(NQC):
                        u = oc * 4 + tq
                        t.wait_ge(s.h(ln_ap), ln_base + tq + 1)
                        if u >= 2:
                            t.wait_ge(s.h("ev_qk"), b["ev_qk"] + u - 1)
                        for kc in range(NCH):
                            i = t.matmul(PS[6 + u % 2][:, :], WST[:, oc % 2, kc, :],
                                         self.XN(kc, tq * QC, QC),
                                         start=(kc == 0), stop=(kc == NCH - 1))
                        s.inc(i, "mm_qk", 1)
                t.wait_ge(s.h("awl"), b["awl"] + 12 * 16)
                for tt in range(16):
                    t.wait_ge(s.h(ln_ap), ln_base + tt // 4 + 1)
                    if tt >= 2:
                        t.wait_ge(s.h("ev_v"), b["ev_v"] + tt - 1)
                    for kc in range(NCH):
                        t.matmul(PS[4 + tt % 2][:, 0:256],
                                 self.XN(kc, tt * 128, 128),
                                 WST[:, :, kc, :], start=(kc == 0), stop=False)
                    i = t.matmul(PS[4 + tt % 2][:, 0:256], self.ONE1B[:], BVR[:],
                                 start=False, stop=True)
                    s.inc(i, "mm_v", 1)

            @blk.vector
            def _(v):
                v.wait_ge(s.h("aw"), b["aw"] + 7 * 16)
                v.memset(R[64:65, 4, 0:128], 1.0)
                s.inc(v.tensor_copy(R32[64:65, 2, 0:128], R[64:65, 4, 0:128]), "lnabr", 1)
                for u in range(40):
                    oc = u // 4
                    v.wait_ge(s.h("mm_qk"), b["mm_qk"] + u + 1)
                    if u >= 2:
                        v.wait_ge(s.h(f"stq{u % 2}"), b[f"stq{u % 2}"] + (u // 2) * 16)
                    s.inc(v.tensor_scalar(STG[:, u % 2, :], PS[6 + u % 2][:, :],
                                          BATT[:, oc:oc + 1], None, OP.add), "ev_qk", 1)
                for tt in range(16):
                    v.wait_ge(s.h("mm_v"), b["mm_v"] + tt + 1)
                    if tt >= 2:
                        v.wait_ge(s.h(f"stv{tt % 2}"), b[f"stv{tt % 2}"] + (tt // 2) * 16)
                    s.inc(v.tensor_copy(STGV[:, tt % 2, :, :],
                                        PS[4 + tt % 2][:, 0:256]), "ev_v", 1)
                nu = 0
                for r in range(2):
                    dst = bass.AP(KS, GBLK[(r, 0)] * 1024,
                                  [[NT, 128], [(GBLK[(r, 1)] - GBLK[(r, 0)]) * 1024, 2], [1, 1024]])
                    v.wait_ge(s.h("rb_k"), b["rb_k"] + (r + 1) * 32)
                    v.tensor_scalar_mul(HSTG[:, 0, :], HSTG[:, 0, :], FLG[:, 0:1])
                    s.inc(v.scalar_tensor_tensor(dst, HSTG[:, 1, :], FLG[:, 1:2],
                                                 HSTG[:, 0, :], OP.mult, OP.add), "lbc", 1)
                    nu += 1
                for gp in range(4):
                    for r in range(2):
                        dst = bass.AP(self.BIGA, gp * NT + GBLK[(r, 0)] * 1024,
                                      [[16384, 128], [(GBLK[(r, 1)] - GBLK[(r, 0)]) * 1024, 2], [1, 1024]])
                        v.wait_ge(s.h("rb_q"), b["rb_q"] + (gp * 2 + r + 1) * 32)
                        v.tensor_scalar_mul(HSTG[:, 0, :], HSTG[:, 0, :], FLG[:, 0:1])
                        s.inc(v.scalar_tensor_tensor(dst, HSTG[:, 1, :], FLG[:, 1:2],
                                                     HSTG[:, 0, :], OP.mult, OP.add), "lbc", 1)
                        nu += 1
                for r in range(2):
                    for pos in range(2):
                        gb = GBLK[(r, pos)]
                        for lh in range(2):
                            dst = bass.AP(VS, gb * 8 * 130 + lh * 65,
                                          [[32 * 130, 128], [130, 8], [1, 64]])
                            k = (r * 2 + pos) * 2 + lh
                            v.wait_ge(s.h("rb_v"), b["rb_v"] + (k + 1) * 32)
                            h0 = bass.AP(HSTG, 0, [[2 * 2048, 128], [64, 8], [1, 64]])
                            h1 = bass.AP(HSTG, 2048, [[2 * 2048, 128], [64, 8], [1, 64]])
                            v.tensor_scalar_mul(h0, h0, FLG[:, 0:1])
                            s.inc(v.scalar_tensor_tensor(dst, h1, FLG[:, 1:2],
                                                         h0, OP.mult, OP.add), "lbc", 1)
                s.inc(v.tensor_copy(R32[32:33, 5:7, :], R[32:33, 4:6, :]), "lnabr", 1)


            @blk.sync
            def _(sy):
                for u in range(40):
                    oc, tq = u // 4, u % 4
                    sy.wait_ge(s.h("ev_qk"), b["ev_qk"] + u + 1)
                    s.inc(sy.dma_start(out=io["agin_f"][agf_row(oc):agf_row(oc) + 128,
                                                        tq * QC:(tq + 1) * QC],
                                       in_=STG[:, u % 2, :]), f"stq{u % 2}", 16)
                for tt in range(16):
                    sy.wait_ge(s.h("ev_v"), b["ev_v"] + tt + 1)
                    dst = bass.AP(io["agin_v"], tt * 128 * 128,
                                  [[128, 128], [2048 * 128, 2], [1, 128]])
                    s.inc(sy.dma_start(out=dst, in_=STGV[:, tt % 2, :, :]), f"stv{tt % 2}", 16)

        if "o_ks" in io:
            with nc.Block() as blkd:
                @blkd.gpsimd
                def _(g):
                    bst = s.val("st")
                    s.inc(g.dma_start(out=io["o_ks"][:, :], in_=KS[:, :]), "st", 16)
                    for gp in range(4):
                        s.inc(g.dma_start(out=io["o_qs"][gp * 128:(gp + 1) * 128, :],
                                          in_=self.BIGA[:, gp * NT:(gp + 1) * NT]), "st", 16)
                    s.inc(g.dma_start(out=io["o_vs"][:, :],
                                      in_=bass.AP(VS, 0, [[32 * 130, 128], [1, 32 * 130]])), "st", 16)
                    g.wait_ge(s.h("st"), bst + 6 * 16)

        def epi_range(qc):
            return (qc - 1, qc) if qc % 2 == 1 else ()

        with nc.Block() as blk:
            @blk.tensor
            def _(t):
                t.wait_ge(s.h("lbc"), b["lbc"] + 18)
                t.wait_ge(s.h("awc"), b["awc"] + 1)
                t.wait_ge(s.h("ev_qk"), b["ev_qk"] + 40)
                t.wait_ge(s.h("ev_v"), b["ev_v"] + 16)
                for qc in range(8):
                    for gp in range(4):
                        w = qc * 4 + gp
                        u0, nj = UPREF[w], NJ[qc]
                        for j in range(nj):
                            u = u0 + j
                            if u >= 2:
                                t.wait_ge(s.h("expm"), b["expm"] + u - 1)
                            t.matmul(PS[2 * (u % 2)][:, :], KS[0:64, j * 128:(j + 1) * 128],
                                     QSv(gp, qc * QC, QC)[0:64, :], start=True, stop=True,
                                     tile_position=(0, 0))
                            s.inc(t.matmul(PS[2 * (u % 2) + 1][:, :],
                                           KS[64:128, j * 128:(j + 1) * 128],
                                           QSv(gp, qc * QC, QC)[64:128, :], start=True, stop=True,
                                           tile_position=(64, 0)), "st_d", 1)
                            for jj in ([j - 1] if j >= 1 else []) + ([j] if j == nj - 1 else []):
                                uu = u0 + jj
                                t.wait_ge(s.h("expm"), b["expm"] + uu + 1)
                                if jj == 0:
                                    t.wait_ge(s.h("otd"), b["otd"] + 2 * w)
                                for lh in range(2):
                                    i = t.matmul(PS[4 + lh][0:65, :], VS[:, jj, lh, :],
                                                 PT[:, uu % 2, lh, :],
                                                 start=(jj == 0), stop=(jj == nj - 1))
                                s.inc(i, "av_d", 1)
                        t.wait_ge(s.h("lrow"), b["lrow"] + w + 1)
                        t.wait_ge(s.h("b67r"), b["b67r"] + b67_use(qc, "lbc", gp))
                        t.wait_ge(s.h("lnabr"), b["lnabr"] + 2)
                        t.matmul(PS[6][0:64, :], R32[64:65, 2, 0:64], R32[64:65, 0, :],
                                 start=True, stop=True)
                        s.inc(t.matmul(PS[7][0:64, :], R32[64:65, 2, 0:64], R32[64:65, 1, :],
                                       start=True, stop=True), "lbc", 1)
                    t.wait_ge(s.h("otd"), b["otd"] + 8 * (qc + 1))
                    t.wait_ge(s.h("b67r"), b["b67r"] + b67_use(qc, "stats"))
                    for m in range(8):
                        i = t.matmul(PS[6][32:33, :], self.ONECB[0:64, :], OT[0:64, qc % 2, m, :],
                                     start=(m == 0), stop=(m == 7))
                    s.inc(i, "ssum", 1)
                    for rnd in range(4):
                        t.wait_ge(s.h("sq2"), b["sq2"] + qc * 4 + rnd + 1)
                        for mm2 in range(2):
                            i = t.matmul(PS[7][32:33, :], self.ONECB[0:64, :], STG[0:64, mm2, :],
                                         start=(rnd == 0 and mm2 == 0), stop=(rnd == 3 and mm2 == 1))
                        s.inc(i, "s2m", 1)
                    s.inc(t.drain(), "ssum", 1)
                    for q2 in epi_range(qc):
                        t.wait_ge(s.h("lnabr"), b["lnabr"] + 2)
                        for m in range(8):
                            t.wait_ge(s.h("arrow"), b["arrow"] + 3 * q2 + 3)
                            t.wait_ge(s.h("b67r"), b["b67r"] + b67_use(q2, "bc", m))
                            par = 3 * (q2 % 2)
                            lg_m = R32[32:33, 5, m * 64:(m + 1) * 64]
                            lb_m = R32[32:33, 6, m * 64:(m + 1) * 64]
                            t.matmul(PS[6][0:64, :], lg_m, R32[32:33, 0 + par, :],
                                     start=True, stop=True)
                            t.matmul(PS[7][0:64, :], lg_m, R32[32:33, 1 + par, :],
                                     start=True, stop=False)
                            s.inc(t.matmul(PS[7][0:64, :], lb_m, R32[32:33, 2, :],
                                           start=False, stop=True), "bcap", 1)
                        for dch in range(NCH):
                            t.wait_ge(s.h("apm"), b["apm"] + 8 * (q2 + 1))
                            t.wait_ge(s.h("b67r"), b["b67r"] + b67_use(q2, "wo", dch))
                            for m in range(8):
                                i = t.matmul(PS[6 + dch % 2][:, :], WOA[0:64, m, dch, :],
                                             OT[0:64, q2 % 2, m, :], start=(m == 0), stop=(m == 7))
                            s.inc(i, "wod", 1)

            @blk.scalar
            def _(sc):
                for qc in range(8):
                    for gp in range(4):
                        w = qc * 4 + gp
                        u0, nj = UPREF[w], NJ[qc]
                        for j in range(nj):
                            u = u0 + j
                            sc.wait_ge(s.h("st_d"), b["st_d"] + u + 1)
                            if u >= 2:
                                sc.wait_ge(s.h("av_d"), b["av_d"] + u - 1)
                            i = sc.activation(PT[:, u % 2, :, :], G[u % 2], AF.Exp,
                                              scale=self.C0125[:])
                            if j >= 4 * qc:
                                s.inc(i, "exprm", 1)
                            else:
                                s.inc(i, "expm", 1)
                    sc.wait_ge(s.h("otd"), b["otd"] + 8 * (qc + 1))
                    for rnd in range(4):
                        if qc * 4 + rnd >= 1:
                            sc.wait_ge(s.h("s2m"), b["s2m"] + qc * 4 + rnd)
                        s.inc(sc.activation(STG[0:64, :, :], OT[0:64, qc % 2, 2 * rnd:2 * rnd + 2, :],
                                            AF.Square), "sq2", 1)
                    for q2 in epi_range(qc):
                        sc.wait_ge(s.h("arrow"), b["arrow"] + 3 * q2 + 1)
                        sc.activation(R[32:33, 3, :], R[32:33, 3, :], AF.Ln, bias=self.EPS[32:33, :])
                        s.inc(sc.activation(R[32:33, 4, :], R[32:33, 3, :], AF.Exp,
                                            scale=self.NHALF[32:33, :]), "arrow", 1)

            @blk.vector
            def _(v):
                nmask = 0
                for qc in range(8):
                    for gp in range(4):
                        w = qc * 4 + gp
                        u0, nj = UPREF[w], NJ[qc]
                        for j in range(4 * qc, nj):
                            u = u0 + j
                            k = j - 4 * qc
                            v.wait_ge(s.h("exprm"), b["exprm"] + nmask + 1)
                            nmask += 1
                            for lh in range(2):
                                i = v.tensor_mul(PT[:, u % 2, lh, :], PT[:, u % 2, lh, :],
                                                 AM[:, k, :])
                            s.inc(i, "expm", 1)
                        v.wait_ge(s.h("av_d"), b["av_d"] + u0 + nj)
                        v.tensor_copy(R[64:65, 0, :], PS[4][64:65, :])
                        v.tensor_copy(R[64:65, 1, :], PS[5][64:65, :])
                        v.reciprocal(R[64:65, 2, :], R[64:65, 0, :])
                        v.reciprocal(R[64:65, 3, :], R[64:65, 1, :])
                        v.tensor_copy(R32[64:65, 0, :], R[64:65, 2, :])
                        s.inc(v.tensor_copy(R32[64:65, 1, :], R[64:65, 3, :]), "lrow", 1)
                        v.wait_ge(s.h("lbc"), b["lbc"] + 18 + w + 1)
                        v.tensor_copy(LBC[0:64, 0, :], PS[6][0:64, :])
                        v.tensor_copy(LBC[0:64, 1, :], PS[7][0:64, :])
                        s.inc(v.drain(), "b67r", 1)
                        if qc >= 2:
                            v.wait_ge(s.h("wod"), b["wod"] + 8 * (qc - 1))
                        for lh in range(2):
                            m = gp * 2 + lh
                            s.inc(v.scalar_tensor_tensor(OT[0:64, qc % 2, m, :], PS[4 + lh][0:64, :],
                                                         1.0, LBC[0:64, lh, :], OP.mult, OP.mult),
                                  "otd", 1)
                    v.wait_ge(s.h("ssum"), b["ssum"] + 2 * (qc + 1))
                    if qc >= 1:
                        pn = f"star{(qc - 1) % 2}"
                        v.wait_ge(s.h(pn), b[pn] + ((qc - 1) // 2 + 1) * 32)
                    v.tensor_copy(R[32:33, 0, :], PS[6][32:33, :])
                    v.tensor_copy(R[32:33, 1, :], PS[7][32:33, :])
                    s.inc(v.drain(), "b67r", 1)
                    s.inc(v.drain(), "strow", 1)
                    for q2 in epi_range(qc):
                        v.wait_ge(s.h("rb_ar"), b["rb_ar"] + (q2 + 1) * 32)
                        v.tensor_scalar_mul(R[32:33, 2, :], R[32:33, 4, :], 1.0 / D)
                        v.tensor_scalar_mul(R[32:33, 3, :], R[32:33, 5, :], 1.0 / D)
                        v.tensor_tensor(self.TMP[32:33, 0, 0:QC], R[32:33, 2, :], R[32:33, 2, :], OP.mult)
                        s.inc(v.tensor_sub(R[32:33, 3, :], R[32:33, 3, :], self.TMP[32:33, 0, 0:QC]),
                              "arrow", 1)
                        v.wait_ge(s.h("arrow"), b["arrow"] + 3 * q2 + 2)
                        par = 3 * (q2 % 2)
                        v.tensor_mul(R[32:33, 5, :], R[32:33, 2, :], R[32:33, 4, :])
                        v.tensor_scalar_mul(R[32:33, 5, :], R[32:33, 5, :], -1.0)
                        v.tensor_copy(R32[32:33, 0 + par, :], R[32:33, 4, :])
                        s.inc(v.tensor_copy(R32[32:33, 1 + par, :], R[32:33, 5, :]), "arrow", 1)
                        for m in range(8):
                            v.wait_ge(s.h("bcap"), b["bcap"] + 8 * q2 + m + 1)
                            v.tensor_tensor(TMPB[0:64, :], OT[0:64, q2 % 2, m, :], PS[6][0:64, :],
                                            OP.mult)
                            v.tensor_add(OT[0:64, q2 % 2, m, :], TMPB[0:64, :], PS[7][0:64, :])
                            s.inc(v.drain(), "b67r", 1)
                            s.inc(v.drain(), "apm", 1)
                        for dch in range(NCH):
                            v.wait_ge(s.h("wod"), b["wod"] + 8 * q2 + dch + 1)
                            kk = 8 * q2 + dch
                            if kk >= 2:
                                v.wait_ge(s.h(f"sty{kk % 2}"), b[f"sty{kk % 2}"] + (kk // 2) * 16)
                            v.tensor_copy(YPS[:, dch % 2, :], PS[6 + dch % 2][:, :])
                            s.inc(v.drain(), "b67r", 1)
                            s.inc(v.drain(), "wo_ev", 1)
                if True:
                    v.wait_ge(s.h("rb_y"), b["rb_y"] + NCH * 16)
                    for c in range(NCH):
                        s.inc(v.tensor_scalar(self.X[:, c, :], self.X[:, c, :],
                                              BO[:, c:c + 1], None, OP.add), "xadd", 1)

            @blk.sync
            def _(sy):
                for qc in range(8):
                    sy.wait_ge(s.h("strow"), b["strow"] + qc + 1)
                    gidx, qoff = qc // 2, (qc % 2) * QC
                    arn = f"star{qc % 2}"
                    s.inc(sy.dma_start(out=io["ar_in"][gidx][0:1, qoff:qoff + QC],
                                       in_=R[32:33, 0, :]), arn, 16)
                    s.inc(sy.dma_start(out=io["ar_in"][gidx][1:2, qoff:qoff + QC],
                                       in_=R[32:33, 1, :]), arn, 16)
                    for q2 in epi_range(qc):
                        rr, cbase = RSQ[q2]
                        for dch in range(NCH):
                            sy.wait_ge(s.h("wo_ev"), b["wo_ev"] + 8 * q2 + dch + 1)
                            kk = 8 * q2 + dch
                            row0 = rr * 1024 + dch * 128
                            s.inc(sy.dma_start(out=io["rs_in"][row0:row0 + 128, cbase:cbase + QC],
                                               in_=YPS[:, dch % 2, :]), f"sty{kk % 2}", 16)

            @blk.gpsimd
            def _(g):
                for gidx in range(4):
                    g.wait_ge(s.h("star0"), b["star0"] + (gidx + 1) * 32)
                    g.wait_ge(s.h("star1"), b["star1"] + (gidx + 1) * 32)
                    s.inc(g.collective_compute("AllReduce", OP.add, replica_groups=PAIRS,
                                               ins=[io["ar_in"][gidx].ap().opt()],
                                               outs=[io["ar_out"][gidx].ap().opt()]), "cc_ar", 1)
                    g.wait_ge(s.h("cc_ar"), b["cc_ar"] + gidx + 1)
                    for qq in range(2):
                        qc = gidx * 2 + qq
                        qoff = qq * QC
                        if qc >= 1:
                            g.wait_ge(s.h("arrow"), b["arrow"] + 3 * (qc - 1) + 3)
                        g.wait_ge(s.h("lnabr"), b["lnabr"] + 2)
                        s.inc(g.dma_start(out=R[32:33, 4, :],
                                          in_=io["ar_out"][gidx][0:1, qoff:qoff + QC]), "rb_ar", 16)
                        s.inc(g.dma_start(out=R[32:33, 5, :],
                                          in_=io["ar_out"][gidx][1:2, qoff:qoff + QC]), "rb_ar", 16)
                g.wait_ge(s.h("sty0"), b["sty0"] + 32 * 16)
                g.wait_ge(s.h("sty1"), b["sty1"] + 32 * 16)
                s.inc(g.collective_compute("ReduceScatter", OP.add, replica_groups=PAIRS,
                                           ins=[io["rs_in"].ap().opt()],
                                           outs=[io["rs_out"].ap().opt()]), "cc_rs", 1)
                s.wait(g, "cc_rs")
                for c in range(NCH):
                    s.inc(g.dma_start(out=self.X[:, c, :],
                                      in_=io["rs_out"][c * 128:(c + 1) * 128, :],
                                      accum_op=OP.add), "rb_y", 16)
        if "o_lbc" in io:
            with nc.Block() as blkd2:
                @blkd2.gpsimd
                def _(g):
                    bst = s.val("st")
                    s.inc(g.dma_start(out=io["o_lbc"].ap()[0:64, :],
                                      in_=bass.AP(LBC, 0, [[2 * QC, 64], [1, 2 * QC]])), "st", 16)
                    s.inc(g.dma_start(out=bass.AP(io["o_r32"], 0, [[3584, 1], [512, 2], [1, 512]]),
                                      in_=self.R32[64:65, 0:2, :]), "st", 16)
                    s.inc(g.dma_start(out=io["o_ot"].ap()[0:64, :],
                                      in_=bass.AP(OT, 0, [[16 * QC, 64], [1, 16 * QC]])), "st", 16)
                    g.wait_ge(s.h("st"), bst + 3 * 16)
        lctx.close()
        return "xadd"

    def attn_b(self, io, ln_ap, ln_base, upto=3):
        nc, s = self.nc, self.s
        R, R32 = self.ROWS, self.R32
        PS, G = self.PS, self.G
        GBLK = {(0, 0): 0, (0, 1): 3, (1, 0): 1, (1, 1): 2}
        RSQ = [(0, 0), (0, 512), (1, 0), (1, 512), (1, 1024), (1, 1536), (0, 1024), (0, 1536)]

        ctxA = ExitStack()
        E = ctxA.enter_context
        HSTG = E(nc.sbuf_tensor("bHSTG", [128, 2, 2048], BF16))
        OTB = E(nc.sbuf_tensor("bOT", [128, 4, 8, QC], BF16))
        FLG = E(nc.sbuf_tensor("bFLG", [128, 2], F32))

        ctxP = ExitStack()
        EP = ctxP.enter_context
        WSTB = EP(nc.sbuf_tensor("bWST", [128, 2, NCH, 128], BF16))
        BQB = EP(nc.sbuf_tensor("bBQ", [128, 16], F32))
        BVRB = EP(nc.sbuf_tensor("bBVR", [1, 1024], BF16))
        TB = EP(nc.sbuf_tensor("bTB", [128, 2, T], BF16))
        ROTB = EP(nc.sbuf_tensor("bROT", [128, 128], BF16))
        STG = EP(nc.sbuf_tensor("bSTG", [128, 2, QC], BF16))
        STG2 = EP(nc.sbuf_tensor("bSG2", [128, 2, QC], BF16))
        TMPT = EP(nc.sbuf_tensor("bTMT", [128, QC], BF16))
        STGV = EP(nc.sbuf_tensor("bSGV", [128, 2, 2, 128], BF16))

        with nc.Block() as blk:
            b1 = {k: s.val(k) for k in ("aw awl mm_qk mm_v ev_qk ev_v sq2 exprm expm s2m "
                                        "stq0 stq1 stv0 stv1 cc_agf cc_agv lbc rb_q").split()}

            @blk.gpsimd
            def _(g):
                s.inc(g.dma_start(out=BQB[:], in_=io["b_bq"][:, :]), "aw", 16)
                s.inc(g.dma_start(out=BVRB[:], in_=io["bvr_b"][:, :]), "aw", 16)
                s.inc(g.dma_start(out=TB[:, 0, :], in_=io["b_tab"][:, 0:T]), "aw", 16)
                s.inc(g.dma_start(out=TB[:, 1, :], in_=io["b_tab"][:, T:2 * T]), "aw", 16)
                s.inc(g.dma_start(out=ROTB[:], in_=io["rotp"][:, :]), "aw", 16)
                s.inc(g.dma_start(out=FLG[:], in_=io["hflag"][:, :]), "aw", 16)
                for oc in range(16):
                    if oc >= 2:
                        g.wait_ge(s.h("mm_qk"), b1["mm_qk"] + (oc - 1) * 4)
                    g.wait_ge(s.h("awl"), b1["awl"] + oc * 16)
                    s.inc(g.dma_start(out=WSTB[:, oc % 2, :, :],
                                      in_=bass.AP(io["w_b"], oc * 128,
                                                  [[3072, 128], [3072 * 128, 8], [1, 128]])), "awl", 16)
                g.wait_ge(s.h("mm_qk"), b1["mm_qk"] + 64)
                for vbp in range(4):
                    if vbp >= 1:
                        g.wait_ge(s.h("mm_v"), b1["mm_v"] + vbp * 16)
                    for sl in range(2):
                        g.wait_ge(s.h("awl"), b1["awl"] + (16 + vbp * 2 + sl) * 16)
                        s.inc(g.dma_start(out=WSTB[:, sl, :, :],
                                          in_=bass.AP(io["w_b"], (16 + vbp * 2 + sl) * 128,
                                                      [[3072, 128], [3072 * 128, 8], [1, 128]])), "awl", 16)
                g.wait_ge(s.h("stq0"), b1["stq0"] + 32 * 16)
                g.wait_ge(s.h("stq1"), b1["stq1"] + 32 * 16)
                g.wait_ge(s.h("stv0"), b1["stv0"] + 32 * 16)
                g.wait_ge(s.h("stv1"), b1["stv1"] + 32 * 16)
                s.inc(g.collective_compute("AllGather", OP.bypass, replica_groups=PAIRS,
                                           ins=[io["agin_fb"].ap().opt()],
                                           outs=[io["agout_fb"].ap().opt()]), "cc_agf", 1)
                s.inc(g.collective_compute("AllGather", OP.bypass, replica_groups=PAIRS,
                                           ins=[io["agin_vb"].ap().opt()],
                                           outs=[io["agout_vb"].ap().opt()]), "cc_agv", 1)
                g.wait_ge(s.h("cc_agf"), b1["cc_agf"] + 1)
                for nu in range(8):
                    hp, r = nu // 2, nu % 2
                    g.wait_ge(s.h("lbc"), b1["lbc"] + nu)
                    for hh in range(2):
                        row = r * 2048 + hh * 1024 + hp * 128
                        s.inc(g.dma_start(out=HSTG[:, hh, :],
                                          in_=bass.AP(io["agout_fb"], row * T,
                                                      [[2048, 128], [1024, 2], [1, 1024]])), "rb_q", 16)

            @blk.tensor
            def _(t):
                t.wait_ge(s.h("aw"), b1["aw"] + 6 * 16)
                for oc in range(16):
                    t.wait_ge(s.h("awl"), b1["awl"] + (oc + 1) * 16)
                    for tq in range(NQC):
                        u = oc * 4 + tq
                        t.wait_ge(s.h(ln_ap), ln_base + tq + 1)
                        if u >= 2:
                            t.wait_ge(s.h("ev_qk"), b1["ev_qk"] + u - 1)
                        for kc in range(NCH):
                            i = t.matmul(PS[6 + u % 2][:, :], WSTB[:, oc % 2, kc, :],
                                         self.XN(kc, tq * QC, QC),
                                         start=(kc == 0), stop=(kc == NCH - 1))
                        s.inc(i, "mm_qk", 1)
                        for uu in ([u - 1] if u >= 1 else []) + ([u] if u == 63 else []):
                            t.wait_ge(s.h("ev_qk"), b1["ev_qk"] + uu + 1)
                            if uu >= 2:
                                t.wait_ge(s.h("exprm"), b1["exprm"] + uu - 1)
                            s.inc(t.matmul(PS[2 + uu % 2][:, :], ROTB[:, :], STG[:, uu % 2, :],
                                           start=True, stop=True), "sq2", 1)
                t.wait_ge(s.h("awl"), b1["awl"] + 18 * 16)
                for vbp in range(4):
                    for tt in range(16):
                        vt = vbp * 16 + tt
                        if vbp >= 1:
                            t.wait_ge(s.h("awl"), b1["awl"] + (16 + vbp * 2 + 2) * 16)
                        t.wait_ge(s.h(ln_ap), ln_base + tt // 4 + 1)
                        if vt >= 2:
                            t.wait_ge(s.h("ev_v"), b1["ev_v"] + vt - 1)
                        for kc in range(NCH):
                            t.matmul(PS[4 + vt % 2][:, 0:256],
                                     self.XN(kc, tt * 128, 128),
                                     WSTB[:, :, kc, :], start=(kc == 0), stop=False)
                        i = t.matmul(PS[4 + vt % 2][:, 0:256], self.ONE1B[:],
                                     BVRB[0:1, vbp * 256:(vbp + 1) * 256],
                                     start=False, stop=True)
                        s.inc(i, "mm_v", 1)

            @blk.vector
            def _(v):
                v.wait_ge(s.h("aw"), b1["aw"] + 6 * 16)
                for u in range(64):
                    oc, tq = u // 4, u % 4
                    v.wait_ge(s.h("mm_qk"), b1["mm_qk"] + u + 1)
                    if u >= 2:
                        v.wait_ge(s.h("sq2"), b1["sq2"] + u - 1)
                    s.inc(v.tensor_scalar(STG[:, u % 2, :], PS[6 + u % 2][:, :],
                                          BQB[:, oc:oc + 1], None, OP.add), "ev_qk", 1)
                    v.wait_ge(s.h("sq2"), b1["sq2"] + u + 1)
                    if u >= 2:
                        v.wait_ge(s.h(f"stq{u % 2}"), b1[f"stq{u % 2}"] + (u // 2) * 16)
                    v.tensor_mul(TMPT[:, :], STG[:, u % 2, :], TB[:, 0, tq * QC:(tq + 1) * QC])
                    v.scalar_tensor_tensor(STG2[:, u % 2, :], PS[2 + u % 2][:, :], 1.0,
                                           TB[:, 1, tq * QC:(tq + 1) * QC], OP.mult, OP.mult)
                    s.inc(v.drain(), "exprm", 1)
                    s.inc(v.tensor_add(STG2[:, u % 2, :], STG2[:, u % 2, :], TMPT[:, :]),
                          "expm", 1)
                for vt in range(64):
                    v.wait_ge(s.h("mm_v"), b1["mm_v"] + vt + 1)
                    if vt >= 2:
                        v.wait_ge(s.h(f"stv{vt % 2}"), b1[f"stv{vt % 2}"] + (vt // 2) * 16)
                    s.inc(v.tensor_copy(STGV[:, vt % 2, :, :],
                                        PS[4 + vt % 2][:, 0:256]), "ev_v", 1)
                for nu in range(8):
                    hp, r = nu // 2, nu % 2
                    dst = bass.AP(self.BIGA, hp * NT + GBLK[(r, 0)] * 1024,
                                  [[16384, 128], [(GBLK[(r, 1)] - GBLK[(r, 0)]) * 1024, 2], [1, 1024]])
                    v.wait_ge(s.h("rb_q"), b1["rb_q"] + (nu + 1) * 32)
                    v.tensor_scalar_mul(HSTG[:, 0, :], HSTG[:, 0, :], FLG[:, 0:1])
                    s.inc(v.scalar_tensor_tensor(dst, HSTG[:, 1, :], FLG[:, 1:2],
                                                 HSTG[:, 0, :], OP.mult, OP.add), "lbc", 1)

            @blk.sync
            def _(sy):
                for u in range(64):
                    oc, tq = u // 4, u % 4
                    sy.wait_ge(s.h("expm"), b1["expm"] + u + 1)
                    s.inc(sy.dma_start(out=io["agin_fb"][oc * 128:(oc + 1) * 128,
                                                         tq * QC:(tq + 1) * QC],
                                       in_=STG2[:, u % 2, :]), f"stq{u % 2}", 16)
                for vt in range(64):
                    vbp, tt = vt // 16, vt % 16
                    sy.wait_ge(s.h("ev_v"), b1["ev_v"] + vt + 1)
                    dst = bass.AP(io["agin_vb"], (vbp * 2 * 2048 + tt * 128) * 128,
                                  [[128, 128], [2048 * 128, 2], [1, 128]])
                    s.inc(sy.dma_start(out=dst, in_=STGV[:, vt % 2, :, :]), f"stv{vt % 2}", 16)
        ctxP.close()
        if upto == 1:
            ctxA.close()
            return "lbc"
        ctxB2 = ExitStack()
        EB = ctxB2.enter_context
        LBCB = EB(nc.sbuf_tensor("bLBC", [128, 2, QC], F32))
        TMPB2 = EB(nc.sbuf_tensor("bTM2", [128, QC], BF16))
        IDB = EB(nc.sbuf_tensor("bIDB", [64, 64], BF16))
        PT = EB(nc.sbuf_tensor("bPT", [128, 2, 2, QC], BF16))
        VS = EB(nc.sbuf_tensor("bVS", [128, 32, 2, 65], BF16))
        KB = EB(nc.sbuf_tensor("bKB", [128, NT], BF16))
        HSTGV = EB(nc.sbuf_tensor("bHSV", [128, 2, 8, 128], BF16))

        with nc.Block() as blk:
            b2 = {k: s.val(k) for k in ("s2m rb_k rb_v st_d exprm av_d lrow lbc otd b67r "
                                        "cc_agv lnabr awc aw tb1 ssum").split()}

            @blk.gpsimd
            def _(g):
                s.inc(g.dma_start(out=IDB[:, :], in_=io["eye64"][:, :]), "aw", 16)
                s.inc(g.memset(VS[:, :, :, 64:65], 1.0), "awc", 1)
                g.wait_ge(s.h("cc_agv"), b2["cc_agv"])
                for hp in range(4):
                    for r in range(2):
                        uk = hp * 6 + r
                        g.wait_ge(s.h("s2m"), b2["s2m"] + uk)
                        for hh in range(2):
                            row = r * 2048 + hh * 1024 + 512 + hp * 128
                            s.inc(g.dma_start(out=HSTG[:, hh, :],
                                              in_=bass.AP(io["agout_fb"], row * T,
                                                          [[2048, 128], [1024, 2], [1, 1024]])), "rb_k", 16)
                    for r in range(2):
                        for pos in range(2):
                            uk = hp * 6 + 2 + r * 2 + pos
                            g.wait_ge(s.h("s2m"), b2["s2m"] + uk)
                            for hh in range(2):
                                fc = 4 * hh + hp
                                off = (r * 16384 + fc * 2048 + pos * 1024) * 128
                                s.inc(g.dma_start(out=HSTGV[:, hh, :, :],
                                                  in_=bass.AP(io["agout_vb"], off,
                                                              [[128, 128], [128 * 128, 8], [1, 128]])), "rb_v", 16)

            @blk.vector
            def _(v):
                nun = 0
                for hp in range(4):
                    for r in range(2):
                        nun += 1
                        dst = bass.AP(KB, GBLK[(r, 0)] * 1024,
                                      [[NT, 128], [(GBLK[(r, 1)] - GBLK[(r, 0)]) * 1024, 2], [1, 1024]])
                        v.wait_ge(s.h("rb_k"), b2["rb_k"] + nun * 32)
                        if hp >= 1 and r == 0:
                            v.wait_ge(s.h("st_d"), b2["st_d"] + hp * 256)
                        v.tensor_scalar_mul(HSTG[:, 0, :], HSTG[:, 0, :], FLG[:, 0:1])
                        s.inc(v.scalar_tensor_tensor(dst, HSTG[:, 1, :], FLG[:, 1:2],
                                                     HSTG[:, 0, :], OP.mult, OP.add), "s2m", 1)
                    for r in range(2):
                        for pos in range(2):
                            nvv = hp * 4 + r * 2 + pos
                            v.wait_ge(s.h("rb_v"), b2["rb_v"] + (nvv + 1) * 32)
                            if hp >= 1 and r == 0 and pos == 0:
                                v.wait_ge(s.h("av_d"), b2["av_d"] + hp * 264)
                            gb8 = GBLK[(r, pos)] * 8
                            v.tensor_scalar_mul(HSTGV[:, 0, :, :], HSTGV[:, 0, :, :], FLG[:, 0:1])
                            for lh in range(2):
                                lo = lh * 65
                                dstv = bass.AP(VS, gb8 * 130 + lo, [[32 * 130, 128], [130, 8], [1, 64]])
                                i = v.scalar_tensor_tensor(
                                    dstv,
                                    bass.AP(HSTGV, 1024 + lh * 64, [[2048, 128], [128, 8], [1, 64]]),
                                    FLG[:, 1:2],
                                    bass.AP(HSTGV, lh * 64, [[2048, 128], [128, 8], [1, 64]]),
                                    OP.mult, OP.add)
                            s.inc(i, "s2m", 1)
                    for qc in range(8):
                        idx = hp * 8 + qc
                        v.wait_ge(s.h("av_d"), b2["av_d"] + (idx + 1) * 33)
                        v.tensor_copy(R[64:65, 0, :], PS[4][64:65, :])
                        v.tensor_copy(R[64:65, 1, :], PS[5][64:65, :])
                        v.reciprocal(R[64:65, 2, :], R[64:65, 0, :])
                        v.reciprocal(R[64:65, 3, :], R[64:65, 1, :])
                        v.tensor_copy(R32[64:65, 0, :], R[64:65, 2, :])
                        s.inc(v.tensor_copy(R32[64:65, 1, :], R[64:65, 3, :]), "lrow", 1)
                        v.wait_ge(s.h("lbc"), b2["lbc"] + idx + 1)
                        v.tensor_copy(LBCB[0:64, 0, :], PS[6][0:64, :])
                        v.tensor_copy(LBCB[0:64, 1, :], PS[7][0:64, :])
                        s.inc(v.drain(), "b67r", 1)
                        s.inc(v.scalar_tensor_tensor(OTB[0:64, hp, qc, :], PS[4][0:64, :],
                                                     1.0, LBCB[0:64, 0, :], OP.mult, OP.mult), "otd", 1)
                        s.inc(v.scalar_tensor_tensor(TMPB2[0:64, :], PS[5][0:64, :],
                                                     1.0, LBCB[0:64, 1, :], OP.mult, OP.mult), "tb1", 1)
                        v.wait_ge(s.h("ssum"), b2["ssum"] + idx + 1)
                        v.tensor_copy(OTB[64:128, hp, qc, :], PS[7][64:128, :])
                        s.inc(v.drain(), "b67r", 1)

            @blk.tensor
            def _(t):
                t.wait_ge(s.h("lbc"), b2["lbc"])
                t.wait_ge(s.h("lnabr"), b2["lnabr"])
                t.wait_ge(s.h("awc"), b2["awc"] + 1)
                for hp in range(4):
                    t.wait_ge(s.h("s2m"), b2["s2m"] + (hp + 1) * 6)
                    for qc in range(8):
                        idx = hp * 8 + qc
                        for j in range(32):
                            u = idx * 32 + j
                            if u >= 2:
                                t.wait_ge(s.h("exprm"), b2["exprm"] + u - 1)
                            t.matmul(PS[2 * (u % 2)][:, :], KB[0:64, j * 128:(j + 1) * 128],
                                     self.BIGA[0:64, hp * NT + qc * QC: hp * NT + (qc + 1) * QC],
                                     start=True, stop=True, tile_position=(0, 0))
                            s.inc(t.matmul(PS[2 * (u % 2) + 1][:, :],
                                           KB[64:128, j * 128:(j + 1) * 128],
                                           self.BIGA[64:128, hp * NT + qc * QC: hp * NT + (qc + 1) * QC],
                                           start=True, stop=True, tile_position=(64, 0)), "st_d", 1)
                            for jj in ([j - 1] if j >= 1 else []) + ([j] if j == 31 else []):
                                uu = idx * 32 + jj
                                t.wait_ge(s.h("exprm"), b2["exprm"] + uu + 1)
                                if jj == 0:
                                    t.wait_ge(s.h("otd"), b2["otd"] + idx)
                                    t.wait_ge(s.h("tb1"), b2["tb1"] + idx)
                                t.matmul(PS[4][0:65, :], VS[:, jj, 0, 0:65],
                                         PT[:, uu % 2, 0, :], start=(jj == 0), stop=(jj == 31))
                                i = t.matmul(PS[5][0:65, :], VS[:, jj, 1, 0:65],
                                             PT[:, uu % 2, 1, :], start=(jj == 0), stop=(jj == 31))
                                s.inc(i, "av_d", 2 if jj == 31 else 1)
                        t.wait_ge(s.h("lrow"), b2["lrow"] + idx + 1)
                        t.wait_ge(s.h("b67r"), b2["b67r"] + 2 * idx)
                        t.matmul(PS[6][0:64, :], R32[64:65, 2, 0:64], R32[64:65, 0, :],
                                 start=True, stop=True)
                        s.inc(t.matmul(PS[7][0:64, :], R32[64:65, 2, 0:64], R32[64:65, 1, :],
                                       start=True, stop=True), "lbc", 1)
                        t.wait_ge(s.h("tb1"), b2["tb1"] + idx + 1)
                        t.wait_ge(s.h("aw"), b2["aw"] + 16)
                        s.inc(t.matmul(PS[7][64:128, :], IDB[0:64, :], TMPB2[0:64, :],
                                       start=True, stop=True), "ssum", 1)

            @blk.scalar
            def _(sc):
                for idx in range(32):
                    for j in range(32):
                        u = idx * 32 + j
                        sc.wait_ge(s.h("st_d"), b2["st_d"] + u + 1)
                        if j >= 2:
                            sc.wait_ge(s.h("av_d"), b2["av_d"] + idx * 33 + j - 1)
                        elif idx >= 1:
                            sc.wait_ge(s.h("av_d"), b2["av_d"] + idx * 33)
                        s.inc(sc.activation(PT[:, u % 2, :, :], G[u % 2], AF.Exp,
                                            scale=self.C0125[:]), "exprm", 1)
        ctxB2.close()
        if upto == 2:
            ctxA.close()
            return "otd"
        ctxW = ExitStack()
        EW = ctxW.enter_context
        WOB = EW(nc.sbuf_tensor("bWO", [128, 4, NCH, 128], BF16))
        YPS = EW(nc.sbuf_tensor("bYPS", [128, 2, QC], F32))
        YB = EW(nc.sbuf_tensor("bYB", [128, QC], F32))
        BOB = EW(nc.sbuf_tensor("bBO", [128, NCH], F32))
        with nc.Block() as blk:
            b3 = {k: s.val(k) for k in ("aw wod wo_ev sty0 sty1 cc_rs rb_y xadd st_y").split()}

            @blk.gpsimd
            def _(g):
                s.inc(g.dma_start(out=WOB[:], in_=io["wo_b"][:, :]), "aw", 16)
                s.inc(g.dma_start(out=BOB[:], in_=io["bo_b"][:, :]), "aw", 16)
                g.wait_ge(s.h("sty0"), b3["sty0"] + 32 * 16)
                g.wait_ge(s.h("sty1"), b3["sty1"] + 32 * 16)
                if upto < 3:
                    return
                s.inc(g.collective_compute("ReduceScatter", OP.add, replica_groups=PAIRS,
                                           ins=[io["rs_in"].ap().opt()],
                                           outs=[io["rs_out"].ap().opt()]), "cc_rs", 1)
                g.wait_ge(s.h("cc_rs"), b3["cc_rs"] + 1)
                for c in range(NCH):
                    s.inc(g.dma_start(out=self.X[:, c, :],
                                      in_=io["rs_out"][c * 128:(c + 1) * 128, :],
                                      accum_op=OP.add), "rb_y", 16)

            @blk.tensor
            def _(t):
                t.wait_ge(s.h("aw"), b3["aw"] + 32)
                for q2 in range(8):
                    for dch in range(NCH):
                        kk = q2 * NCH + dch
                        if kk >= 2:
                            t.wait_ge(s.h("wo_ev"), b3["wo_ev"] + kk - 1)
                        for hp in range(4):
                            i = t.matmul(PS[6 + dch % 2][:, :], WOB[:, hp, dch, :],
                                         OTB[:, hp, q2, :], start=(hp == 0), stop=(hp == 3))
                        s.inc(i, "wod", 1)

            @blk.vector
            def _(v):
                for q2 in range(8):
                    for dch in range(NCH):
                        kk = q2 * NCH + dch
                        v.wait_ge(s.h("wod"), b3["wod"] + kk + 1)
                        if kk >= 2:
                            v.wait_ge(s.h(f"sty{kk % 2}"), b3[f"sty{kk % 2}"] + (kk // 2) * 16)
                        v.tensor_copy(YPS[:, dch % 2, :], PS[6 + dch % 2][:, :])
                        s.inc(v.drain(), "wo_ev", 1)
                if upto < 3:
                    return
                v.wait_ge(s.h("rb_y"), b3["rb_y"] + NCH * 16)
                for c in range(NCH):
                    s.inc(v.tensor_scalar(self.X[:, c, :], self.X[:, c, :],
                                          BOB[:, c:c + 1], None, OP.add), "xadd", 1)

            @blk.sync
            def _(sy):
                for q2 in range(8):
                    rr, cbase = RSQ[q2]
                    for dch in range(NCH):
                        kk = q2 * NCH + dch
                        sy.wait_ge(s.h("wo_ev"), b3["wo_ev"] + kk + 1)
                        row0 = rr * 1024 + dch * 128
                        s.inc(sy.dma_start(out=io["rs_in"][row0:row0 + 128, cbase:cbase + QC],
                                           in_=YPS[:, dch % 2, :]), f"sty{kk % 2}", 16)
        ctxW.close()
        ctxA.close()
        return "xadd"


    def ffn_call(self, io, cc):
        nc, s = self.nc, self.s
        with nc.Block() as blk:
            bsp = s.val("xld")

            @blk.sync
            def _(sy):
                for c in range(NCH):
                    s.inc(sy.dma_start(out=io["xspill"][c * 128:(c + 1) * 128, :],
                                       in_=self.X[:, c, :]), "xld", 16)
                sy.wait_ge(s.h("xld"), bsp + 128)
        self.topln(f"lnf{cc}", "xf")
        lctx = ExitStack()
        E = lctx.enter_context
        F = {}
        F["KF"] = E(nc.sbuf_tensor(f"KF{cc}", [128, 4, 1536], BF16))
        F["VF"] = E(nc.sbuf_tensor(f"VF{cc}", [128, 12, 8, 65], BF16))
        F["QF"] = E(nc.sbuf_tensor(f"QF{cc}", [128, 4, 1024], BF16))

        F["WSTF"] = E(nc.sbuf_tensor(f"WSTF{cc}", [128, 2, 8, 128], BF16))
        F["W1ST"] = E(nc.sbuf_tensor(f"W1S{cc}", [128, 2, 8, 256], BF16))
        F["PT"] = E(nc.sbuf_tensor(f"PTF{cc}", [128, 2, 2, QC], BF16))
        F["OTF"] = E(nc.sbuf_tensor(f"OTF{cc}", [128, 8, QC], BF16))
        F["XFH"] = E(nc.sbuf_tensor(f"XFH{cc}", [128, 2, 2, NCH], F32))
        F["QSQ"] = E(nc.sbuf_tensor(f"QSQ{cc}", [128, 2, QC], BF16))
        F["TMP2"] = E(nc.sbuf_tensor(f"TMP2{cc}", [128, QC], BF16))
        F["SEL2"] = E(nc.sbuf_tensor(f"SEL2{cc}", [128, 2], BF16))
        F["SEL2R"] = E(nc.sbuf_tensor(f"S2R{cc}", [2, 128], mybir.dt.float32r))
        F["QSK"] = E(nc.sbuf_tensor(f"QSK{cc}", [128, 2], F32))
        F["B1T"] = E(nc.sbuf_tensor(f"B1T{cc}", [128, 2, 22], F32))
        F["B2F"] = E(nc.sbuf_tensor(f"B2F{cc}", [128, NCH], F32))
        F["AMF"] = E(nc.sbuf_tensor(f"AMF{cc}", [128, 2, 128], BF16))
        F["ROTP"] = E(nc.sbuf_tensor(f"ROTP{cc}", [128, 128], BF16))
        F["FLG"] = E(nc.sbuf_tensor(f"FLGF{cc}", [128, 2], F32))
        with nc.Block() as blk:
            bi = s.val("aw")

            @blk.gpsimd
            def _(g):
                s.inc(g.dma_start(out=F["SEL2"][:], in_=io["sel2c"][:, :]), "aw", 16)
                s.inc(g.dma_start(out=self.ROWS[0:2, 0, 0:128], in_=io["sel2r"][0:2, :]), "aw", 16)
                s.inc(g.dma_start(out=F["AMF"][:], in_=io["amf"][:, :]), "aw", 16)
                s.inc(g.dma_start(out=F["ROTP"][:], in_=io["rotp"][:, :]), "aw", 16)
                s.inc(g.dma_start(out=F["FLG"][:], in_=io["hflag"][:, :]), "aw", 16)
                g.memset(F["VF"][:, :, :, 64:65], 1.0)
                s.inc(g.memset(F["OTF"][64:128, :, :], 0.0), "awc", 1)

            @blk.vector
            def _(v):
                v.memset(self.ROWS[32:33, 0, :], 1.0)
                v.wait_ge(s.h("aw"), bi + 80)
                v.tensor_copy(F["SEL2R"][:], self.ROWS[0:2, 0, 0:128])
                s.inc(v.tensor_copy(self.R32[32:33, 6, :], self.ROWS[32:33, 0, :]), "awc", 1)
        for l in range(2):
            self.ffn_layer(io, F, cc, l)
        with nc.Block() as blk:
            brb = s.val("rb_y")

            @blk.gpsimd
            def _(g):
                for c in range(NCH):
                    s.inc(g.dma_start(out=self.X[:, c, :],
                                      in_=io["xspill"][c * 128:(c + 1) * 128, :],
                                      accum_op=OP.add), "rb_y", 16)
                g.wait_ge(s.h("rb_y"), brb + NCH * 16)
                s.inc(g.nop(), "xadd2", 1)
        lctx.close()
        return "xadd2"

    def ffn_layer(self, io, F, cc, l):
        nc, s = self.nc, self.s
        R, R32 = self.ROWS, self.R32
        XF = self.X
        with nc.Block() as blk:
            bb = {k: s.val(k) for k in ("stq0 fsl cc_agf rb_q lnabr strow").split()}

            @blk.vector
            def _(v):
                v.tensor_copy(R[:, 0, 0:NCH], XF[:, :, 1023:1024])
                s.inc(v.tensor_copy(R[:, 0, NCH:2 * NCH], XF[:, :, 2047:2048]), "strow", 1)

            @blk.sync
            def _(sy):
                sy.wait_ge(s.h("strow"), bb["strow"] + 1)
                s.inc(sy.dma_start(out=bass.AP(io["agx_in"], 0, [[NCH, 128], [1, NCH]]),
                                   in_=R[:, 0, 0:NCH]), "stq0", 16)
                s.inc(sy.dma_start(out=bass.AP(io["agx_in"], 1024, [[NCH, 128], [1, NCH]]),
                                   in_=R[:, 0, NCH:2 * NCH]), "stq0", 16)

            @blk.gpsimd
            def _(g):
                s.inc(g.dma_start(out=F["QSK"][:], in_=io[f"f_qsk{l}"][:, :]), "fsl", 16)
                s.inc(g.dma_start(out=F["B1T"][:], in_=io[f"f_b1{l}"][:, :]), "fsl", 16)
                s.inc(g.dma_start(out=F["B2F"][:], in_=io[f"f_b2{l}"][:, :]), "fsl", 16)
                s.inc(g.dma_start(out=R[32:33, 2:6, :], in_=io[f"f_lnab{l}"][0:1, :]), "fsl", 16)
                g.wait_ge(s.h("stq0"), bb["stq0"] + 32)
                s.inc(g.collective_compute("AllGather", OP.bypass, replica_groups=PAIRS,
                                           ins=[io["agx_in"].ap().opt()],
                                           outs=[io["agx_out"].ap().opt()]), "cc_agf", 1)
                g.wait_ge(s.h("cc_agf"), bb["cc_agf"] + 1)
                s.inc(g.dma_start(out=F["XFH"][:, 0, 0, :],
                                  in_=bass.AP(io["agx_out"], 0, [[NCH, 128], [1, NCH]])), "rb_q", 16)
                s.inc(g.dma_start(out=F["XFH"][:, 1, 0, :],
                                  in_=bass.AP(io["agx_out"], 3 * 1024, [[NCH, 128], [1, NCH]])), "rb_q", 16)
                s.inc(g.dma_start(out=R[:, 1, 0:NCH],
                                  in_=bass.AP(io["agx_out"], 2 * 1024, [[NCH, 128], [1, NCH]])), "rb_q", 16)

            @blk.vector
            def _(v2):
                v2.wait_ge(s.h("fsl"), bb["fsl"] + 4 * 16)
                s.inc(v2.tensor_copy(R32[32:33, 0:4, :], R[32:33, 2:6, :]), "lnabr", 1)
                v2.wait_ge(s.h("rb_q"), bb["rb_q"] + 48)
                v2.tensor_scalar_mul(F["XFH"][:, 0, 0, :], F["XFH"][:, 0, 0, :], F["FLG"][:, 1:2])
                v2.tensor_scalar_mul(F["XFH"][:, 1, 0, :], F["XFH"][:, 1, 0, :], F["FLG"][:, 0:1])
                s.inc(v2.scalar_tensor_tensor(F["XFH"][:, 1, 0, :], R[:, 1, 0:NCH], F["FLG"][:, 1:2],
                                              F["XFH"][:, 1, 0, :], OP.mult, OP.add), "lnabr", 1)
        for bki in range(2):
            self.ffn_block(io, F, cc, l, bki)

    def ffn_block(self, io, F, cc, l, bki):
        nc, s = self.nc, self.s
        R, R32, PS, G = self.ROWS, self.R32, self.PS, self.G
        XF = self.X
        KF, VF, QF = F["KF"], F["VF"], F["QF"]
        b1024 = bki * 1024

        def XSH(c, t0, n):
            return self.BIGA[:, c * T + b1024 + t0: c * T + b1024 + t0 + n]

        wq = io[f"f_wqkv{l}"]
        agk_in = io["agka_in"] if bki == 0 else io["agkb_in"]
        agk_out = io["agka_out"] if bki == 0 else io["agkb_out"]
        agm_in = io["agm1_in"] if bki == 0 else io["agm2_in"]
        agm_out = io["agm1_out"] if bki == 0 else io["agm2_out"]

        ctxF = ExitStack()
        F["FTAB"] = ctxF.enter_context(nc.sbuf_tensor(f"FTAB{cc}_{l}_{bki}", [128, 2, 1024], BF16))
        F["QNN"] = ctxF.enter_context(nc.sbuf_tensor(f"QNN{cc}_{l}_{bki}", [128, 2, QC], BF16))
        F["QRAW"] = ctxF.enter_context(nc.sbuf_tensor(f"QRAW{cc}_{l}_{bki}", [128, 2, QC], BF16))
        F["LBCF"] = ctxF.enter_context(nc.sbuf_tensor(f"LBCFp{cc}_{l}_{bki}", [128, QC], BF16))
        with nc.Block() as blk:
            bb = {k: s.val(k) for k in ("awc ftb fwq mm_qk mm_v lbc ev_qk ev_v st_d "
                                        "exprm expm av_d lrow otd sq2 s2m").split()}

            @blk.gpsimd
            def _(g):
                for c in range(NCH):
                    if c < 4:
                        i = g.tensor_copy(XSH(c, 0, 1024), XF[:, c, b1024:b1024 + 1024])
                    else:
                        g.tensor_copy(XSH(c, 0, 1), F["XFH"][:, bki, 0, c:c + 1])
                        i = g.tensor_copy(XSH(c, 1, 1023), XF[:, c, b1024:b1024 + 1023])
                s.inc(i, "awc", 1)
                for kind in range(2):
                    g.wait_ge(s.h("ftb"), bb["ftb"] + kind * 16)
                    s.inc(g.dma_start(out=F["FTAB"][:, kind, :],
                                      in_=io["f_tab"][:, kind * T + b1024: kind * T + b1024 + 1024]),
                          "ftb", 16)
                for oc in range(8):
                    if oc == 4:
                        g.wait_ge(s.h("otd"), bb["otd"] + 8)
                        for kind in range(2):
                            g.wait_ge(s.h("ftb"), bb["ftb"] + (2 + kind) * 16)
                            s.inc(g.dma_start(out=F["FTAB"][:, kind, :],
                                              in_=io["f_tab"][:, (2 + kind) * T + b1024: (2 + kind) * T + b1024 + 1024]),
                                  "ftb", 16)
                    if oc >= 2:
                        g.wait_ge(s.h("mm_qk"), bb["mm_qk"] + (oc - 1) * 2)
                    g.wait_ge(s.h("fwq"), bb["fwq"] + oc * 16)
                    s.inc(g.dma_start(out=F["WSTF"][:, oc % 2, :, :],
                                      in_=bass.AP(wq, oc * 128, [[1536, 128], [1536 * 128, 8], [1, 128]])),
                          "fwq", 16)
                g.wait_ge(s.h("mm_qk"), bb["mm_qk"] + 16)
                for vb in range(2):
                    g.wait_ge(s.h("fwq"), bb["fwq"] + (8 + vb) * 16)
                    s.inc(g.dma_start(out=F["W1ST"][:, vb, :, :],
                                      in_=bass.AP(wq, 1024 + vb * 256, [[1536, 128], [1536 * 128, 8], [1, 256]])),
                          "fwq", 16)

            @blk.tensor
            def _(t):
                t.wait_ge(s.h("awc"), bb["awc"] + 1)
                t.wait_ge(s.h("lnabr"), s.val("lnabr"))
                for oc in range(8):
                    t.wait_ge(s.h("fwq"), bb["fwq"] + (oc + 1) * 16)
                    for qcb in range(2):
                        u = oc * 2 + qcb
                        if u >= 2:
                            t.wait_ge(s.h("exprm"), bb["exprm"] + u - 1)
                        for kc in range(NCH):
                            i = t.matmul(PS[6 + u % 2][:, :], F["WSTF"][:, oc % 2, kc, :],
                                         XSH(kc, qcb * QC, QC), start=(kc == 0), stop=(kc == NCH - 1))
                        s.inc(i, "mm_qk", 1)
                        t.wait_ge(s.h("s2m"), bb["s2m"] + u + 1)
                        if u >= 2:
                            t.wait_ge(s.h("expm"), bb["expm"] + u - 1)
                        s.inc(t.matmul(PS[u % 2][0:2, :], F["SEL2"][:, :], F["QSQ"][:, u % 2, :],
                                       start=True, stop=True), "mm_v", 1)
                        t.wait_ge(s.h("lrow"), bb["lrow"] + u + 1)
                        if u >= 2:
                            t.wait_ge(s.h("otd"), bb["otd"] + u - 1)
                        s.inc(t.matmul(PS[2 + u % 2][:, :], F["SEL2R"][:, :], R32[0:2, 0, :],
                                       start=True, stop=True), "lbc", 1)
                        t.wait_ge(s.h("ev_qk"), bb["ev_qk"] + u + 1)
                        s.inc(t.matmul(PS[2 + u % 2][:, :], F["ROTP"][:, :], F["QNN"][:, u % 2, :],
                                       start=True, stop=True), "ev_v", 1)
                t.wait_ge(s.h("fwq"), bb["fwq"] + 10 * 16)
                for uv in range(16):
                    tt, vs = uv // 2, uv % 2
                    if uv >= 2:
                        t.wait_ge(s.h("sq2"), bb["sq2"] + uv - 1)
                    for kc in range(NCH):
                        i = t.matmul(PS[4 + uv % 2][:, 0:256], XSH(kc, tt * 128, 128),
                                     F["W1ST"][:, vs, kc, :], start=(kc == 0), stop=(kc == NCH - 1))
                    s.inc(i, "st_d", 1)

            @blk.scalar
            def _(sc):
                for u in range(16):
                    sc.wait_ge(s.h("exprm"), bb["exprm"] + u + 1)
                    if u >= 2:
                        sc.wait_ge(s.h("mm_v"), bb["mm_v"] + u - 1)
                    s.inc(sc.activation(F["QSQ"][:, u % 2, :], F["QRAW"][:, u % 2, :], AF.Square),
                          "s2m", 1)
                    sc.wait_ge(s.h("expm"), bb["expm"] + u + 1)
                    sc.activation(R[0:2, 3, :], R[0:2, 3, :], AF.Ln, bias=self.EPS[0:2, :])
                    s.inc(sc.activation(R[0:2, 4, :], R[0:2, 3, :], AF.Exp, scale=self.NHALF[0:2, :]),
                          "av_d", 1)

            @blk.vector
            def _(v):
                for u in range(16):
                    oc, qcb = u // 2, u % 2
                    isq = oc < 4
                    v.wait_ge(s.h("mm_qk"), bb["mm_qk"] + u + 1)
                    if u >= 2:
                        v.wait_ge(s.h("s2m"), bb["s2m"] + u - 1)
                    s.inc(v.tensor_copy(F["QRAW"][:, u % 2, :], PS[6 + u % 2][:, :]), "exprm", 1)
                    v.wait_ge(s.h("mm_v"), bb["mm_v"] + u + 1)
                    s.inc(v.tensor_copy(R[0:2, 3, :], PS[u % 2][0:2, :]), "expm", 1)
                    v.wait_ge(s.h("av_d"), bb["av_d"] + u + 1)
                    s.inc(v.tensor_copy(R32[0:2, 0, :], R[0:2, 4, :]), "lrow", 1)
                    v.wait_ge(s.h("lbc"), bb["lbc"] + u + 1)
                    if u >= 2:
                        v.wait_ge(s.h("ev_v"), bb["ev_v"] + u - 1)
                    s.inc(v.scalar_tensor_tensor(F["QNN"][:, u % 2, :], F["QRAW"][:, u % 2, :],
                                                 F["QSK"][:, (0 if isq else 1):(1 if isq else 2)],
                                                 PS[2 + u % 2][:, :], OP.mult, OP.mult), "ev_qk", 1)
                    v.wait_ge(s.h("ev_v"), bb["ev_v"] + u + 1)
                    v.wait_ge(s.h("ftb"), bb["ftb"] + (32 if isq else 64))
                    tabc = F["FTAB"][:, 0, qcb * QC:(qcb + 1) * QC]
                    tabs = F["FTAB"][:, 1, qcb * QC:(qcb + 1) * QC]
                    v.tensor_mul(F["TMP2"][:, :], F["QNN"][:, u % 2, :], tabc)
                    v.scalar_tensor_tensor(F["LBCF"][:, :], PS[2 + u % 2][:, :], 1.0,
                                           tabs, OP.mult, OP.mult)
                    if isq:
                        dst = QF[:, oc, qcb * QC:(qcb + 1) * QC]
                    else:
                        dst = KF[:, oc - 4, 512 + qcb * QC: 512 + (qcb + 1) * QC]
                    s.inc(v.tensor_add(dst, F["TMP2"][:, :], F["LBCF"][:, :]), "otd", 1)
                for uv in range(16):
                    tt, vs = uv // 2, uv % 2
                    v.wait_ge(s.h("st_d"), bb["st_d"] + uv + 1)
                    s.inc(v.tensor_copy(VF[:, 4 + tt, vs * 4:(vs + 1) * 4, 0:64],
                                        PS[4 + uv % 2][:, 0:256]), "sq2", 1)

        ctxF.close()
        with nc.Block() as blk:
            bb = {k: s.val(k) for k in ("star0 cc_agv rb_k").split()}

            @blk.sync
            def _(sy):
                s.inc(sy.dma_start(out=agk_in[0:128, 0:2048], in_=KF[:, :, 1024:1536]), "star0", 16)
                s.inc(sy.dma_start(out=bass.AP(agk_in, 2048, [[4128, 128], [520, 4], [65, 8], [1, 65]]),
                                   in_=VF[:, 8:12, :, :]), "star0", 16)
                if bki == 0:
                    s.inc(sy.dma_start(out=io["agkb_in"][128:256, 0:2048], in_=KF[:, :, 1024:1536]),
                          "star0", 16)
                    s.inc(sy.dma_start(out=bass.AP(io["agkb_in"], 128 * 4128 + 2048,
                                                   [[4128, 128], [520, 4], [65, 8], [1, 65]]),
                                       in_=VF[:, 8:12, :, :]), "star0", 16)

            @blk.gpsimd
            def _(g):
                g.wait_ge(s.h("star0"), bb["star0"] + (64 if bki == 0 else 32))
                s.inc(g.collective_compute("AllGather", OP.bypass, replica_groups=PAIRS,
                                           ins=[agk_in.ap().opt()], outs=[agk_out.ap().opt()]), "cc_agv", 1)
                g.wait_ge(s.h("cc_agv"), bb["cc_agv"] + 1)
                if bki == 0:
                    s.inc(g.dma_start(out=KF[:, :, 0:512],
                                      in_=bass.AP(agk_out, 0, [[4128, 128], [512, 4], [1, 512]])), "rb_k", 16)
                    s.inc(g.dma_start(out=VF[:, 0:4, :, :],
                                      in_=bass.AP(agk_out, 2048, [[4128, 128], [520, 4], [65, 8], [1, 65]])), "rb_k", 16)
                else:
                    s.inc(g.dma_start(out=KF[:, :, 0:512],
                                      in_=bass.AP(agk_out, 256 * 4128, [[4128, 128], [512, 4], [1, 512]])), "rb_k", 16)
                    s.inc(g.dma_start(out=VF[:, 0:4, :, :],
                                      in_=bass.AP(agk_out, 256 * 4128 + 2048, [[4128, 128], [520, 4], [65, 8], [1, 65]])), "rb_k", 16)
                    s.inc(g.dma_start(out=bass.AP(F["PT"], 0, [[2048, 128], [512, 4], [1, 512]]),
                                      in_=bass.AP(agk_out, 384 * 4128, [[4128, 128], [512, 4], [1, 512]])), "rb_k", 16)
                    s.inc(g.dma_start(out=bass.AP(F["W1ST"], 0, [[4096, 128], [520, 4], [65, 8], [1, 65]]),
                                      in_=bass.AP(agk_out, 384 * 4128 + 2048, [[4128, 128], [520, 4], [65, 8], [1, 65]])), "rb_k", 16)

            @blk.vector
            def _(v):
                n_dma = 2 if bki == 0 else 4
                v.wait_ge(s.h("rb_k"), bb["rb_k"] + n_dma * 16)
                fsel = F["FLG"][:, 1:2]
                if bki == 0:
                    v.tensor_scalar_mul(KF[:, :, 0:512], KF[:, :, 0:512], fsel)
                    s.inc(v.tensor_scalar_mul(VF[:, 0:4, :, :], VF[:, 0:4, :, :], fsel), "s2m", 1)
                else:
                    v.tensor_scalar_mul(KF[:, :, 0:512], KF[:, :, 0:512], F["FLG"][:, 0:1])
                    v.tensor_scalar_mul(VF[:, 0:4, :, :], VF[:, 0:4, :, :], F["FLG"][:, 0:1])
                    v.scalar_tensor_tensor(KF[:, :, 0:512],
                                           bass.AP(F["PT"], 0, [[2048, 128], [512, 4], [1, 512]]),
                                           fsel, KF[:, :, 0:512], OP.mult, OP.add)
                    s.inc(v.scalar_tensor_tensor(VF[:, 0:4, :, :],
                                                 bass.AP(F["W1ST"], 0, [[4096, 128], [520, 4], [65, 8], [1, 65]]),
                                                 fsel, VF[:, 0:4, :, :], OP.mult, OP.add), "s2m", 1)

        if "o_qf" in io and cc == 0 and l == 0 and bki == 0:
            with nc.Block() as blkd:
                @blkd.gpsimd
                def _(g):
                    bst = s.val("st")
                    s.inc(g.dma_start(out=io["o_qf"].ap()[:, :],
                                      in_=bass.AP(QF, 0, [[4 * 1024, 128], [1, 4 * 1024]])), "st", 16)
                    s.inc(g.dma_start(out=io["o_kf"].ap()[:, :],
                                      in_=bass.AP(KF, 0, [[4 * 1536, 128], [1, 4 * 1536]])), "st", 16)
                    s.inc(g.dma_start(out=io["o_vf"].ap()[:, :],
                                      in_=bass.AP(VF, 0, [[12 * 8 * 65, 128], [1, 12 * 8 * 65]])), "st", 16)
                    g.wait_ge(s.h("st"), bst + 3 * 16)

        ctxW2 = ExitStack()
        F["WOF"] = ctxW2.enter_context(nc.sbuf_tensor(f"WOF{cc}_{l}_{bki}", [64, 8, 8, 128], BF16))
        F["LBCF"] = ctxW2.enter_context(nc.sbuf_tensor(f"LBCF{cc}_{l}_{bki}", [128, QC], BF16))
        with nc.Block() as blk:
            bq = {k: s.val(k) for k in ("st_d expm exprm av_d lrow lbc otd wod b67r apm s2m aw").split()}

            def live_i(j):
                return [i for i in range(4) if (j < 4 and j >= i) or (j >= 4 and i >= j - 4)]

            @blk.gpsimd
            def _(g):
                s.inc(g.dma_start(out=F["WOF"][0:64, :, :, :], in_=io[f"f_wo{l}"][0:64, :]),
                      "aw", 16)

            @blk.tensor
            def _(t):
                t.wait_ge(s.h("s2m"), bq["s2m"])
                nj = 0
                nf67 = 0
                for w in range(2):
                    for hp in range(4):
                        for j in range(8):
                            p2 = nj % 2
                            if nj >= 2:
                                t.wait_ge(s.h("expm"), bq["expm"] + nj - 1)
                            t.matmul(PS[2 * p2][:, :],
                                     KF[0:64, hp, (w * 4 + j) * 128:(w * 4 + j + 1) * 128],
                                     QF[0:64, hp, w * QC: (w + 1) * QC],
                                     start=True, stop=True, tile_position=(0, 0))
                            i2 = t.matmul(PS[2 * p2 + 1][:, :],
                                          KF[64:128, hp, (w * 4 + j) * 128:(w * 4 + j + 1) * 128],
                                          QF[64:128, hp, w * QC: (w + 1) * QC],
                                          start=True, stop=True, tile_position=(64, 0))
                            s.inc(i2, "st_d", 1)
                            for jj in ([j - 1] if j >= 1 else []) + ([j] if j == 7 else []):
                                pj = (nj - (1 if jj == j - 1 else 0)) % 2
                                t.wait_ge(s.h("expm"), bq["expm"] + nj + (1 if jj == j else 0))
                                if jj == 0:
                                    t.wait_ge(s.h("otd"), bq["otd"] + 2 * (w * 4 + hp))
                                for lh in range(2):
                                    i2 = t.matmul(PS[4 + lh][0:65, :], VF[:, w * 4 + jj, hp * 2 + lh, :],
                                                  F["PT"][:, pj, lh, :], start=(jj == 0), stop=(jj == 7))
                                s.inc(i2, "av_d", 1)
                            nj += 1
                        t.wait_ge(s.h("lrow"), bq["lrow"] + w * 4 + hp + 1)
                        t.wait_ge(s.h("b67r"), bq["b67r"] + nf67)
                        nf67 += 1
                        t.matmul(PS[6][0:64, :], R32[64:65, 2, 0:64], R32[64:65, 0, :],
                                 start=True, stop=True)
                        s.inc(t.matmul(PS[7][0:64, :], R32[64:65, 2, 0:64], R32[64:65, 1, :],
                                       start=True, stop=True), "lbc", 1)
                    t.wait_ge(s.h("otd"), bq["otd"] + (w + 1) * 8)
                    t.wait_ge(s.h("aw"), bq["aw"] + 16)
                    for dch in range(NCH):
                        t.wait_ge(s.h("b67r"), bq["b67r"] + nf67)
                        nf67 += 1
                        for m in range(8):
                            i2 = t.matmul(PS[6 + dch % 2][:, :], F["WOF"][0:64, m, dch, :],
                                          F["OTF"][0:64, m, :], start=(m == 0), stop=(m == 7))
                        s.inc(i2, "wod", 1)

            @blk.scalar
            def _(sc):
                nj = 0
                for w in range(2):
                    for hp in range(4):
                        for j in range(8):
                            p2 = nj % 2
                            sc.wait_ge(s.h("st_d"), bq["st_d"] + nj + 1)
                            if nj >= 2:
                                sc.wait_ge(s.h("av_d"), bq["av_d"] + nj - 1)
                            s.inc(sc.activation(F["PT"][:, p2, :, :], G[p2], AF.Exp), "exprm", 1)
                            nj += 1

            @blk.vector
            def _(v):
                nj = 0
                for w in range(2):
                    for hp in range(4):
                        for j in range(8):
                            p2 = nj % 2
                            v.wait_ge(s.h("exprm"), bq["exprm"] + nj + 1)
                            for lh in range(2):
                                if j < 4:
                                    v.tensor_mul(F["PT"][:, p2, lh, j * 128:(j + 1) * 128],
                                                 F["PT"][:, p2, lh, j * 128:(j + 1) * 128],
                                                 F["AMF"][:, 1, :])
                                    if j < 3:
                                        v.memset(F["PT"][:, p2, lh, (j + 1) * 128:512], 0.0)
                                else:
                                    jd = j - 4
                                    v.tensor_mul(F["PT"][:, p2, lh, jd * 128:(jd + 1) * 128],
                                                 F["PT"][:, p2, lh, jd * 128:(jd + 1) * 128],
                                                 F["AMF"][:, 0, :])
                                    if jd > 0:
                                        v.memset(F["PT"][:, p2, lh, 0:jd * 128], 0.0)
                            s.inc(v.drain(), "expm", 1)
                            nj += 1
                        v.wait_ge(s.h("av_d"), bq["av_d"] + (w * 4 + hp + 1) * 8)
                        v.tensor_copy(R[64:65, 0, :], PS[4][64:65, :])
                        v.tensor_copy(R[64:65, 1, :], PS[5][64:65, :])
                        v.reciprocal(R[64:65, 2, :], R[64:65, 0, :])
                        v.reciprocal(R[64:65, 3, :], R[64:65, 1, :])
                        v.tensor_copy(R32[64:65, 0, :], R[64:65, 2, :])
                        s.inc(v.tensor_copy(R32[64:65, 1, :], R[64:65, 3, :]), "lrow", 1)
                        v.wait_ge(s.h("lbc"), bq["lbc"] + w * 4 + hp + 1)
                        if w == 1 and hp == 0:
                            v.wait_ge(s.h("wod"), bq["wod"] + 8)
                        v.tensor_copy(F["LBCF"][0:64, :], PS[6][0:64, :])
                        s.inc(v.scalar_tensor_tensor(F["OTF"][0:64, hp * 2, :],
                                                     PS[4][0:64, :], 1.0,
                                                     F["LBCF"][0:64, :], OP.mult, OP.mult), "otd", 1)
                        v.tensor_copy(F["LBCF"][0:64, :], PS[7][0:64, :])
                        s.inc(v.drain(), "b67r", 1)
                        s.inc(v.scalar_tensor_tensor(F["OTF"][0:64, hp * 2 + 1, :],
                                                     PS[5][0:64, :], 1.0,
                                                     F["LBCF"][0:64, :], OP.mult, OP.mult), "otd", 1)
                    for dch in range(NCH):
                        v.wait_ge(s.h("wod"), bq["wod"] + w * 8 + dch + 1)
                        v.scalar_tensor_tensor(XF[:, dch, b1024 + w * QC: b1024 + (w + 1) * QC],
                                                    PS[6 + dch % 2][:, :], 1.0,
                                                    XF[:, dch, b1024 + w * QC: b1024 + (w + 1) * QC],
                                                    OP.mult, OP.add)
                        s.inc(v.drain(), "b67r", 1)
                        s.inc(v.drain(), "apm", 1)

        if "o_xa" in io and cc == 0 and l == 0 and bki == 0:
            with nc.Block() as blkd:
                @blkd.gpsimd
                def _(g):
                    bst = s.val("st")
                    for c in range(NCH):
                        s.inc(g.dma_start(out=io["o_xa"].ap()[c * 128:(c + 1) * 128, :],
                                          in_=XF[:, c, 0:1024]), "st", 16)
                    g.wait_ge(s.h("st"), bst + 8 * 16)

        ctxW2.close()
        with nc.Block() as blk:
            bb = {k: s.val(k) for k in ("star1 cc_ar rb_v strow").split()}

            @blk.vector
            def _(v):
                s.inc(v.tensor_copy(R[:, 0, 0:NCH], XF[:, :, b1024 + 1023:b1024 + 1024]), "strow", 1)

            @blk.sync
            def _(sy):
                sy.wait_ge(s.h("strow"), bb["strow"] + 1)
                s.inc(sy.dma_start(out=bass.AP(agm_in, 0, [[NCH, 128], [1, NCH]]),
                                   in_=R[:, 0, 0:NCH]), "star1", 16)
                if bki == 0:
                    s.inc(sy.dma_start(out=bass.AP(io["agm2_in"], 1024, [[NCH, 128], [1, NCH]]),
                                       in_=R[:, 0, 0:NCH]), "star1", 16)

            @blk.gpsimd
            def _(g):
                g.wait_ge(s.h("star1"), bb["star1"] + (32 if bki == 0 else 16))
                s.inc(g.collective_compute("AllGather", OP.bypass, replica_groups=PAIRS,
                                           ins=[agm_in.ap().opt()], outs=[agm_out.ap().opt()]), "cc_ar", 1)
                g.wait_ge(s.h("cc_ar"), bb["cc_ar"] + 1)
                if bki == 0:
                    s.inc(g.dma_start(out=F["XFH"][:, 0, 1, :],
                                      in_=bass.AP(agm_out, 0, [[NCH, 128], [1, NCH]])), "rb_v", 16)
                else:
                    s.inc(g.dma_start(out=F["XFH"][:, 1, 1, :],
                                      in_=bass.AP(agm_out, 2 * 1024, [[NCH, 128], [1, NCH]])), "rb_v", 16)
                    s.inc(g.dma_start(out=R[:, 1, 0:NCH],
                                      in_=bass.AP(agm_out, 3 * 1024, [[NCH, 128], [1, NCH]])), "rb_v", 16)

            @blk.vector
            def _(v):
                n_dma = 1 if bki == 0 else 2
                v.wait_ge(s.h("rb_v"), bb["rb_v"] + n_dma * 16)
                if bki == 0:
                    s.inc(v.tensor_scalar_mul(F["XFH"][:, 0, 1, :], F["XFH"][:, 0, 1, :],
                                              F["FLG"][:, 1:2]), "strow", 1)
                else:
                    v.tensor_scalar_mul(F["XFH"][:, 1, 1, :], F["XFH"][:, 1, 1, :], F["FLG"][:, 0:1])
                    s.inc(v.scalar_tensor_tensor(F["XFH"][:, 1, 1, :], R[:, 1, 0:NCH], F["FLG"][:, 1:2],
                                                 F["XFH"][:, 1, 1, :], OP.mult, OP.add), "strow", 1)

        ctxH = ExitStack()
        F["H2"] = ctxH.enter_context(nc.sbuf_tensor(f"H2{cc}_{l}_{bki}", [128, 22, 512], BF16))
        with nc.Block() as blk:
            bm = {k: s.val(k) for k in ("awc sq2 ssum arrow bcap apm lrow mm_qk mm_v st_d "
                                        "fw1s fw2s xadd rb_v").split()}

            @blk.gpsimd
            def _(g):
                g.wait_ge(s.h("strow"), s.val("strow"))
                for c in range(NCH):
                    if c < 4:
                        i = g.tensor_copy(XSH(c, 0, 1024), XF[:, c, b1024:b1024 + 1024])
                    else:
                        g.tensor_copy(XSH(c, 0, 1), F["XFH"][:, bki, 1, c:c + 1])
                        i = g.tensor_copy(XSH(c, 1, 1023), XF[:, c, b1024:b1024 + 1023])
                s.inc(i, "awc", 1)
                nsl = 0
                for qc2 in range(2):
                    for p in range(22):
                        gidx = qc2 * 22 + p
                        if gidx >= 2:
                            g.wait_ge(s.h("mm_qk"), bm["mm_qk"] + (gidx - 1) * 2)
                        g.wait_ge(s.h("fw1s"), bm["fw1s"] + gidx * 16)
                        s.inc(g.dma_start(out=F["W1ST"][:, p % 2, :, :],
                                          in_=bass.AP(io[f"f_w1{l}"], p * 256, [[5632, 128], [5632 * 128, 8], [1, 256]])),
                              "fw1s", 16)
                    for pas in range(2):
                        for kc in range(22):
                            if nsl >= 2:
                                g.wait_ge(s.h("st_d"), bm["st_d"] + nsl - 1)
                            g.wait_ge(s.h("fw2s"), bm["fw2s"] + nsl * 16)
                            s.inc(g.dma_start(out=F["WSTF"][:, nsl % 2, 0:4, :],
                                              in_=bass.AP(io[f"f_w2{l}"], kc * 128 * 1024 + pas * 512,
                                                          [[1024, 128], [128, 4], [1, 128]])),
                                  "fw2s", 16)
                            nsl += 1

            @blk.tensor
            def _(t):
                t.wait_ge(s.h("awc"), bm["awc"] + 1)
                for qc in range(2):
                    if qc == 1:
                        t.wait_ge(s.h("apm"), bm["apm"] + NCH)
                    for c in range(NCH):
                        i = t.matmul(PS[6][32:33, :], self.ONECB[:], XSH(c, qc * QC, QC),
                                     start=(c == 0), stop=(c == NCH - 1))
                    s.inc(i, "ssum", 1)
                    for rnd in range(2):
                        t.wait_ge(s.h("sq2"), bm["sq2"] + qc * 2 + rnd + 1)
                        for c4 in range(4):
                            i = t.matmul(PS[7][32:33, :], self.ONECB[:],
                                         F["PT"][:, c4 // 2, c4 % 2, :],
                                         start=(rnd == 0 and c4 == 0), stop=(rnd == 1 and c4 == 3))
                        s.inc(i, "ssum", 1)
                    t.wait_ge(s.h("arrow"), bm["arrow"] + 3 * qc + 3)
                    for c in range(NCH):
                        lgc = R32[32:33, 0 + c // 4, (c % 4) * 128:(c % 4) * 128 + 128]
                        lbc_ = R32[32:33, 2 + c // 4, (c % 4) * 128:(c % 4) * 128 + 128]
                        if qc * NCH + c >= 1:
                            t.wait_ge(s.h("apm"), bm["apm"] + qc * NCH + c)
                        t.matmul(PS[6][0:128, :], lgc, R32[32:33, 4, :], start=True, stop=True)
                        t.matmul(PS[7][0:128, :], lgc, R32[32:33, 5, :], start=True, stop=False)
                        s.inc(t.matmul(PS[7][0:128, :], lbc_, R32[32:33, 6, 0:QC],
                                       start=False, stop=True), "bcap", 1)
                for qc2 in range(2):
                    t.wait_ge(s.h("apm"), bm["apm"] + (qc2 + 1) * NCH)
                    for p in range(22):
                        gidx = qc2 * 22 + p
                        t.wait_ge(s.h("fw1s"), bm["fw1s"] + (gidx + 1) * 16)
                        if gidx >= 1:
                            t.wait_ge(s.h("lrow"), bm["lrow"] + gidx - 1)
                        for kc in range(NCH):
                            i = t.matmul(PS[0 + p % 2][:, :], F["W1ST"][:, p % 2, kc, 0:128],
                                         XSH(kc, qc2 * 512, 512), start=(kc == 0), stop=(kc == NCH - 1))
                        s.inc(i, "mm_qk", 1)
                        for kc in range(NCH):
                            i = t.matmul(PS[2 + p % 2][:, :], F["W1ST"][:, p % 2, kc, 128:256],
                                         XSH(kc, qc2 * 512, 512), start=(kc == 0), stop=(kc == NCH - 1))
                        s.inc(i, "mm_qk", 1)
                    if qc2 == 0:
                        t.wait_ge(s.h("apm"), bm["apm"] + 16)
                    for pas in range(2):
                        for kc in range(22):
                            nsl = qc2 * 44 + pas * 22 + kc
                            t.wait_ge(s.h("fw2s"), bm["fw2s"] + (nsl + 1) * 16)
                            t.wait_ge(s.h("lrow"), bm["lrow"] + qc2 * 22 + kc + 1)
                            if kc == 0:
                                if pas == 0 and qc2 >= 1:
                                    t.wait_ge(s.h("xadd"), bm["xadd"] + qc2 * 8)
                                if pas == 1:
                                    t.wait_ge(s.h("xadd"), bm["xadd"] + qc2 * 8 + 4)
                            for dc in range(4):
                                i = t.matmul(PS[4 + dc][:, :],
                                             F["WSTF"][:, nsl % 2, dc, :],
                                             F["H2"][:, kc, :], start=(kc == 0), stop=(kc == 21))
                            s.inc(i, "st_d", 1)

            @blk.scalar
            def _(sc):
                sc.wait_ge(s.h("awc"), bm["awc"] + 1)
                for qc in range(2):
                    for rnd in range(2):
                        ridx = qc * 2 + rnd
                        if ridx >= 1:
                            prev = ridx - 1
                            sc.wait_ge(s.h("ssum"), bm["ssum"] + 3 * (prev // 2) + 2 + (prev % 2))
                        i = sc.activation(F["PT"][:, :, :, :],
                                          bass.AP(self.BIGA, b1024 + rnd * 4 * T + qc * QC,
                                                  [[16384, 128], [T, 4], [1, QC]]),
                                          AF.Square)
                        s.inc(i, "sq2", 1)
                    sc.wait_ge(s.h("arrow"), bm["arrow"] + 3 * qc + 1)
                    sc.activation(R[32:33, 3, :], R[32:33, 3, :], AF.Ln, bias=self.EPS[32:33, :])
                    s.inc(sc.activation(R[32:33, 4, :], R[32:33, 3, :], AF.Exp,
                                        scale=self.NHALF[32:33, :]), "arrow", 1)
                for qc2 in range(2):
                    for p in range(22):
                        gidx = qc2 * 22 + p
                        sc.wait_ge(s.h("mm_qk"), bm["mm_qk"] + (gidx + 1) * 2)
                        if gidx >= 2:
                            sc.wait_ge(s.h("lrow"), bm["lrow"] + gidx - 1)
                        s.inc(sc.activation(F["QSQ"][:, gidx % 2, :], PS[2 + p % 2][:, :], AF.Gelu,
                                            bias=F["B1T"][:, 1, p:p + 1]), "mm_v", 1)

            @blk.vector
            def _(v):
                for qc in range(2):
                    v.wait_ge(s.h("ssum"), bm["ssum"] + 3 * qc + 3)
                    v.tensor_scalar_mul(R[32:33, 2, :], PS[6][32:33, :], 1.0 / D)
                    v.tensor_scalar_mul(R[32:33, 3, :], PS[7][32:33, :], 1.0 / D)
                    v.tensor_tensor(self.TMP[32:33, 0, 0:QC], R[32:33, 2, :], R[32:33, 2, :], OP.mult)
                    s.inc(v.tensor_sub(R[32:33, 3, :], R[32:33, 3, :], self.TMP[32:33, 0, 0:QC]),
                          "arrow", 1)
                    v.wait_ge(s.h("arrow"), bm["arrow"] + 3 * qc + 2)
                    v.tensor_mul(R[32:33, 5, :], R[32:33, 2, :], R[32:33, 4, :])
                    v.tensor_scalar_mul(R[32:33, 5, :], R[32:33, 5, :], -1.0)
                    v.tensor_copy(R32[32:33, 4, :], R[32:33, 4, :])
                    s.inc(v.tensor_copy(R32[32:33, 5, :], R[32:33, 5, :]), "arrow", 1)
                    for c in range(NCH):
                        v.wait_ge(s.h("bcap"), bm["bcap"] + qc * NCH + c + 1)
                        v.tensor_tensor(F["TMP2"][:, :], XSH(c, qc * QC, QC), PS[6][:, :], OP.mult)
                        i = v.tensor_add(XSH(c, qc * QC, QC), F["TMP2"][:, :], PS[7][:, :])
                        s.inc(i, "apm", 1)
                for qc2 in range(2):
                    for p in range(22):
                        gidx = qc2 * 22 + p
                        v.wait_ge(s.h("mm_v"), bm["mm_v"] + gidx + 1)
                        if qc2 >= 1:
                            v.wait_ge(s.h("st_d"), bm["st_d"] + (qc2 - 1) * 44 + 22 + p + 1)
                        i = v.scalar_tensor_tensor(F["H2"][:, p, :], PS[0 + p % 2][:, :],
                                                   F["B1T"][:, 0, p:p + 1], F["QSQ"][:, gidx % 2, :],
                                                   OP.add, OP.mult)
                        s.inc(i, "lrow", 1)
                    for pas in range(2):
                        for dc in range(4):
                            dch = pas * 4 + dc
                            v.wait_ge(s.h("st_d"), bm["st_d"] + qc2 * 44 + pas * 22 + 22)
                            i = v.scalar_tensor_tensor(XF[:, dch, b1024 + qc2 * 512: b1024 + (qc2 + 1) * 512],
                                                       PS[4 + dc][:, :],
                                                       F["B2F"][:, dch:dch + 1],
                                                       XF[:, dch, b1024 + qc2 * 512: b1024 + (qc2 + 1) * 512],
                                                       OP.add, OP.add)
                            s.inc(i, "xadd", 1)
        ctxH.close()
        if "o_xa" in io and cc == 0 and l == 0 and bki == 0:
            with nc.Block() as blkd:
                @blkd.gpsimd
                def _(g):
                    bst = s.val("st")
                    for c in range(NCH):
                        s.inc(g.dma_start(out=io["o_xm"].ap()[c * 128:(c + 1) * 128, :],
                                          in_=XF[:, c, 0:1024]), "st", 16)
                        s.inc(g.dma_start(out=io["o_xn2"].ap()[c * 128:(c + 1) * 128, :],
                                          in_=XSH(c, 0, 1024)), "st", 16)
                    g.wait_ge(s.h("st"), bst + 16 * 16)

    def dump_xn(self, o_xn, o_xf, ap_xn, ap_xf):
        nc, s = self.nc, self.s
        with nc.Block() as blk:
            @blk.gpsimd
            def _(g):
                s.wait(g, ap_xn)
                s.wait(g, ap_xf)
                for c in range(NCH):
                    s.inc(g.dma_start(out=o_xn[c * 128:(c + 1) * 128, :],
                                      in_=self.XN(c, 0, T)), "st", 16)
                    s.inc(g.dma_start(out=o_xf[c * 128:(c + 1) * 128, :],
                                      in_=self.X[:, c, :]), "st", 16)
                s.wait(g, "st")


def build(dbg="x1", races=True):
    nc = bass.Bass("TRN2", target_bir_lowering=False, detect_race_conditions=races)
    x_fm = nc.dram_tensor("x_fm", [D, T], F32, kind="ExternalInput").ap()
    io = {}
    io["w_a"] = nc.dram_tensor("w_a", [D, 1536], F32, kind="ExternalInput")
    io["b_a"] = nc.dram_tensor("b_a", [128, 10], F32, kind="ExternalInput")
    io["bv_row"] = nc.dram_tensor("bv_row", [1, 256], F32, kind="ExternalInput")
    io["amask"] = nc.dram_tensor("amask", [128, 4 * QC], F32, kind="ExternalInput")
    io["hflag"] = nc.dram_tensor("hflag", [128, 2], F32, kind="ExternalInput")
    io["lnab_a"] = nc.dram_tensor("lnab_a", [1, 1024], F32, kind="ExternalInput")
    io["wo_a"] = nc.dram_tensor("wo_a", [64, 8192], F32, kind="ExternalInput")
    io["bo_fm"] = nc.dram_tensor("bo_fm", [128, NCH], F32, kind="ExternalInput")
    io["agin_f"] = nc.dram_tensor("agin_f", [1280, T], BF16)
    io["agout_f"] = nc.dram_tensor("agout_f", [2560, T], BF16)
    io["agin_v"] = nc.dram_tensor("agin_v", [NT, 128], BF16)
    io["agout_v"] = nc.dram_tensor("agout_v", [2 * NT, 128], BF16)
    io["ar_in"] = [nc.dram_tensor(f"ar_in{g}", [2, 1024], F32) for g in range(4)]
    io["ar_out"] = [nc.dram_tensor(f"ar_out{g}", [2, 1024], F32) for g in range(4)]
    io["rs_in"] = nc.dram_tensor("rs_in", [2048, T], F32)
    io["rs_out"] = nc.dram_tensor("rs_out", [1024, T], F32)
    for l in range(2):
        io[f"f_wqkv{l}"] = nc.dram_tensor(f"f_wqkv{l}", [D, 1536], F32, kind="ExternalInput")
        io[f"f_qsk{l}"] = nc.dram_tensor(f"f_qsk{l}", [128, 2], F32, kind="ExternalInput")
        io[f"f_b1{l}"] = nc.dram_tensor(f"f_b1{l}", [128, 44], F32, kind="ExternalInput")
        io[f"f_b2{l}"] = nc.dram_tensor(f"f_b2{l}", [128, NCH], F32, kind="ExternalInput")
        io[f"f_lnab{l}"] = nc.dram_tensor(f"f_lnab{l}", [1, 2048], F32, kind="ExternalInput")
        io[f"f_wo{l}"] = nc.dram_tensor(f"f_wo{l}", [64, 8192], F32, kind="ExternalInput")
        io[f"f_w1{l}"] = nc.dram_tensor(f"f_w1{l}", [D, 5632], F32, kind="ExternalInput")
        io[f"f_w2{l}"] = nc.dram_tensor(f"f_w2{l}", [2816, D], F32, kind="ExternalInput")
    io["f_tab"] = nc.dram_tensor("f_tab", [128, 8192], F32, kind="ExternalInput")
    io["sel2c"] = nc.dram_tensor("sel2c", [128, 2], F32, kind="ExternalInput")
    io["sel2r"] = nc.dram_tensor("sel2r", [2, 128], F32, kind="ExternalInput")
    io["amf"] = nc.dram_tensor("amf", [128, 256], F32, kind="ExternalInput")
    io["rotp"] = nc.dram_tensor("rotp", [128, 128], F32, kind="ExternalInput")
    io["xspill"] = nc.dram_tensor("xspill", [D, T], F32)
    io["agx_in"] = nc.dram_tensor("agx_in", [2, 1024], F32)
    io["agx_out"] = nc.dram_tensor("agx_out", [4, 1024], F32)
    io["agka_in"] = nc.dram_tensor("agka_in", [128, 4128], BF16)
    io["agka_out"] = nc.dram_tensor("agka_out", [256, 4128], BF16)
    io["agkb_in"] = nc.dram_tensor("agkb_in", [256, 4128], BF16)
    io["agkb_out"] = nc.dram_tensor("agkb_out", [512, 4128], BF16)
    io["agm1_in"] = nc.dram_tensor("agm1_in", [1, 1024], F32)
    io["agm1_out"] = nc.dram_tensor("agm1_out", [2, 1024], F32)
    io["agm2_in"] = nc.dram_tensor("agm2_in", [2, 1024], F32)
    io["agm2_out"] = nc.dram_tensor("agm2_out", [4, 1024], F32)
    io["w_b"] = nc.dram_tensor("w_b", [D, 3072], F32, kind="ExternalInput")
    io["b_bq"] = nc.dram_tensor("b_bq", [128, 16], F32, kind="ExternalInput")
    io["bvr_b"] = nc.dram_tensor("bvr_b", [1, 1024], F32, kind="ExternalInput")
    io["b_tab"] = nc.dram_tensor("b_tab", [128, 2 * T], F32, kind="ExternalInput")
    io["wo_b"] = nc.dram_tensor("wo_b", [128, 4096], F32, kind="ExternalInput")
    io["bo_b"] = nc.dram_tensor("bo_b", [128, NCH], F32, kind="ExternalInput")
    io["eye64"] = nc.dram_tensor("eye64", [64, 64], F32, kind="ExternalInput")
    io["agin_fb"] = nc.dram_tensor("agin_fb", [2048, T], BF16)
    io["agout_fb"] = nc.dram_tensor("agout_fb", [4096, T], BF16)
    io["agin_vb"] = nc.dram_tensor("agin_vb", [16384, 128], BF16)
    io["agout_vb"] = nc.dram_tensor("agout_vb", [32768, 128], BF16)
    o_y = nc.dram_tensor("o_y", [D, T], F32, kind="ExternalOutput").ap()
    if dbg == "kqv":
        io["o_ks"] = nc.dram_tensor("o_ks", [128, NT], F32, kind="ExternalOutput").ap()
        io["o_qs"] = nc.dram_tensor("o_qs", [512, NT], F32, kind="ExternalOutput").ap()
        io["o_vs"] = nc.dram_tensor("o_vs", [128, 32 * 130], F32, kind="ExternalOutput").ap()
        io["o_xn"] = nc.dram_tensor("o_xn", [D, T], F32, kind="ExternalOutput").ap()
        io["o_xf"] = nc.dram_tensor("o_xf", [D, T], F32, kind="ExternalOutput").ap()
        io["o_lbc"] = nc.dram_tensor("o_lbc", [64, 2 * QC], F32, kind="ExternalOutput")
    if dbg == "fkv":
        io["o_qf"] = nc.dram_tensor("o_qf", [128, 4096], F32, kind="ExternalOutput")
        io["o_xa"] = nc.dram_tensor("o_xa", [D, 1024], F32, kind="ExternalOutput")
        io["o_xm"] = nc.dram_tensor("o_xm", [D, 1024], F32, kind="ExternalOutput")
        io["o_xn2"] = nc.dram_tensor("o_xn2", [D, 1024], F32, kind="ExternalOutput")
        io["o_h2"] = nc.dram_tensor("o_h2", [128, 22 * 256], F32, kind="ExternalOutput")
        io["o_kf"] = nc.dram_tensor("o_kf", [128, 6144], F32, kind="ExternalOutput")
        io["o_vf"] = nc.dram_tensor("o_vf", [128, 6240], F32, kind="ExternalOutput")
        io["o_r32"] = nc.dram_tensor("o_r32", [1, 7 * QC], F32, kind="ExternalOutput")
        io["o_ot"] = nc.dram_tensor("o_ot", [64, 16 * QC], F32, kind="ExternalOutput")
    with ExitStack() as ctx:
        p = Prog(nc, ctx, dbg)
        p.p0_load(x_fm)
        a1 = p.topln("ln0", "xn")
        ln_base = p.s.val(a1) - NQC
        if dbg == "kqv":
            p.dump_xn(io["o_xn"], io["o_xf"], a1, "xld")
        done = p.attn_a(io, a1, ln_base)
        if dbg in ("x2", "full", "fkv", "x3"):
            done = p.ffn_call(io, 0)
        if dbg in ("x3", "full"):
            a3 = p.topln("ln2", "xn")
            done = p.attn_b(io, a3, p.s.val(a3) - NQC)
        if dbg == "full":
            done = p.ffn_call(io, 1)
        if dbg == "x3o":
            done = p.attn_b(io, a1, ln_base)
        if dbg in ("x3p1", "x3p2"):
            done = p.attn_b(io, a1, ln_base, upto=int(dbg[-1]))
        if dbg == "x3pw":
            done = p.attn_b(io, a1, ln_base, upto=25)
        s = p.s
        with nc.Block() as blk:
            @blk.sync
            def _(sy):
                s.wait(sy, done)
                for c in range(NCH):
                    s.inc(sy.dma_start(out=o_y[c * 128:(c + 1) * 128, :],
                                       in_=p.X[:, c, :]), "st_out", 16)
                s.wait(sy, "st_out")
    return nc


def host_shards(x):
    out = []
    for c in range(8):
        b, h = c // 2, c % 2
        blocks = [0, 3] if h == 0 else [1, 2]
        toks = np.concatenate([x[b, blk * 1024:(blk + 1) * 1024] for blk in blocks], axis=0)
        out.append(np.ascontiguousarray(toks.T.astype(np.float32)))
    return out


def host_attn_a(I):
    Wq, bq = I["L0_attn_a_Wq"], I["L0_attn_a_bq"]
    Wk, bk = I["L0_attn_a_Wk"], I["L0_attn_a_bk"]
    Wv, bv = I["L0_attn_a_Wv"], I["L0_attn_a_bv"]
    Wo, bo = I["L0_attn_a_Wo"], I["L0_attn_a_bo"]
    lng, lnb = I["L0_attn_a_ln_g"], I["L0_attn_a_ln_b"]
    qcols = []
    for r in range(2):
        for g in range(4):
            for lh in range(2):
                hk = 2 * r + lh
                qcols.append((g * 256 + hk * 64, g * 256 + hk * 64 + 64))
    w_q = np.concatenate([Wq[:, a:bb] for a, bb in qcols], axis=1)
    b_q = np.concatenate([bq[a:bb] for a, bb in qcols])
    w_a = np.ascontiguousarray(np.concatenate([w_q, Wk, Wv], axis=1).astype(np.float32))
    b_chunks = np.concatenate([b_q, bk])
    b_a = np.zeros((128, 10), np.float32)
    for ch in range(10):
        b_a[:, ch] = b_chunks[ch * 128:(ch + 1) * 128]
    bv_row = bv.reshape(1, 256).astype(np.float32)
    amask = np.zeros((128, 4, QC), np.float32)
    for k in range(4):
        for i in range(4):
            if i > k:
                amask[:, k, i * 128:(i + 1) * 128] = 1.0
            elif i == k:
                sl = np.arange(128)
                amask[:, k, i * 128:(i + 1) * 128] = (sl[None, :] >= sl[:, None]).astype(np.float32)
    amask = amask.reshape(128, 4 * QC)
    bo_fm = np.zeros((128, NCH), np.float32)
    for ch in range(NCH):
        bo_fm[:, ch] = bo[ch * 128:(ch + 1) * 128]
    per_core = []
    for c in range(8):
        h = c % 2
        hflag = np.zeros((128, 2), np.float32)
        hflag[:, 0] = 1.0 if h == 0 else 0.0
        hflag[:, 1] = 0.0 if h == 0 else 1.0
        lg_s, lb_s, wo_s = [], [], []
        for g in range(4):
            for lh in range(2):
                hk = 2 * h + lh
                a0 = g * 256 + hk * 64
                lg_s.append(lng[a0:a0 + 64])
                lb_s.append(lnb[a0:a0 + 64])
                wo_s.append(Wo[a0:a0 + 64, :])
        lnab = np.concatenate([np.concatenate(lg_s), np.concatenate(lb_s)]).reshape(1, 1024)
        wo_my = np.stack(wo_s, axis=1)
        wo_my = wo_my.reshape(64, 8, 8, 128).reshape(64, 8192)
        per_core.append(dict(
            w_a=w_a, b_a=b_a, bv_row=bv_row, amask=amask,
            hflag=hflag,
            lnab_a=np.ascontiguousarray(lnab.astype(np.float32)),
            wo_a=np.ascontiguousarray(wo_my.astype(np.float32)),
            bo_fm=bo_fm))
    return per_core


def host_ffn(I):
    DH, H, W, NP = 64, 8, 512, 22
    shared = {}
    for l in range(2):
        wqkv = I["L0_ffn_wqkv"][l].astype(np.float32)
        shared[f"f_wqkv{l}"] = np.ascontiguousarray(wqkv)
        qs, ks = I["L0_ffn_q_scale"][l], I["L0_ffn_k_scale"][l]
        qsk = np.zeros((128, 2), np.float32)
        qsk[:, 0] = np.tile(qs, 2)
        qsk[:, 1] = np.tile(ks, 2)
        shared[f"f_qsk{l}"] = qsk
        w1 = I["L0_ffn_w1"][l].astype(np.float32)
        b1 = I["L0_ffn_b1"][l].astype(np.float32)
        w2 = I["L0_ffn_w2"][l].astype(np.float32)
        b2 = I["L0_ffn_b2"][l].astype(np.float32)
        HID = 2730
        w1p = np.zeros((1024, NP * 256), np.float32)
        b1t = np.zeros((128, 2, NP), np.float32)
        for p_ in range(NP):
            a0, a1_ = p_ * 128, min((p_ + 1) * 128, HID)
            nw = a1_ - a0
            w1p[:, p_ * 256: p_ * 256 + nw] = w1[:, a0:a1_]
            w1p[:, p_ * 256 + 128: p_ * 256 + 128 + nw] = w1[:, HID + a0: HID + a1_]
            b1t[0:nw, 0, p_] = b1[a0:a1_]
            b1t[0:nw, 1, p_] = b1[HID + a0: HID + a1_]
        shared[f"f_w1{l}"] = np.ascontiguousarray(w1p)
        shared[f"f_b1{l}"] = np.ascontiguousarray(b1t.reshape(128, 44))
        w2p = np.zeros((2816, 1024), np.float32)
        w2p[0:HID] = w2
        shared[f"f_w2{l}"] = w2p
        b2f = np.zeros((128, NCH), np.float32)
        for ch in range(NCH):
            b2f[:, ch] = b2[ch * 128:(ch + 1) * 128]
        shared[f"f_b2{l}"] = b2f
        lg, lb = I["L0_ffn_ln_g"][l], I["L0_ffn_ln_b"][l]
        shared[f"f_lnab{l}"] = np.concatenate([lg, lb]).reshape(1, 2048).astype(np.float32)
        wo = I["L0_ffn_wo"][l].astype(np.float32)
        wof = np.zeros((64, 8, 8, 128), np.float32)
        for h in range(H):
            for dch in range(NCH):
                wof[:, h, dch, :] = wo[h * 64:(h + 1) * 64, dch * 128:(dch + 1) * 128]
        shared[f"f_wo{l}"] = np.ascontiguousarray(wof.reshape(64, 8192))
    sel2c = np.zeros((128, 2), np.float32)
    sel2c[0:64, 0] = 1.0
    sel2c[64:128, 1] = 1.0
    shared["sel2c"] = sel2c
    sel2r = np.zeros((2, 128), np.float32)
    sel2r[0, 0:64] = 1.0
    sel2r[1, 64:128] = 1.0
    shared["sel2r"] = sel2r
    sl = np.arange(128)
    amf = np.zeros((128, 256), np.float32)
    amf[:, 0:128] = (sl[None, :] >= sl[:, None]).astype(np.float32)
    amf[:, 128:256] = (sl[:, None] >= sl[None, :]).astype(np.float32)
    shared["amf"] = amf
    rotp = np.zeros((128, 128), np.float32)
    for hb in range(2):
        o = hb * 64
        for m in range(32):
            rotp[o + m + 32, o + m] = -1.0
            rotp[o + m, o + m + 32] = 1.0
    shared["rotp"] = rotp
    half = np.arange(0, DH, 2, dtype=np.float32)
    inv_freq = 1.0 / 10000.0 ** (half / DH)
    base = (half + 0.4 * DH) / (1.4 * DH)
    per_core = []
    for c in range(8):
        h = c % 2
        blocks = [0, 3] if h == 0 else [1, 2]
        tpos = np.concatenate([np.arange(blk * 1024, (blk + 1) * 1024) for blk in blocks]).astype(np.float32)
        freqs = np.concatenate([tpos[:, None] * inv_freq, tpos[:, None] * inv_freq], axis=1)
        power = (tpos - NT // 2) / (W // 2)
        scale = np.concatenate([base[None, :] ** power[:, None]] * 2, axis=1)
        cosf, sinf = np.cos(freqs), np.sin(freqs)
        tab = np.zeros((128, 4, 2048), np.float32)
        for dd in range(64):
            tab[dd, 0, :] = 8.0 * cosf[:, dd] * scale[:, dd]
            tab[dd, 1, :] = 8.0 * sinf[:, dd] * scale[:, dd]
            tab[dd, 2, :] = cosf[:, dd] / scale[:, dd]
            tab[dd, 3, :] = sinf[:, dd] / scale[:, dd]
        tab[64:128] = tab[0:64]
        per_core.append(dict(f_tab=np.ascontiguousarray(tab.reshape(128, 8192)), **shared))
    return per_core


def host_attn_b(I):
    qkv_w, qkv_b = I["L0_attn_b_qkv_w"], I["L0_attn_b_qkv_b"]
    out_w, out_b = I["L0_attn_b_out_w"], I["L0_attn_b_out_b"]
    W = qkv_w.astype(np.float32).T
    bq = qkv_b.astype(np.float32)
    perm64 = np.concatenate([np.arange(0, 64, 2), np.arange(1, 64, 2)])
    qcols = []
    for hb in range(2):
        for qk in range(2):
            for slot in range(4):
                for hd2 in range(2):
                    head = 8 * hb + 2 * slot + hd2
                    base = qk * 1024 + head * 64
                    qcols.append(base + perm64)
    colidx = np.concatenate(qcols)
    w_b = np.concatenate([W[:, colidx], W[:, 2048:3072]], axis=1)
    b_qk = bq[colidx]
    b_bq = np.zeros((128, 16), np.float32)
    for oc in range(16):
        b_bq[:, oc] = b_qk[oc * 128:(oc + 1) * 128]
    bvr_b = np.ascontiguousarray(bq[2048:3072].reshape(1, 1024))
    hd = 64
    inv_freq = 1.0 / 10000.0 ** (np.arange(0, hd, 2, dtype=np.float32) / hd)
    Weff = out_w.astype(np.float32).T
    bo_b = np.zeros((128, NCH), np.float32)
    for ch in range(NCH):
        bo_b[:, ch] = out_b[ch * 128:(ch + 1) * 128]
    per_core = []
    for c in range(8):
        h = c % 2
        blocks = [0, 3] if h == 0 else [1, 2]
        tpos = np.concatenate([np.arange(blk * 1024, (blk + 1) * 1024) for blk in blocks]).astype(np.float32)
        freqs = tpos[:, None] * inv_freq[None, :]
        emb = np.concatenate([np.sin(freqs), np.cos(freqs)], -1)
        Sp = emb[:, ::2].T
        Cp = emb[:, 1::2].T
        tab = np.zeros((128, 2, 2048), np.float32)
        for b4 in range(4):
            tab[b4 * 32:(b4 + 1) * 32, 0, :] = Cp
            tab[b4 * 32:(b4 + 1) * 32, 1, :] = Sp
        wo2 = np.zeros((128, 4, NCH, 128), np.float32)
        for hp in range(4):
            for lh in range(2):
                head = 8 * h + 2 * hp + lh
                strip = Weff[head * 64:(head + 1) * 64, :]
                for dch in range(NCH):
                    wo2[lh * 64:(lh + 1) * 64, hp, dch, :] = strip[:, dch * 128:(dch + 1) * 128]
        per_core.append(dict(w_b=np.ascontiguousarray(w_b), b_bq=b_bq, bvr_b=bvr_b,
                             b_tab=np.ascontiguousarray(tab.reshape(128, 4096)),
                             wo_b=np.ascontiguousarray(wo2.reshape(128, 4096)),
                             bo_b=bo_b, eye64=np.eye(64, dtype=np.float32)))
    return per_core



def make_inmaps(I, dbg):
    shards = host_shards(I["L0_attn_a_x"])
    aa = host_attn_a(I)
    ff = host_ffn(I)
    bb = host_attn_b(I)
    return [dict(x_fm=shards[c], **aa[c], **ff[c], **bb[c]) for c in range(8)]



def kernel(**inputs):
    nc = build("full", races=False)
    in_maps = make_inmaps(inputs, "full")
    res = run_bass_kernel_spmd(nc, in_maps, core_ids=list(range(8)))
    out = np.zeros((4, NT, D), np.float32)
    for c in range(8):
        b, h = c // 2, c % 2
        blocks = [0, 3] if h == 0 else [1, 2]
        oy = res.results[c]["o_y"]
        for i, blk in enumerate(blocks):
            out[b, blk * 1024:(blk + 1) * 1024, :] = oy[:, i * 1024:(i + 1) * 1024].T
    return out
